# version 07
# baseline (speedup 1.0000x reference)
import contextlib
import numpy as np
import ml_dtypes

import concourse.bass as bass
import concourse.mybir as mybir
from concourse.bass_utils import run_bass_kernel_spmd

F32 = mybir.dt.float32
BF16 = mybir.dt.bfloat16
AF = mybir.ActivationFunctionType
ALU = mybir.AluOpType
AX = mybir.AxisListType

D = 2048
DQKV = 3 * D
DFF = 8192
DSW = 2 * DFF
NE = 8
DH = 1408
KC = D // 128
HC = DH // 128
E1 = 1e-5
E2 = 1.1920929e-07
NCORES = 8
BIG_NEG = -1.0e9
T = 512
MS = T // 128


def build_nc(n_tok=4096, dbg=False):
    assert n_tok % T == 0
    NT = n_tok // T

    nc = bass.Bass("TRN2", target_bir_lowering=False)

    SHK = D // NCORES
    SHKO = DFF // NCORES
    x_in = nc.dram_tensor("x", [n_tok, D], F32, kind="ExternalInput")
    sh_qkv = nc.dram_tensor("sh_qkv", [DQKV, SHK], F32, kind="ExternalInput")
    sh_sw = nc.dram_tensor("sh_sw", [DSW, SHK], F32, kind="ExternalInput")
    sh_wo = nc.dram_tensor("sh_wo", [D, SHKO], F32, kind="ExternalInput")
    w_g = nc.dram_tensor("w_g", [NE, D], F32, kind="ExternalInput")
    sh_w1 = nc.dram_tensor("sh_w1", [NE, DH, SHK], F32, kind="ExternalInput")
    sh_w2 = nc.dram_tensor("sh_w2", [NE, DH, SHK], F32, kind="ExternalInput")
    sh_w3 = nc.dram_tensor("sh_w3", [NE, D, SHK], F32, kind="ExternalInput")
    ident_bf = nc.dram_tensor("ident_bf", [128, 128], BF16, kind="ExternalInput")
    ones_col = nc.dram_tensor("ones_col", [128, 1], BF16, kind="ExternalInput")
    ones_row = nc.dram_tensor("ones_row", [1, 128], BF16, kind="ExternalInput")
    mask_in = nc.dram_tensor("mask", [128, NCORES], F32, kind="ExternalInput")
    y_out = nc.dram_tensor("y", [n_tok, D], F32, kind="ExternalOutput")

    qkvBs = nc.dram_tensor("qkvBs", [DQKV, SHK], BF16)
    swBs = nc.dram_tensor("swBs", [DSW, SHK], BF16)
    woBs = nc.dram_tensor("woBs", [D, SHKO], BF16)
    wgB = nc.dram_tensor("wgB", [NE, D], BF16)
    w1Bs = nc.dram_tensor("w1Bs", [NE, DH, SHK], BF16)
    w2Bs = nc.dram_tensor("w2Bs", [NE, DH, SHK], BF16)
    w3Bs = nc.dram_tensor("w3Bs", [NE, D, SHK], BF16)
    qkvTs = nc.dram_tensor("qkvTs", [SHK, DQKV], BF16)
    swTs = nc.dram_tensor("swTs", [SHK, DSW], BF16)
    woTs = nc.dram_tensor("woTs", [SHKO, D], BF16)
    w1Ts = nc.dram_tensor("w1Ts", [NE * SHK, DH], BF16)
    w2Ts = nc.dram_tensor("w2Ts", [NE * SHK, DH], BF16)
    w3Ts = nc.dram_tensor("w3Ts", [NE * SHK, D], BF16)
    qkvT = nc.dram_tensor("qkvT", [D, DQKV], BF16, addr_space="Shared")
    swT = nc.dram_tensor("swT", [D, DSW], BF16, addr_space="Shared")
    woT = nc.dram_tensor("woT", [DFF, D], BF16, addr_space="Shared")
    w1T = nc.dram_tensor("w1T", [NCORES * NE * SHK, DH], BF16, addr_space="Shared")
    w2T = nc.dram_tensor("w2T", [NCORES * NE * SHK, DH], BF16, addr_space="Shared")
    w3T = nc.dram_tensor("w3T", [NCORES * NE * SHK, D], BF16, addr_space="Shared")
    sq_sp = nc.dram_tensor("sq_sp", [128, KC, n_tok], BF16)
    w_sp = nc.dram_tensor("w_sp", [128, KC, n_tok], BF16)
    wv_sp = nc.dram_tensor("wv_sp", [128, KC, n_tok], BF16)
    x1_sp = nc.dram_tensor("x1_sp", [n_tok, D], F32)
    st_sp = nc.dram_tensor("st_sp", [NT, 128, 2 * MS], F32)
    tt_sp = nc.dram_tensor("tt_sp", [NT, 128, 2 * KC], F32)
    cc_in = nc.dram_tensor("cc_in", [128, 2 * KC], F32)
    cc_out = nc.dram_tensor("cc_out", [NCORES * 128, 2 * KC], F32,
                            addr_space="Shared")

    n = {}

    def inc(k, a=1):
        n[k] = n.get(k, 0) + a
        return n[k]

    def val(k):
        return n.get(k, 0)

    with contextlib.ExitStack() as est:
        sbt = lambda nm, sh, dt: est.enter_context(nc.sbuf_tensor(nm, sh, dt))
        pstens = lambda nm, sh, dt: est.enter_context(nc.psum_tensor(nm, sh, dt))
        semt = lambda nm: est.enter_context(nc.semaphore(nm))

        c_ident = sbt("c_ident", [128, 128], BF16)
        c_onescol = sbt("c_onescol", [128, 1], BF16)
        c_onesrow = sbt("c_onesrow", [1, 128], BF16)
        c_mask = sbt("c_mask", [128, NCORES], F32)
        c_zero = sbt("c_zero", [128, T], BF16)
        wgT_sb = sbt("wgT_sb", [128, KC * NE], BF16)
        wg_ld = sbt("wg_ld", [NE, 256], BF16)

        UBN = NE * HC * T
        ub = sbt("ub", [128, UBN], BF16)
        qb_o, kb_o, vb_o, xr_o, q2_o = (0, KC * T, 2 * KC * T, 3 * KC * T,
                                        4 * KC * T)
        k2_o = xr_o
        y_o = q2_o

        wsA = sbt("wsA", [128, KC * T], BF16)
        wsB = sbt("wsB", [128, KC * T], BF16)
        fm = sbt("fm", [128, KC * T], BF16)
        u2 = sbt("u2", [128, MS * D], F32)
        scrA = sbt("scrA", [128, D], F32)
        scrB = sbt("scrB", [128, D], F32)
        scrD = sbt("scrD", [128, 4096], BF16)
        ttot = sbt("ttot", [128, 2 * KC], F32)
        bw_all = sbt("bw_all", [128, NE * T], BF16)
        bq_sb = sbt("bq_sb", [128, T], BF16)
        bk_sb = fm[:, 0:T]
        tmpA = sbt("tmpA", [128, T], F32)
        tmpB = sbt("tmpB", [128, T], F32)
        colf = sbt("colf", [128, 16], F32)
        wtot = sbt("wtot", [128, 2 * KC], F32)
        carry_w = sbt("carry_w", [128, KC], F32)
        carry_wv = sbt("carry_wv", [128, KC], F32)
        strow = sbt("strow", [128, 2 * MS], F32)
        cbias = sbt("cbias", [128, 4], F32)

        p0 = pstens("p0", [128, 512], F32)
        p1 = pstens("p1", [128, 512], F32)
        p2 = pstens("p2", [128, 512], F32)
        p3 = pstens("p3", [128, 512], F32)
        p4 = pstens("p4", [128, 512], F32)
        p5 = pstens("p5", [128, 512], F32)
        pbf = pstens("pbf", [128, 512], BF16)
        pstat = pstens("pstat", [1, 512], F32)
        pms = [p0, p1, p2, p3]

        s_ld = semt("s_ld")
        s_g = semt("s_g")
        s_sc = semt("s_sc")
        s_pe = semt("s_pe")
        s_dve = semt("s_dve")
        s_act = semt("s_act")
        s_cc = semt("s_cc")
        SEM = {"ld": s_ld, "g": s_g, "sc": s_sc, "pe": s_pe, "dve": s_dve,
               "act": s_act, "cc": s_cc}
        s_vv = semt("s_vv")
        s_aa = semt("s_aa")

        _SER_OPS = {"tensor_tensor", "tensor_copy", "tensor_scalar",
                    "tensor_scalar_mul", "tensor_scalar_add", "tensor_mul",
                    "tensor_add", "tensor_sub", "reduce_sum", "reduce_max",
                    "reciprocal", "tensor_tensor_scan", "memset",
                    "tensor_single_scalar", "scalar_tensor_tensor",
                    "activation", "copy"}

        class _Ser:

            def __init__(self, eng, ms_sem, ms_key, ser_sem, ser_key):
                self._e = eng
                self._ms = (ms_sem, ms_key)
                self._ser = (ser_sem, ser_key)
                self._prev = None

            def __getattr__(self, name):
                f = getattr(self._e, name)
                if name not in _SER_OPS:
                    return f

                def call(*a, ms=False, **kw):
                    if self._prev is not None:
                        self._e.wait_ge(*self._prev)
                    ins = f(*a, **kw)
                    sem, key = self._ms if ms else self._ser
                    ins.then_inc(sem, 1)
                    inc(key)
                    self._prev = (sem, val(key))
                    return ins
                return call

        sw_u = [semt("sw_u0"), semt("sw_u1")]
        sw_g = [semt("sw_g0"), semt("sw_g1")]
        op_s = [semt("op_s0"), semt("op_s1")]
        m1s = [semt("m1s0"), semt("m1s1")]
        m2s = [semt("m2s0"), semt("m2s1")]
        w3s = [semt("w3s0"), semt("w3s1")]
        wgl = [semt("wgl0"), semt("wgl1")]
        s_wc = semt("s_wc")
        qk = [semt("qk0"), semt("qk1")]
        stg_t = [semt("stg_t0"), semt("stg_t1")]
        stg_s = [semt("stg_s0"), semt("stg_s1")]
        xl_s = semt("xl_s")
        xsp_s = semt("xsp_s")
        cast_sems = {w: semt(f"cast_{w}") for w in
                     ("sw", "wo", "w1", "w2", "w3")}
        cast_q = semt("cast_q")

        def sdma(eng, sem, key, dst, src):
            eng.dma_start(dst, src).then_inc(sem, 16)
            return inc(key, 16)

        def wrapv(v):
            return _Ser(v, s_dve, "dve", s_vv, "vv")

        def wrapa(a):
            return _Ser(a, s_act, "act", s_aa, "aa")

        def dma(eng, q, dst, src, **kw):
            eng.dma_start(dst, src, **kw).then_inc(SEM[q], 16)
            inc(q, 16)

        def ub_c(off, j):
            return ub[:, off + j * T: off + (j + 1) * T]

        def fm_c(kc):
            return fm[:, kc * T:(kc + 1) * T]

        def sp_view(sp, ti):
            return sp.ap()[:, :, ti * T:(ti + 1) * T]

        def w12_slab(sy, sem, key, ws, st, wt, e, h0, hw):
            base = (st % 2) * (KC * 256)
            view = wt.ap().rearrange("(r e ih p) h -> e ih p r h",
                                     r=NCORES, e=NE, p=128)[e]
            dst = ws[:, base:base + KC * hw].rearrange(
                "p (kc h) -> p kc h", h=hw)
            for ih in range(2):
                sdma(sy, sem, key,
                     dst[:, ih::2, :],
                     view[ih][:, :, h0:h0 + hw])

        def w3_slab(sy, st, dr, e):
            buf = [wsA, wsB][st % 2]
            view = w3T.ap().rearrange("(r e ih p) d -> e ih p r d",
                                      r=NCORES, e=NE, p=128)[e]
            dst = buf[:, 0:HC * 512].rearrange("p (kc d) -> p kc d", d=512)
            sdma(sy, w3s[st % 2], f"w3{st % 2}",
                 dst[:, 0::2, :][:, 0:6, :],
                 view[0][:, 0:6, dr * 512:(dr + 1) * 512])
            sdma(sy, w3s[st % 2], f"w3{st % 2}",
                 dst[:, 1::2, :],
                 view[1][:, 0:5, dr * 512:(dr + 1) * 512])

        def ub_3d(off):
            return ub[:, off: off + KC * T].rearrange("p (kc t) -> p kc t", t=T)

        CH = 1 << 21

        def cast_chunks(csrc, cdst):
            sh = csrc.shape
            nm = int(np.prod(sh))
            spec = ("a b -> (a b)" if len(sh) == 2 else "a b c -> (a b c)")
            fsrc = csrc.ap().rearrange(spec)
            fdst = cdst.ap().rearrange(spec)
            return [(fsrc[o:min(o + CH, nm)], fdst[o:min(o + CH, nm)])
                    for o in range(0, nm, CH)]

        sh_pairs = []
        for srcB, dstT, C, R in [(qkvBs.ap(), qkvTs.ap(), SHK, DQKV),
                                 (swBs.ap(), swTs.ap(), SHK, DSW),
                                 (woBs.ap(), woTs.ap(), SHKO, D)]:
            for kc in range(C // 128):
                r0 = 0
                while r0 < R:
                    rs = min(4096, R - r0)
                    sh_pairs.append((srcB, dstT, kc, r0, rs))
                    r0 += rs
        for e in range(NE):
            for srcB, dstT, R in [
                    (w1Bs[e], w1Ts.ap()[e * SHK:(e + 1) * SHK, :], DH),
                    (w2Bs[e], w2Ts.ap()[e * SHK:(e + 1) * SHK, :], DH),
                    (w3Bs[e], w3Ts.ap()[e * SHK:(e + 1) * SHK, :], D)]:
                for kc in range(SHK // 128):
                    sh_pairs.append((srcB, dstT, kc, 0, R))

        stg = [(wsA, 0), (wsA, 4096), (wsB, 0), (wsB, 4096)]
        sl_t = [semt(f"sl_t{i}") for i in range(4)]
        sl_s = [semt(f"sl_s{i}") for i in range(4)]
        cast_all = semt("cast_all")
        s_ccq = semt("s_ccq")
        n_casts = [0]

        with nc.Block("prep") as blk:
            def _g(g):
                for csrc, cdst in [(sh_qkv, qkvBs), (sh_sw, swBs),
                                   (sh_wo, woBs), (w_g, wgB),
                                   (sh_w1, w1Bs), (sh_w2, w2Bs),
                                   (sh_w3, w3Bs)]:
                    for fsrc, fdst in cast_chunks(csrc, cdst):
                        g.dma_start(fdst, fsrc).then_inc(cast_all, 16)
                        n_casts[0] += 1
            blk.gpsimd(_g)

            def _sc(sc):
                sc.wait_ge(cast_all, 16 * n_casts[0])
                for i, (srcB, dstT, kc, r0, rs) in enumerate(sh_pairs):
                    sl = i % 4
                    buf, off = stg[sl]
                    if i >= 4:
                        sc.wait_ge(sl_s[sl], 16 * (i // 4))
                    sc.dma_start(buf[:, off:off + rs],
                                 srcB[r0:r0 + rs, kc * 128:(kc + 1) * 128],
                                 transpose=True).then_inc(sl_t[sl], 16)
                for sl in range(4):
                    nuse = (len(sh_pairs) - sl + 3) // 4
                    sc.wait_ge(sl_t[sl], 16 * nuse)
            blk.scalar(_sc)

            def _sy(sy):
                dma(sy, "ld", c_ident[:, :], ident_bf[:, :])
                dma(sy, "ld", c_onescol[:, :], ones_col[:, :])
                dma(sy, "ld", c_onesrow[:, :], ones_row[:, :])
                dma(sy, "ld", c_mask[:, :], mask_in[:, :])
                for i, (srcB, dstT, kc, r0, rs) in enumerate(sh_pairs):
                    sl = i % 4
                    buf, off = stg[sl]
                    sy.wait_ge(sl_t[sl], 16 * (i // 4 + 1))
                    sy.dma_start(dstT[kc * 128:(kc + 1) * 128, r0:r0 + rs],
                                 buf[:, off:off + rs]).then_inc(sl_s[sl], 16)
                for sl in range(4):
                    nuse = (len(sh_pairs) - sl + 3) // 4
                    sy.wait_ge(sl_s[sl], 16 * nuse)
                sy.wait_ge(s_ld, val("ld"))
            blk.sync(_sy)

            def _v(v):
                v = wrapv(v)
                v.memset(c_zero[:, :], 0.0)
                v.memset(wtot[:, :], 0.0)
                v.memset(cbias[:, 0:1], E1)
                v.memset(cbias[:, 1:2], E1 * E2)
                v.memset(cbias[:, 2:3], E2)
            blk.vector(_v)

        with nc.Block("gather") as blk:
            def _g(g):
                g.collective_compute(
                    "AllGather", ALU.bypass,
                    replica_groups=[list(range(NCORES))],
                    ins=[qkvTs.ap().opt()],
                    outs=[qkvT.ap().opt()]).then_inc(s_ccq, 1)
                for tin, tout in [(swTs, swT), (woTs, woT),
                                  (w1Ts, w1T), (w2Ts, w2T), (w3Ts, w3T)]:
                    g.collective_compute(
                        "AllGather", ALU.bypass,
                        replica_groups=[list(range(NCORES))],
                        ins=[tin.ap().opt()],
                        outs=[tout.ap().opt()]).then_inc(s_cc, 1)
                    inc("cc")
                g.wait_ge(s_ccq, 1)
            blk.gpsimd(_g)

        with nc.Block("wg") as blk:
            pe_b, dve_b = val("pe"), val("dve")
            wgl_base = [val("wgl0"), val("wgl1")]

            def _g(g, pe_b=pe_b, dve_b=dve_b):
                for kc in range(KC):
                    if kc >= 2:
                        g.wait_ge(s_dve, dve_b + kc - 1)
                    sdma(g, wgl[kc % 2], f"wgl{kc % 2}",
                         wg_ld[:, (kc % 2) * 128:(kc % 2) * 128 + 128],
                         wgB.ap()[:, kc * 128:(kc + 1) * 128])
            blk.gpsimd(_g)

            def _t(t, pe_b=pe_b, dve_b=dve_b, wgl_base=wgl_base):
                for kc in range(KC):
                    t.wait_ge(wgl[kc % 2],
                              wgl_base[kc % 2] + 16 * (kc // 2 + 1))
                    if kc >= 1:
                        t.wait_ge(s_dve, dve_b + kc)
                    t.transpose(pbf[:, 0:NE],
                                wg_ld[:, (kc % 2) * 128:(kc % 2) * 128 + 128],
                                c_ident[0:NE, 0:NE]).then_inc(s_pe, 1)
                    inc("pe")
            blk.tensor(_t)

            def _v(v, pe_b=pe_b):
                v = wrapv(v)
                for kc in range(KC):
                    v.wait_ge(s_pe, pe_b + kc + 1)
                    v.tensor_copy(wgT_sb[:, kc * NE:(kc + 1) * NE],
                                  pbf[:, 0:NE], ms=True)
            blk.vector(_v)

        NSLAB_QKV = DQKV // T

        def qkv_slab_dma(sy, s):
            dma(sy, "ld", wsA[:, :] if s % 2 == 0 else wsB[:, :],
                qkvT.ap().rearrange("(kc p) o -> p kc o", p=128)[:, :, s * T:(s + 1) * T])

        for ti in range(NT):
            with nc.Block(f"A_C0_{ti}") as blk:
                dve_b, act_b, ld_b = val("dve"), val("act"), val("ld")

                def _sy(sy, ti=ti):
                    dma(sy, "ld", u2[:, :].rearrange("p (m d) -> p m d", m=MS),
                        x_in.ap().rearrange("(tt m p) d -> tt p m d",
                                            m=MS, p=128)[ti])
                blk.sync(_sy)

                def _v(v, ti=ti, dve_b=dve_b, act_b=act_b, ld_b=ld_b):
                    v = wrapv(v)
                    v.wait_ge(s_ld, ld_b + 16)
                    for m in range(MS):
                        xm = u2[:, m * D:(m + 1) * D]
                        v.reduce_sum(colf[:, 0:1], xm, axis=AX.X)
                        v.wait_ge(s_act, act_b + 2 * m + 1)
                        v.tensor_scalar_mul(colf[:, 2:3], colf[:, 0:1], 1.0 / D)
                        v.tensor_scalar_mul(colf[:, 3:4], colf[:, 1:2], 1.0 / D)
                        v.tensor_mul(colf[:, 4:5], colf[:, 2:3], colf[:, 2:3])
                        v.tensor_tensor(colf[:, 5:6], colf[:, 3:4], colf[:, 4:5],
                                        ALU.subtract, ms=True)
                        v.wait_ge(s_act, act_b + 2 * m + 2)
                        v.reciprocal(colf[:, 6:7], colf[:, 6:7])
                        v.reciprocal(colf[:, 7:8], colf[:, 7:8])
                        v.tensor_copy(strow[:, 2 * m:2 * m + 1], colf[:, 6:7])
                        v.tensor_mul(strow[:, 2 * m + 1:2 * m + 2],
                                     colf[:, 2:3], colf[:, 6:7])
                        v.tensor_mul(colf[:, 4:5], colf[:, 2:3], colf[:, 7:8])
                        v.tensor_scalar(ub[:, xr_o + m * D: xr_o + (m + 1) * D],
                                        xm, colf[:, 7:8], colf[:, 4:5],
                                        ALU.mult, ALU.subtract, ms=True)
                blk.vector(_v)

                def _a(a, dve_b=dve_b, ld_b=ld_b):
                    a = wrapa(a)
                    a.wait_ge(s_ld, ld_b + 16)
                    for m in range(MS):
                        if m > 0:
                            a.wait_ge(s_dve, dve_b + 2 * m - 1)
                        a.activation(scrA[:, :], u2[:, m * D:(m + 1) * D],
                                     AF.Square, accum_out=colf[:, 1:2], ms=True)
                        a.wait_ge(s_dve, dve_b + 2 * m + 1)
                        a.activation(colf[:, 6:7], colf[:, 5:6], AF.Sqrt,
                                     bias=cbias[:, 0:1])
                        a.activation(colf[:, 7:8], colf[:, 5:6], AF.Sqrt,
                                     scale=1.0 + E2,
                                     bias=cbias[:, 1:2], ms=True)
                blk.scalar(_a)

                def _g(g, ti=ti, dve_b=dve_b):
                    g.wait_ge(s_dve, dve_b + 2 * MS)
                    dma(g, "g", st_sp[ti], strow[:, :])
                    g.wait_ge(s_g, val("g"))
                blk.gpsimd(_g)

            _ablk_guard = nc.Block(f"A_main_{ti}")
            blk = _ablk_guard.__enter__()
            if True:
                pe_b, dve_b = val("pe"), val("dve")
                NG = MS * (KC // 4)

                def _t(t, pe_b=pe_b, dve_b=dve_b):
                    gidx = 0
                    for m in range(MS):
                        for kg in range(KC // 4):
                            if gidx > 0:
                                t.wait_ge(s_dve, dve_b + gidx)
                            for i in range(4):
                                kc = kg * 4 + i
                                t.transpose(pbf[:, i * 128:(i + 1) * 128],
                                            ub[:, xr_o + m * D + kc * 128:
                                               xr_o + m * D + (kc + 1) * 128],
                                            c_ident[:, :]).then_inc(s_pe, 1)
                                inc("pe")
                            gidx += 1
                blk.tensor(_t)

                def _v(v, pe_b=pe_b):
                    v = wrapv(v)
                    gidx = 0
                    for m in range(MS):
                        for kg in range(KC // 4):
                            v.wait_ge(s_pe, pe_b + 4 * (gidx + 1))
                            v.tensor_copy(
                                fm[:, :].rearrange("p (kc t) -> p kc t", t=T)
                                [:, kg * 4:(kg + 1) * 4, m * 128:(m + 1) * 128],
                                pbf[:, :].rearrange("p (a b) -> p a b", b=128)
                            , ms=True)
                            gidx += 1
                blk.vector(_v)

                def _sy(sy):
                    for s in range(2):
                        sdma(sy, qk[s % 2], f"qk{s % 2}",
                             [wsA, wsB][s % 2][:, :],
                             qkvT.ap().rearrange("(kc p) o -> p kc o", p=128)
                             [:, :, s * T:(s + 1) * T])
                blk.sync(_sy)

            if True:
                pe_b, dve_b, act_b = val("pe"), val("dve"), val("act")
                qk_b = [val("qk0") - 16, val("qk1") - 16]

                def _t(t, pe_b=pe_b, dve_b=dve_b, qk_b=qk_b):
                    t.wait_ge(s_dve, dve_b)
                    for s in range(NSLAB_QKV):
                        t.wait_ge(qk[s % 2], qk_b[s % 2] + 16 * (s // 2 + 1))
                        ws = [wsA, wsB][s % 2]
                        for jj in range(4):
                            j = s * 4 + jj
                            if j >= 2:
                                t.wait_ge(s_dve, dve_b + j - 1)
                            pb = [p0, p1][j % 2]
                            for kc in range(KC):
                                ins = t.matmul(pb[:, 0:T],
                                               ws[:, kc * T + jj * 128:
                                                  kc * T + (jj + 1) * 128],
                                               fm_c(kc),
                                               start=(kc == 0),
                                               stop=(kc == KC - 1))
                            ins.then_inc(s_pe, 1)
                            inc("pe")
                blk.tensor(_t)

                def _v(v, pe_b=pe_b):
                    v = wrapv(v)
                    for j in range(4 * NSLAB_QKV):
                        v.wait_ge(s_pe, pe_b + j + 1)
                        off = qb_o if j < KC else (kb_o if j < 2 * KC else vb_o)
                        v.tensor_copy(ub_c(off, j % KC),
                                      [p0, p1][j % 2][:, 0:T], ms=True)
                blk.vector(_v)

                def _a(a, dve_b=dve_b):
                    a = wrapa(a)
                    for j in range(2 * KC):
                        a.wait_ge(s_dve, dve_b + j + 1)
                        src_off = qb_o if j < KC else kb_o
                        dst_off = q2_o if j < KC else k2_o
                        a.activation(ub_c(dst_off, j % KC),
                                     ub_c(src_off, j % KC),
                                     AF.Square, ms=True)
                blk.scalar(_a)

                def _sy(sy, pe_b=pe_b):
                    for s in range(2, NSLAB_QKV):
                        sy.wait_ge(s_pe, pe_b + 4 * (s - 1))
                        sdma(sy, qk[s % 2], f"qk{s % 2}",
                             [wsA, wsB][s % 2][:, :],
                             qkvT.ap().rearrange("(kc p) o -> p kc o", p=128)
                             [:, :, s * T:(s + 1) * T])
                    sy.wait_ge(s_ld, val("ld"))
                blk.sync(_sy)


            if True:
                pe_b, dve_b, act_b = val("pe"), val("dve"), val("act")

                def _t(t, pe_b=pe_b, dve_b=dve_b, act_b=act_b):
                    t.wait_ge(s_act, act_b)
                    for kc in range(KC):
                        ins = t.matmul(pstat[0:1, 0:T], c_onescol[:, 0:1],
                                       ub_c(q2_o, kc), start=(kc == 0),
                                       stop=(kc == KC - 1))
                    ins.then_inc(s_pe, 1)
                    inc("pe")
                    t.wait_ge(s_dve, dve_b + 1)
                    for kc in range(KC):
                        ins = t.matmul(pstat[0:1, 0:T], c_onescol[:, 0:1],
                                       ub_c(k2_o, kc), start=(kc == 0),
                                       stop=(kc == KC - 1))
                    ins.then_inc(s_pe, 1)
                    inc("pe")
                    t.wait_ge(s_dve, dve_b + 3)
                    t.matmul(p2[:, 0:T], c_onesrow[0:1, :], bw_all[0:1, 0:T],
                             start=True, stop=True).then_inc(s_pe, 1)
                    inc("pe")
                    t.matmul(p3[:, 0:T], c_onesrow[0:1, :], bw_all[0:1, T:2 * T],
                             start=True, stop=True).then_inc(s_pe, 1)
                    inc("pe")
                blk.tensor(_t)

                def _v(v, pe_b=pe_b, act_b=act_b):
                    v = wrapv(v)
                    v.wait_ge(s_pe, pe_b + 1)
                    v.tensor_copy(u2[0:1, 0:T], pstat[0:1, 0:T], ms=True)
                    v.wait_ge(s_pe, pe_b + 2)
                    v.tensor_copy(u2[0:1, T:2 * T], pstat[0:1, 0:T], ms=True)
                    v.wait_ge(s_act, act_b + 1)
                    v.reciprocal(u2[0:1, 0:T], u2[0:1, 0:T])
                    v.reciprocal(u2[0:1, T:2 * T], u2[0:1, T:2 * T])
                    v.tensor_copy(bw_all[0:1, 0:2 * T], u2[0:1, 0:2 * T], ms=True)
                    v.wait_ge(s_pe, pe_b + 3)
                    v.tensor_copy(bq_sb[:, :], p2[:, 0:T])
                    v.wait_ge(s_pe, pe_b + 4)
                    v.tensor_copy(bk_sb[:, :], p3[:, 0:T])
                blk.vector(_v)

                def _a(a, dve_b=dve_b):
                    a = wrapa(a)
                    a.wait_ge(s_dve, dve_b + 2)
                    a.activation(u2[0:1, 0:T], u2[0:1, 0:T], AF.Sqrt,
                                 scale=1.0 / D, bias=cbias[0:1, 2:3])
                    a.activation(u2[0:1, T:2 * T], u2[0:1, T:2 * T],
                                 AF.Sqrt, scale=1.0 / D,
                                 bias=cbias[0:1, 2:3], ms=True)
                blk.scalar(_a)

            if True:
                dve_b, act_b = val("dve"), val("act")
                tp = [tmpA, tmpB]

                def _v(v, dve_b=dve_b, act_b=act_b):
                    v = wrapv(v)
                    for j in range(KC):
                        if j >= 2:
                            v.wait_ge(s_act, act_b + j - 1)
                        v.tensor_tensor(tp[j % 2][:, :], ub_c(qb_o, j),
                                        bq_sb[:, :], ALU.mult, ms=True)
                    for j in range(KC):
                        v.wait_ge(s_act, act_b + KC + max(0, j - 1))
                        v.tensor_tensor(tp[j % 2][:, :], ub_c(kb_o, j),
                                        bk_sb[:, :], ALU.mult, ms=True)
                    for j in range(KC):
                        v.wait_ge(s_act, act_b + KC + j + 1)
                        v.tensor_tensor(ub_c(vb_o, j), ub_c(kb_o, j),
                                        ub_c(vb_o, j), ALU.mult)
                        v.tensor_tensor_scan(ub_c(kb_o, j), ub_c(kb_o, j),
                                             c_zero[:, :], 0.0,
                                             ALU.add, ALU.add)
                        v.tensor_copy(ttot[:, j:j + 1],
                                      ub_c(kb_o, j)[:, T - 1:T])
                        v.tensor_tensor(wtot[:, j:j + 1], wtot[:, j:j + 1],
                                        ttot[:, j:j + 1], ALU.add)
                        v.tensor_tensor_scan(ub_c(vb_o, j), ub_c(vb_o, j),
                                             c_zero[:, :], 0.0,
                                             ALU.add, ALU.add)
                        v.tensor_copy(ttot[:, KC + j:KC + j + 1],
                                      ub_c(vb_o, j)[:, T - 1:T])
                        v.tensor_tensor(wtot[:, KC + j:KC + j + 1],
                                        wtot[:, KC + j:KC + j + 1],
                                        ttot[:, KC + j:KC + j + 1],
                                        ALU.add, ms=True)
                blk.vector(_v)

                def _a(a, dve_b=dve_b):
                    a = wrapa(a)
                    for j in range(KC):
                        a.wait_ge(s_dve, dve_b + j + 1)
                        a.activation(ub_c(qb_o, j), tp[j % 2][:, :],
                                     AF.Sigmoid, ms=True)
                    for j in range(KC):
                        a.wait_ge(s_dve, dve_b + KC + j + 1)
                        a.activation(ub_c(kb_o, j), tp[j % 2][:, :],
                                     AF.Exp, ms=True)
                blk.scalar(_a)

                dve_end = dve_b + 3 * KC
                act_end = act_b + 2 * KC

                def _g(g, ti=ti, dve_end=dve_end, act_end=act_end):
                    g.wait_ge(s_dve, dve_end)
                    g.wait_ge(s_act, act_end)
                    dma(g, "g", sp_view(sq_sp, ti), ub_3d(qb_o))
                    dma(g, "g", sp_view(w_sp, ti), ub_3d(kb_o))
                    dma(g, "g", sp_view(wv_sp, ti), ub_3d(vb_o))
                    dma(g, "g", tt_sp[ti], ttot[:, :])
                    g.wait_ge(s_g, val("g"))
                blk.gpsimd(_g)

            _ablk_guard.__exit__(None, None, None)

        gbuf = scrA
        with nc.Block("cc1") as blk:
            def _g(g):
                g.wait_ge(s_cc, 5)
                dma(g, "g", cc_in[:, :], wtot[:, :])
                g.wait_ge(s_g, val("g"))
                g.collective_compute(
                    "AllGather", ALU.bypass,
                    replica_groups=[list(range(NCORES))],
                    ins=[cc_in.ap().opt()],
                    outs=[cc_out.ap().opt()]).then_inc(s_cc, 1)
                inc("cc")
                g.wait_ge(s_cc, val("cc"))
                dma(g, "g", gbuf[:, 0:NCORES * 2 * KC].rearrange("p (r c) -> p r c", r=NCORES),
                    cc_out.ap().rearrange("(r p) c -> p r c", p=128))
                g.wait_ge(s_g, val("g"))
            blk.gpsimd(_g)

        with nc.Block("cc2") as blk:
            def _v(v):
                v = wrapv(v)
                v.memset(carry_w[:, :], 0.0)
                v.memset(carry_wv[:, :], 0.0)
                for r in range(NCORES):
                    v.tensor_scalar_mul(tmpA[:, 0:2 * KC],
                                        gbuf[:, r * 2 * KC:(r + 1) * 2 * KC],
                                        c_mask[:, r:r + 1])
                    v.tensor_tensor(carry_w[:, :], carry_w[:, :],
                                    tmpA[:, 0:KC], ALU.add)
                    v.tensor_tensor(carry_wv[:, :], carry_wv[:, :],
                                    tmpA[:, KC:2 * KC], ALU.add)
                v.tensor_scalar_add(carry_w[:, :], carry_w[:, :], 1e-6)
            blk.vector(_v)

        NSW = DFF // T
        NDR = D // 512

        tile_swbase = {}

        def emit_tile_loads(sy, ti):
            tile_swbase[ti] = {k: val(k) for k in
                               ("swu0", "swu1", "swg0", "swg1")}
            dma(sy, "ld", strow[:, :], st_sp[ti])
            dma(sy, "ld", ttot[:, :], tt_sp[ti])
            dma(sy, "ld", ub_3d(qb_o), sp_view(sq_sp, ti))
            dma(sy, "ld", ub_3d(kb_o), sp_view(w_sp, ti))
            dma(sy, "ld", ub_3d(vb_o), sp_view(wv_sp, ti))
            dma(sy, "ld", u2[:, :].rearrange("p (m d) -> p m d", m=MS),
                x_in.ap().rearrange("(tt m p) d -> tt p m d",
                                    m=MS, p=128)[ti])
            OWp = 256
            for st in range(2):
                o0 = st * OWp
                sdma(sy, sw_u[st % 2], f"swu{st % 2}",
                     wsA[:, (st % 2) * KC * OWp:
                         (st % 2 + 1) * KC * OWp].rearrange(
                         "p (kc o) -> p kc o", o=OWp),
                     swT.ap().rearrange("(kc p) o -> p kc o", p=128)
                     [:, :, o0:o0 + OWp])
                sdma(sy, sw_g[st % 2], f"swg{st % 2}",
                     wsB[:, (st % 2) * KC * OWp:
                         (st % 2 + 1) * KC * OWp].rearrange(
                         "p (kc o) -> p kc o", o=OWp),
                     swT.ap().rearrange("(kc p) o -> p kc o", p=128)
                     [:, :, DFF + o0:DFF + o0 + OWp])
            sy.wait_ge(s_ld, val("ld"))

        for ti in range(NT):
            slotb = {k: val(k) for k in ("swu0", "swu1", "swg0", "swg1",
                                         "ops0", "ops1", "mh10", "mh11",
                                         "mh20", "mh21", "w30", "w31")}
            if ti == 0:
                with nc.Block(f"B_L0_{ti}") as blk:
                    def _sy(sy, ti=ti):
                        emit_tile_loads(sy, ti)
                    blk.sync(_sy)

            OW = 256
            NJ = DFF // 128
            NST = DFF // OW
            with nc.Block(f"B_sw_{ti}") as blk:
                pe_b, dve_b, act_b = val("pe"), val("dve"), val("act")
                swu_b = [tile_swbase[ti]["swu0"], tile_swbase[ti]["swu1"]]
                swg_b = [tile_swbase[ti]["swg0"], tile_swbase[ti]["swg1"]]
                tp = [tmpA, tmpB]

                def _t(t, pe_b=pe_b, dve_b=dve_b, act_b=act_b,
                       swu_b=swu_b, swg_b=swg_b):
                    for j in range(NJ):
                        st = j // 2
                        jj = j % 2
                        if jj == 0:
                            t.wait_ge(sw_u[st % 2],
                                      swu_b[st % 2] + 16 * (st // 2 + 1))
                            t.wait_ge(sw_g[st % 2],
                                      swg_b[st % 2] + 16 * (st // 2 + 1))
                        pu = [p0, p1][j % 2]
                        pg = [p2, p3][j % 2]
                        if j >= 2:
                            t.wait_ge(s_dve, dve_b + 16 + j - 1)
                            t.wait_ge(s_act, act_b + 16 + j - 1)
                        if j == 4:
                            t.wait_ge(s_dve, dve_b + KC)
                        ubase = (st % 2) * KC * OW
                        for kc in range(KC):
                            if j < 4:
                                t.wait_ge(s_dve, dve_b + kc + 1)
                            ins = t.matmul(pu[:, 0:T],
                                           wsA[:, ubase + kc * OW + jj * 128:
                                               ubase + kc * OW + (jj + 1) * 128],
                                           ub_c(y_o, kc),
                                           start=(kc == 0), stop=(kc == KC - 1))
                        ins.then_inc(s_pe, 1)
                        inc("pe")
                        for kc in range(KC):
                            if j < 4:
                                t.wait_ge(s_dve, dve_b + kc + 1)
                            ins = t.matmul(pg[:, 0:T],
                                           wsB[:, ubase + kc * OW + jj * 128:
                                               ubase + kc * OW + (jj + 1) * 128],
                                           ub_c(y_o, kc),
                                           start=(kc == 0), stop=(kc == KC - 1))
                        ins.then_inc(s_pe, 1)
                        inc("pe")
                blk.tensor(_t)

                def _a(a, pe_b=pe_b, dve_b=dve_b):
                    a = wrapa(a)
                    for kc in range(KC):
                        if kc >= 4:
                            a.wait_ge(s_dve, dve_b + kc - 3)
                        sl = scrA[:, (kc % 4) * 512:(kc % 4) * 512 + T]
                        a.activation(sl, ub_c(kb_o, kc), AF.Ln,
                                     bias=carry_w[:, kc:kc + 1])
                        a.activation(sl, sl, AF.Exp, scale=-1.0, ms=True)
                    for j in range(NJ):
                        a.wait_ge(s_pe, pe_b + 2 * j + 2)
                        a.activation(tp[j % 2][:, :], [p2, p3][j % 2][:, 0:T],
                                     AF.Silu, ms=True)
                blk.scalar(_a)

                def _v(v, pe_b=pe_b, act_b=act_b):
                    v = wrapv(v)
                    for kc in range(KC):
                        v.tensor_scalar(tmpB[:, :], ub_c(vb_o, kc),
                                        carry_wv[:, kc:kc + 1], None, ALU.add)
                        v.wait_ge(s_act, act_b + kc + 1)
                        v.tensor_mul(tmpB[:, :], tmpB[:, :],
                                     scrA[:, (kc % 4) * 512:(kc % 4) * 512 + T])
                        v.tensor_tensor(ub_c(y_o, kc), tmpB[:, :],
                                        ub_c(qb_o, kc), ALU.mult, ms=True)
                    for m in range(MS):
                        v.tensor_scalar(u2[:, m * D:(m + 1) * D],
                                        u2[:, m * D:(m + 1) * D],
                                        strow[:, 2 * m:2 * m + 1],
                                        strow[:, 2 * m + 1:2 * m + 2],
                                        ALU.mult, ALU.subtract)
                    v.tensor_tensor(carry_w[:, :], carry_w[:, :],
                                    ttot[:, 0:KC], ALU.add)
                    v.tensor_tensor(carry_wv[:, :], carry_wv[:, :],
                                    ttot[:, KC:2 * KC], ALU.add)
                    for j in range(NJ):
                        v.wait_ge(s_act, act_b + 16 + j + 1)
                        v.tensor_tensor(ub_c(0, j), tp[j % 2][:, :],
                                        [p0, p1][j % 2][:, 0:T],
                                        ALU.mult, ms=True)
                blk.vector(_v)

                def _sy(sy, pe_b=pe_b):
                    for st in range(2, NST):
                        sy.wait_ge(s_pe, pe_b + 4 * (st - 1))
                        o0 = st * OW
                        sdma(sy, sw_u[st % 2], f"swu{st % 2}",
                             wsA[:, (st % 2) * KC * OW:
                                 (st % 2 + 1) * KC * OW].rearrange(
                                 "p (kc o) -> p kc o", o=OW),
                             swT.ap().rearrange("(kc p) o -> p kc o", p=128)
                             [:, :, o0:o0 + OW])
                        sdma(sy, sw_g[st % 2], f"swg{st % 2}",
                             wsB[:, (st % 2) * KC * OW:
                                 (st % 2 + 1) * KC * OW].rearrange(
                                 "p (kc o) -> p kc o", o=OW),
                             swT.ap().rearrange("(kc p) o -> p kc o", p=128)
                             [:, :, DFF + o0:DFF + o0 + OW])
                    sy.wait_ge(s_pe, pe_b + 2 * NJ)
                    for st in range(2):
                        sdma(sy, op_s[st % 2], f"ops{st % 2}",
                             [wsA, wsB][st % 2][:, :].rearrange(
                                 "p (kc d) -> p kc d", d=512),
                             woT.ap().rearrange("(kc p) d -> p kc d", p=128)
                             [:, (st % 4) * 16:(st % 4 + 1) * 16, 0:512])
                    sy.wait_ge(s_ld, val("ld"))
                blk.sync(_sy)

            with nc.Block(f"B_op_{ti}") as blk:
                pe_b, dve_b = val("pe"), val("dve")
                ops_b = [slotb["ops0"], slotb["ops1"]]

                def _t(t, pe_b=pe_b, dve_b=dve_b, ops_b=ops_b):
                    for st in range(16):
                        dr, qi = st // 4, st % 4
                        t.wait_ge(op_s[st % 2],
                                  ops_b[st % 2] + 16 * (st // 2 + 1))
                        if qi == 0 and dr > 0:
                            t.wait_ge(s_dve, dve_b + dr)
                        buf = [wsA, wsB][st % 2]
                        for kcl in range(16):
                            kc = qi * 16 + kcl
                            for m in range(MS):
                                ins = t.matmul(
                                    pms[m][:, 0:512],
                                    ub[:, kc * T + m * 128:
                                       kc * T + (m + 1) * 128],
                                    buf[:, kcl * 512:(kcl + 1) * 512],
                                    start=(qi == 0 and kcl == 0),
                                    stop=(qi == 3 and kcl == 15))
                        ins.then_inc(s_pe, 1)
                        inc("pe")
                blk.tensor(_t)

                def _v(v, pe_b=pe_b):
                    v = wrapv(v)
                    for dr in range(NDR):
                        v.wait_ge(s_pe, pe_b + 4 * (dr + 1))
                        for m in range(MS):
                            v.tensor_tensor(
                                u2[:, m * D + dr * 512:m * D + (dr + 1) * 512],
                                u2[:, m * D + dr * 512:m * D + (dr + 1) * 512],
                                pms[m][:, 0:512], ALU.add, ms=(m == MS - 1))
                blk.vector(_v)

                def _sy(sy, pe_b=pe_b):
                    for st in range(2, 16):
                        dr, qi = st // 4, st % 4
                        sy.wait_ge(s_pe, pe_b + st - 1)
                        sdma(sy, op_s[st % 2], f"ops{st % 2}",
                             [wsA, wsB][st % 2][:, :].rearrange(
                                 "p (kc d) -> p kc d", d=512),
                             woT.ap().rearrange("(kc p) d -> p kc d", p=128)
                             [:, qi * 16:(qi + 1) * 16,
                              dr * 512:(dr + 1) * 512])
                    sy.wait_ge(s_ld, val("ld"))
                blk.sync(_sy)


            with nc.Block(f"B_x1_{ti}") as blk:
                dve_b, act_b = val("dve"), val("act")
                xl_b, xsp_b = val("xl"), val("xsp")

                def _sy(sy, ti=ti, dve_b=dve_b, xsp_b=xsp_b):
                    for m in range(MS):
                        if m >= 1:
                            sy.wait_ge(s_dve, dve_b + 4 * m)
                            sy.wait_ge(xsp_s, xsp_b + 16 * m)
                        sy.dma_start(
                            scrB[:, :],
                            x_in.ap()[ti * T + m * 128:
                                      ti * T + (m + 1) * 128, :]).then_inc(
                                          xl_s, 16)
                        inc("xl", 16)
                blk.sync(_sy)

                def _v(v, dve_b=dve_b, act_b=act_b, xl_b=xl_b):
                    v = wrapv(v)
                    for m in range(MS):
                        am = u2[:, m * D:(m + 1) * D]
                        v.reduce_sum(colf[:, 8:9], am, axis=AX.X)
                        v.wait_ge(s_act, act_b + 4 * m + 1)
                        v.tensor_scalar_mul(colf[:, 11:12], colf[:, 9:10], 1.0 / D)
                        v.tensor_scalar_mul(colf[:, 10:11], colf[:, 8:9], 1.0 / D)
                        v.tensor_mul(colf[:, 12:13], colf[:, 10:11], colf[:, 10:11])
                        v.tensor_tensor(colf[:, 13:14], colf[:, 11:12],
                                        colf[:, 12:13], ALU.subtract, ms=True)
                        v.wait_ge(s_act, act_b + 4 * m + 2)
                        v.reciprocal(colf[:, 12:13], colf[:, 12:13])
                        v.tensor_mul(colf[:, 15:16], colf[:, 10:11], colf[:, 12:13])
                        v.wait_ge(xl_s, xl_b + 16 * (m + 1))
                        v.scalar_tensor_tensor(scrB[:, :], am, colf[:, 12:13],
                                               scrB[:, :], ALU.mult, ALU.add)
                        v.tensor_scalar(scrB[:, :], scrB[:, :], colf[:, 15:16],
                                        None, ALU.subtract, ms=True)
                        v.reduce_sum(colf[:, 8:9], scrB[:, :], axis=AX.X)
                        v.wait_ge(s_act, act_b + 4 * m + 3)
                        v.tensor_scalar_mul(colf[:, 11:12], colf[:, 9:10], 1.0 / D)
                        v.tensor_scalar_mul(colf[:, 10:11], colf[:, 8:9], 1.0 / D)
                        v.tensor_mul(colf[:, 12:13], colf[:, 10:11], colf[:, 10:11])
                        v.tensor_tensor(colf[:, 13:14], colf[:, 11:12],
                                        colf[:, 12:13], ALU.subtract, ms=True)
                        v.wait_ge(s_act, act_b + 4 * m + 4)
                        v.reciprocal(colf[:, 12:13], colf[:, 12:13])
                        v.tensor_mul(colf[:, 15:16], colf[:, 10:11], colf[:, 12:13])
                        v.tensor_scalar(ub[:, m * D:(m + 1) * D], scrB[:, :],
                                        colf[:, 12:13], colf[:, 15:16],
                                        ALU.mult, ALU.subtract, ms=True)
                blk.vector(_v)

                def _a(a, dve_b=dve_b):
                    a = wrapa(a)
                    for m in range(MS):
                        if m >= 1:
                            a.wait_ge(s_dve, dve_b + 4 * m)
                        a.activation(scrA[:, :], u2[:, m * D:(m + 1) * D],
                                     AF.Square, accum_out=colf[:, 9:10],
                                     ms=True)
                        a.wait_ge(s_dve, dve_b + 4 * m + 1)
                        a.activation(colf[:, 12:13], colf[:, 13:14], AF.Sqrt,
                                     bias=cbias[:, 0:1], ms=True)
                        a.wait_ge(s_dve, dve_b + 4 * m + 2)
                        a.activation(scrA[:, :], scrB[:, :], AF.Square,
                                     accum_out=colf[:, 9:10], ms=True)
                        a.wait_ge(s_dve, dve_b + 4 * m + 3)
                        a.activation(colf[:, 12:13], colf[:, 13:14], AF.Sqrt,
                                     bias=cbias[:, 0:1], ms=True)
                blk.scalar(_a)

                def _g(g, ti=ti, dve_b=dve_b):
                    for m in range(MS):
                        g.wait_ge(s_dve, dve_b + 4 * m + 2)
                        g.dma_start(
                            x1_sp.ap()[ti * T + m * 128:
                                       ti * T + (m + 1) * 128, :],
                            scrB[:, :]).then_inc(xsp_s, 16)
                        inc("xsp", 16)
                    g.wait_ge(xsp_s, val("xsp"))
                blk.gpsimd(_g)


            with nc.Block(f"B_trt_{ti}") as blk:
                pe_b, dve_b, act_b = val("pe"), val("dve"), val("act")

                def _t(t, pe_b=pe_b, dve_b=dve_b):
                    for m in range(MS):
                        for kg in range(KC // 4):
                            if m > 0 or kg > 0:
                                t.wait_ge(s_dve,
                                          dve_b + 8 * m + kg if kg > 0
                                          else dve_b + 8 * m)
                            for i in range(4):
                                kc = kg * 4 + i
                                t.transpose(pbf[:, i * 128:(i + 1) * 128],
                                            ub[:, m * D + kc * 128:
                                               m * D + (kc + 1) * 128],
                                            c_ident[:, :]).then_inc(s_pe, 1)
                                inc("pe")
                        t.wait_ge(s_dve, dve_b + 8 * m + 4)
                        for kc in range(KC):
                            ins = t.matmul(p4[:, 0:NE],
                                           fm[:, kc * T + m * 128:
                                              kc * T + (m + 1) * 128],
                                           wgT_sb[:, kc * NE:(kc + 1) * NE],
                                           start=(kc == 0), stop=(kc == KC - 1))
                        ins.then_inc(s_pe, 1)
                        inc("pe")
                        t.wait_ge(s_dve, dve_b + 8 * m + 6)
                        for e in range(4):
                            ins = t.transpose(
                                pbf[0:1, e * 128:(e + 1) * 128],
                                bq_sb[:, e:e + 1], c_ident[:, :])
                        ins.then_inc(s_pe, 1)
                        inc("pe")
                        t.wait_ge(s_dve, dve_b + 8 * m + 7)
                        for e in range(4):
                            ins = t.transpose(
                                pbf[0:1, e * 128:(e + 1) * 128],
                                bq_sb[:, 4 + e:5 + e], c_ident[:, :])
                        ins.then_inc(s_pe, 1)
                        inc("pe")
                    t.wait_ge(s_dve, dve_b + 8 * MS)
                    for e in range(NE):
                        pb = [p2, p3][e % 2]
                        if e >= 2:
                            t.wait_ge(s_dve, dve_b + 8 * MS + e - 1)
                        t.matmul(pb[:, 0:T], c_onesrow[0:1, :],
                                 bw_all[0:1, e * T:(e + 1) * T],
                                 start=True, stop=True).then_inc(s_pe, 1)
                        inc("pe")
                blk.tensor(_t)

                def _v(v, pe_b=pe_b, dve_b=dve_b, act_b=act_b):
                    v = wrapv(v)
                    for m in range(MS):
                        for kg in range(KC // 4):
                            v.wait_ge(s_pe, pe_b + 19 * m + 4 * (kg + 1))
                            v.tensor_copy(
                                fm[:, :].rearrange("p (kc t) -> p kc t", t=T)
                                [:, kg * 4:(kg + 1) * 4, m * 128:(m + 1) * 128],
                                pbf[:, :].rearrange("p (a b) -> p a b", b=128),
                                ms=True)
                        v.wait_ge(s_pe, pe_b + 19 * m + 17)
                        v.tensor_copy(tmpA[:, 0:NE], p4[:, 0:NE])
                        v.reduce_max(colf[:, 8:9], tmpA[:, 0:NE], axis=AX.X)
                        v.tensor_single_scalar(tmpB[:, 0:NE], tmpA[:, 0:NE],
                                               colf[:, 8:9], ALU.is_ge)
                        v.scalar_tensor_tensor(tmpB[:, 8:16], tmpB[:, 0:NE],
                                               BIG_NEG, tmpA[:, 0:NE],
                                               ALU.mult, ALU.add)
                        v.reduce_max(colf[:, 9:10], tmpB[:, 8:16], axis=AX.X)
                        v.tensor_tensor(colf[:, 10:11], colf[:, 8:9],
                                        colf[:, 9:10],
                                        ALU.subtract, ms=True)
                        v.wait_ge(s_act, act_b + m + 1)
                        v.tensor_tensor(colf[:, 13:14], colf[:, 11:12],
                                        colf[:, 12:13], ALU.subtract)
                        v.tensor_single_scalar(tmpA[:, 8:16], tmpA[:, 0:NE],
                                               colf[:, 9:10], ALU.is_ge)
                        v.tensor_single_scalar(tmpA[:, 0:NE], tmpA[:, 8:16],
                                               colf[:, 12:13], ALU.mult)
                        v.scalar_tensor_tensor(tmpA[:, 0:NE], tmpB[:, 0:NE],
                                               colf[:, 13:14], tmpA[:, 0:NE],
                                               ALU.mult, ALU.add)
                        v.tensor_copy(bq_sb[:, 0:NE],
                                      tmpA[:, 0:NE], ms=True)
                        v.wait_ge(s_pe, pe_b + 19 * m + 18)
                        for e in range(4):
                            v.tensor_copy(
                                bw_all[0:1, e * T + m * 128:
                                       e * T + (m + 1) * 128],
                                pbf[0:1, e * 128:(e + 1) * 128], ms=(e == 3))
                        v.wait_ge(s_pe, pe_b + 19 * m + 19)
                        for e in range(4):
                            v.tensor_copy(
                                bw_all[0:1, (4 + e) * T + m * 128:
                                       (4 + e) * T + (m + 1) * 128],
                                pbf[0:1, e * 128:(e + 1) * 128], ms=(e == 3))
                    for e in range(NE):
                        v.wait_ge(s_pe, pe_b + 19 * MS + e + 1)
                        v.tensor_copy(bw_all[:, e * T:(e + 1) * T],
                                      [p2, p3][e % 2][:, 0:T], ms=True)
                blk.vector(_v)

                def _a(a, dve_b=dve_b):
                    a = wrapa(a)
                    for m in range(MS):
                        a.wait_ge(s_dve, dve_b + 8 * m + 5)
                        a.activation(colf[:, 11:12], colf[:, 10:11], AF.Sigmoid)
                        a.activation(colf[:, 12:13], colf[:, 10:11], AF.Sigmoid,
                                     scale=-1.0, ms=True)
                blk.scalar(_a)

                def _sy(sy):
                    for st in range(2):
                        e, h0, hw = (0, st * 256, 256)
                        w12_slab(sy, m1s[st % 2], f"mh1{st % 2}", wsA, st,
                                 w1T, e, h0, hw)
                        w12_slab(sy, m2s[st % 2], f"mh2{st % 2}", wsB, st,
                                 w2T, e, h0, hw)
                blk.sync(_sy)


            MH_STEPS = []
            for e in range(NE):
                for q, (h0, hw) in enumerate([(0, 256), (256, 256), (512, 256),
                                              (768, 256), (1024, 256),
                                              (1280, 128)]):
                    MH_STEPS.append((e, h0, hw))
            GCUM = [0]
            for (_, _, hw) in MH_STEPS:
                GCUM.append(GCUM[-1] + hw // 128)

            with nc.Block(f"B_mh_{ti}") as blk:
                pe_b, dve_b, act_b = val("pe"), val("dve"), val("act")
                tp = [tmpA, tmpB]

                def _t(t, pe_b=pe_b, dve_b=dve_b, slotb=slotb):
                    for st, (e, h0, hw) in enumerate(MH_STEPS):
                        t.wait_ge(m1s[st % 2],
                                  slotb[f"mh1{st % 2}"] + 32 * (st // 2 + 1))
                        t.wait_ge(m2s[st % 2],
                                  slotb[f"mh2{st % 2}"] + 32 * (st // 2 + 1))
                        base = (st % 2) * (KC * 256)
                        for hj in range(hw // 128):
                            g = GCUM[st] + hj
                            pa = [p0, p1][g % 2]
                            pb = [p2, p3][g % 2]
                            if g >= 2:
                                t.wait_ge(s_dve, dve_b + 2 * (g - 1))
                            for kc in range(KC):
                                ins = t.matmul(pa[:, 0:T],
                                               wsA[:, base + kc * hw + hj * 128:
                                                   base + kc * hw + (hj + 1) * 128],
                                               fm_c(kc), start=(kc == 0),
                                               stop=(kc == KC - 1))
                            ins.then_inc(s_pe, 1)
                            inc("pe")
                            for kc in range(KC):
                                ins = t.matmul(pb[:, 0:T],
                                               wsB[:, base + kc * hw + hj * 128:
                                                   base + kc * hw + (hj + 1) * 128],
                                               fm_c(kc), start=(kc == 0),
                                               stop=(kc == KC - 1))
                            ins.then_inc(s_pe, 1)
                            inc("pe")
                blk.tensor(_t)

                def _a(a, pe_b=pe_b, dve_b=dve_b):
                    a = wrapa(a)
                    for g in range(GCUM[-1]):
                        a.wait_ge(s_pe, pe_b + 2 * g + 1)
                        a.activation(tp[g % 2][:, :], [p0, p1][g % 2][:, 0:T],
                                     AF.Silu, ms=True)
                blk.scalar(_a)

                def _v(v, pe_b=pe_b, act_b=act_b):
                    v = wrapv(v)
                    for st, (e, h0, hw) in enumerate(MH_STEPS):
                        for hj in range(hw // 128):
                            g = GCUM[st] + hj
                            v.wait_ge(s_act, act_b + g + 1)
                            v.wait_ge(s_pe, pe_b + 2 * g + 2)
                            v.tensor_tensor(tp[g % 2][:, :], tp[g % 2][:, :],
                                            [p2, p3][g % 2][:, 0:T],
                                            ALU.mult, ms=True)
                            slot = e * HC + h0 // 128 + hj
                            v.tensor_tensor(ub_c(0, slot), tp[g % 2][:, :],
                                            bw_all[:, e * T:(e + 1) * T],
                                            ALU.mult, ms=True)
                blk.vector(_v)

                def _sy(sy, pe_b=pe_b, slotb=slotb):
                    for st in range(2, len(MH_STEPS)):
                        e, h0, hw = MH_STEPS[st]
                        sy.wait_ge(s_pe, pe_b + 2 * GCUM[st - 1])
                        w12_slab(sy, m1s[st % 2], f"mh1{st % 2}", wsA, st,
                                 w1T, e, h0, hw)
                        w12_slab(sy, m2s[st % 2], f"mh2{st % 2}", wsB, st,
                                 w2T, e, h0, hw)
                    sy.wait_ge(s_pe, pe_b + 2 * GCUM[-1])
                    for st in range(2):
                        w3_slab(sy, st, 0, st)
                    sy.wait_ge(s_ld, val("ld"))
                blk.sync(_sy)

            with nc.Block(f"B_w3_{ti}") as blk:
                pe_b, dve_b = val("pe"), val("dve")

                def _t(t, pe_b=pe_b, dve_b=dve_b, slotb=slotb):
                    for st in range(NDR * NE):
                        dr, e = st // NE, st % NE
                        t.wait_ge(w3s[st % 2],
                                  slotb[f"w3{st % 2}"] + 32 * (st // 2 + 1))
                        if e == 0 and dr > 0:
                            t.wait_ge(s_dve, dve_b + dr)
                        buf = [wsA, wsB][st % 2]
                        for kc in range(HC):
                            for m in range(MS):
                                ins = t.matmul(
                                    pms[m][:, 0:512],
                                    ub[:, (e * HC + kc) * T + m * 128:
                                       (e * HC + kc) * T + (m + 1) * 128],
                                    buf[:, kc * 512:(kc + 1) * 512],
                                    start=(e == 0 and kc == 0),
                                    stop=(e == NE - 1 and kc == HC - 1))
                        ins.then_inc(s_pe, 1)
                        inc("pe")
                blk.tensor(_t)

                def _v(v, pe_b=pe_b):
                    v = wrapv(v)
                    for dr in range(NDR):
                        v.wait_ge(s_pe, pe_b + NE * (dr + 1))
                        for m in range(MS):
                            v.tensor_copy(
                                u2[:, m * D + dr * 512:m * D + (dr + 1) * 512],
                                pms[m][:, 0:512], ms=(m == MS - 1))
                blk.vector(_v)

                def _sy(sy, pe_b=pe_b, slotb=slotb):
                    for st in range(2, NDR * NE):
                        dr, e = st // NE, st % NE
                        sy.wait_ge(s_pe, pe_b + st - 1)
                        w3_slab(sy, st, dr, e)
                    sy.wait_ge(s_ld, val("ld"))
                blk.sync(_sy)


            with nc.Block(f"B_x2_{ti}") as blk:
                dve_b, act_b = val("dve"), val("act")
                xl_b, xsp_b = val("xl"), val("xsp")

                def _sy(sy, ti=ti, dve_b=dve_b, xsp_b=xsp_b):
                    for m in range(MS):
                        if m >= 1:
                            sy.wait_ge(s_dve, dve_b + 2 * m)
                            sy.wait_ge(xsp_s, xsp_b + 16 * m)
                        sy.dma_start(
                            scrB[:, :],
                            x1_sp.ap()[ti * T + m * 128:
                                       ti * T + (m + 1) * 128, :]).then_inc(
                                           xl_s, 16)
                        inc("xl", 16)
                    if ti + 1 < NT:
                        sy.wait_ge(s_dve, dve_b + 2 * MS)
                        emit_tile_loads(sy, ti + 1)
                blk.sync(_sy)

                def _v(v, dve_b=dve_b, act_b=act_b, xl_b=xl_b):
                    v = wrapv(v)
                    for m in range(MS):
                        fmv = u2[:, m * D:(m + 1) * D]
                        v.reduce_sum(colf[:, 8:9], fmv, axis=AX.X)
                        v.wait_ge(s_act, act_b + 2 * m + 1)
                        v.tensor_scalar_mul(colf[:, 11:12], colf[:, 9:10], 1.0 / D)
                        v.tensor_scalar_mul(colf[:, 10:11], colf[:, 8:9], 1.0 / D)
                        v.tensor_mul(colf[:, 12:13], colf[:, 10:11], colf[:, 10:11])
                        v.tensor_tensor(colf[:, 13:14], colf[:, 11:12],
                                        colf[:, 12:13], ALU.subtract, ms=True)
                        v.wait_ge(s_act, act_b + 2 * m + 2)
                        v.reciprocal(colf[:, 12:13], colf[:, 12:13])
                        v.tensor_mul(colf[:, 15:16], colf[:, 10:11], colf[:, 12:13])
                        v.wait_ge(xl_s, xl_b + 16 * (m + 1))
                        v.scalar_tensor_tensor(scrB[:, :], fmv, colf[:, 12:13],
                                               scrB[:, :], ALU.mult, ALU.add)
                        v.tensor_scalar(scrB[:, :], scrB[:, :], colf[:, 15:16],
                                        None, ALU.subtract, ms=True)
                blk.vector(_v)

                def _a(a, dve_b=dve_b):
                    a = wrapa(a)
                    for m in range(MS):
                        if m >= 1:
                            a.wait_ge(s_dve, dve_b + 2 * m)
                        a.activation(scrA[:, :], u2[:, m * D:(m + 1) * D],
                                     AF.Square, accum_out=colf[:, 9:10],
                                     ms=True)
                        a.wait_ge(s_dve, dve_b + 2 * m + 1)
                        a.activation(colf[:, 12:13], colf[:, 13:14], AF.Sqrt,
                                     bias=cbias[:, 0:1], ms=True)
                blk.scalar(_a)

                def _g(g, ti=ti, dve_b=dve_b):
                    for m in range(MS):
                        g.wait_ge(s_dve, dve_b + 2 * m + 2)
                        g.dma_start(
                            y_out.ap()[ti * T + m * 128:
                                       ti * T + (m + 1) * 128, :],
                            scrB[:, :]).then_inc(xsp_s, 16)
                        inc("xsp", 16)
                    g.wait_ge(xsp_s, val("xsp"))
                blk.gpsimd(_g)


        if dbg:
            dumps = [("d_qkvT", qkvT, [D, DQKV], BF16),
                     ("d_swT", swT, [D, DSW], BF16),
                     ("d_sq", sq_sp, [128, KC, n_tok], BF16),
                     ("d_w", w_sp, [128, KC, n_tok], BF16),
                     ("d_wv", wv_sp, [128, KC, n_tok], BF16),
                     ("d_st", st_sp, [NT, 128, 2 * MS], F32),
                     ("d_cc", cc_out, [NCORES * 128, 2 * KC], F32),
                     ("d_x1", x1_sp, [n_tok, D], F32)]
            with nc.Block("dbg") as blk:
                def _g(g):
                    for nm, src, sh, dt in dumps:
                        od = nc.dram_tensor(nm, sh, dt, kind="ExternalOutput")
                        spec = ("a b -> (a b)" if len(sh) == 2 else
                                "a b c -> (a b c)")
                        dma(g, "g", od.ap().rearrange(spec),
                            src.ap().rearrange(spec))
                    g.wait_ge(s_g, val("g"))
                blk.gpsimd(_g)

        with nc.Block("fin") as blk:
            def _g(g):
                g.wait_ge(s_g, val("g"))
            blk.gpsimd(_g)

    return nc


_NC_CACHE = {}


def kernel(L0_attn_x, L0_attn_w_qkv, L0_attn_w_swiglu, L0_attn_w_out,
           L0_ffn_Wg, L0_ffn_W1, L0_ffn_W2, L0_ffn_W3):
    B, S, _ = L0_attn_x.shape
    n_tok = (B * S) // NCORES
    chunks_per_seq = NCORES // B

    if n_tok not in _NC_CACHE:
        _NC_CACHE[n_tok] = build_nc(n_tok)
    nc = _NC_CACHE[n_tok]

    x = np.ascontiguousarray(np.asarray(L0_attn_x, dtype=np.float32))
    wq = np.asarray(L0_attn_w_qkv, np.float32)
    wsw = np.asarray(L0_attn_w_swiglu, np.float32)
    wo = np.asarray(L0_attn_w_out, np.float32)
    w1 = np.asarray(L0_ffn_W1, np.float32)
    w2 = np.asarray(L0_ffn_W2, np.float32)
    w3 = np.asarray(L0_ffn_W3, np.float32)
    w3p = np.zeros((NE, D, D), np.float32)
    w3p[:, :, 0:DH] = w3
    SHK = D // NCORES
    SHKO = DFF // NCORES
    consts = {
        "w_g": np.ascontiguousarray(np.asarray(L0_ffn_Wg, np.float32)),
        "ident_bf": np.eye(128, dtype=ml_dtypes.bfloat16),
        "ones_col": np.ones((128, 1), dtype=ml_dtypes.bfloat16),
        "ones_row": np.ones((1, 128), dtype=ml_dtypes.bfloat16),
    }
    in_maps = []
    for c in range(NCORES):
        b = c // chunks_per_seq
        p = c % chunks_per_seq
        shard = np.ascontiguousarray(x[b, p * n_tok:(p + 1) * n_tok, :])
        mrow = np.zeros((NCORES,), dtype=np.float32)
        for r in range(NCORES):
            if r // chunks_per_seq == b and r % chunks_per_seq < p:
                mrow[r] = 1.0
        mask = np.broadcast_to(mrow, (128, NCORES)).copy()
        in_maps.append(dict(
            consts, x=shard, mask=mask,
            sh_qkv=np.ascontiguousarray(wq[:, c * SHK:(c + 1) * SHK]),
            sh_sw=np.ascontiguousarray(wsw[:, c * SHK:(c + 1) * SHK]),
            sh_wo=np.ascontiguousarray(wo[:, c * SHKO:(c + 1) * SHKO]),
            sh_w1=np.ascontiguousarray(w1[:, :, c * SHK:(c + 1) * SHK]),
            sh_w2=np.ascontiguousarray(w2[:, :, c * SHK:(c + 1) * SHK]),
            sh_w3=np.ascontiguousarray(w3p[:, :, c * SHK:(c + 1) * SHK])))

    res = run_bass_kernel_spmd(nc, in_maps, core_ids=list(range(NCORES)))
    out = np.zeros((B, S, D), dtype=np.float32)
    for c in range(NCORES):
        b = c // chunks_per_seq
        p = c % chunks_per_seq
        out[b, p * n_tok:(p + 1) * n_tok, :] = np.asarray(res.results[c]["y"])
    return out


# version 08
# speedup vs baseline: 1.0260x; 1.0260x over previous
import contextlib
import numpy as np
import ml_dtypes

import concourse.bass as bass
import concourse.mybir as mybir
from concourse.bass_utils import run_bass_kernel_spmd

F32 = mybir.dt.float32
BF16 = mybir.dt.bfloat16
AF = mybir.ActivationFunctionType
ALU = mybir.AluOpType
AX = mybir.AxisListType

D = 2048
DQKV = 3 * D
DFF = 8192
DSW = 2 * DFF
NE = 8
DH = 1408
KC = D // 128
HC = DH // 128
E1 = 1e-5
E2 = 1.1920929e-07
NCORES = 8
BIG_NEG = -1.0e9
T = 512
MS = T // 128


def build_nc(n_tok=4096, dbg=False):
    assert n_tok % T == 0
    NT = n_tok // T

    nc = bass.Bass("TRN2", target_bir_lowering=False)

    SHK = D // NCORES
    SHKO = DFF // NCORES
    x_in = nc.dram_tensor("x", [n_tok, D], F32, kind="ExternalInput")
    sh_qkv = nc.dram_tensor("sh_qkv", [DQKV, SHK], F32, kind="ExternalInput")
    sh_sw = nc.dram_tensor("sh_sw", [DSW, SHK], F32, kind="ExternalInput")
    sh_wo = nc.dram_tensor("sh_wo", [D, SHKO], F32, kind="ExternalInput")
    w_g = nc.dram_tensor("w_g", [NE, D], F32, kind="ExternalInput")
    sh_w1 = nc.dram_tensor("sh_w1", [NE, DH, SHK], F32, kind="ExternalInput")
    sh_w2 = nc.dram_tensor("sh_w2", [NE, DH, SHK], F32, kind="ExternalInput")
    sh_w3 = nc.dram_tensor("sh_w3", [NE, D, SHK], F32, kind="ExternalInput")
    ident_bf = nc.dram_tensor("ident_bf", [128, 128], BF16, kind="ExternalInput")
    ones_col = nc.dram_tensor("ones_col", [128, 1], BF16, kind="ExternalInput")
    ones_row = nc.dram_tensor("ones_row", [1, 128], BF16, kind="ExternalInput")
    mask_in = nc.dram_tensor("mask", [128, NCORES], F32, kind="ExternalInput")
    y_out = nc.dram_tensor("y", [n_tok, D], F32, kind="ExternalOutput")

    qkvBs = nc.dram_tensor("qkvBs", [DQKV, SHK], BF16)
    swBs = nc.dram_tensor("swBs", [DSW, SHK], BF16)
    woBs = nc.dram_tensor("woBs", [D, SHKO], BF16)
    wgB = nc.dram_tensor("wgB", [NE, D], BF16)
    w1Bs = nc.dram_tensor("w1Bs", [NE, DH, SHK], BF16)
    w2Bs = nc.dram_tensor("w2Bs", [NE, DH, SHK], BF16)
    w3Bs = nc.dram_tensor("w3Bs", [NE, D, SHK], BF16)
    qkvTs = nc.dram_tensor("qkvTs", [SHK, DQKV], BF16)
    swTs = nc.dram_tensor("swTs", [SHK, DSW], BF16)
    woTs = nc.dram_tensor("woTs", [SHKO, D], BF16)
    w1Ts = nc.dram_tensor("w1Ts", [NE * SHK, DH], BF16)
    w2Ts = nc.dram_tensor("w2Ts", [NE * SHK, DH], BF16)
    w3Ts = nc.dram_tensor("w3Ts", [NE * SHK, D], BF16)
    qkvT = nc.dram_tensor("qkvT", [D, DQKV], BF16, addr_space="Shared")
    swT = nc.dram_tensor("swT", [D, DSW], BF16, addr_space="Shared")
    woT = nc.dram_tensor("woT", [DFF, D], BF16, addr_space="Shared")
    w1T = nc.dram_tensor("w1T", [NCORES * NE * SHK, DH], BF16, addr_space="Shared")
    w2T = nc.dram_tensor("w2T", [NCORES * NE * SHK, DH], BF16, addr_space="Shared")
    w3T = nc.dram_tensor("w3T", [NCORES * NE * SHK, D], BF16, addr_space="Shared")
    sq_sp = nc.dram_tensor("sq_sp", [128, KC, n_tok], BF16)
    w_sp = nc.dram_tensor("w_sp", [128, KC, n_tok], BF16)
    wv_sp = nc.dram_tensor("wv_sp", [128, KC, n_tok], BF16)
    x1_sp = nc.dram_tensor("x1_sp", [n_tok, D], F32)
    st_sp = nc.dram_tensor("st_sp", [NT, 128, 2 * MS], F32)
    tt_sp = nc.dram_tensor("tt_sp", [NT, 128, 2 * KC], F32)
    cc_in = nc.dram_tensor("cc_in", [128, 2 * KC], F32)
    cc_out = nc.dram_tensor("cc_out", [NCORES * 128, 2 * KC], F32,
                            addr_space="Shared")

    n = {}

    def inc(k, a=1):
        n[k] = n.get(k, 0) + a
        return n[k]

    def val(k):
        return n.get(k, 0)

    with contextlib.ExitStack() as est:
        sbt = lambda nm, sh, dt: est.enter_context(nc.sbuf_tensor(nm, sh, dt))
        pstens = lambda nm, sh, dt: est.enter_context(nc.psum_tensor(nm, sh, dt))
        semt = lambda nm: est.enter_context(nc.semaphore(nm))

        c_ident = sbt("c_ident", [128, 128], BF16)
        c_onescol = sbt("c_onescol", [128, 1], BF16)
        c_onesrow = sbt("c_onesrow", [1, 128], BF16)
        c_mask = sbt("c_mask", [128, NCORES], F32)
        c_zero = sbt("c_zero", [128, T], BF16)
        wgT_sb = sbt("wgT_sb", [128, KC * NE], BF16)
        wg_ld = sbt("wg_ld", [NE, 256], BF16)

        UBN = NE * HC * T
        ub = sbt("ub", [128, UBN], BF16)
        qb_o, kb_o, vb_o, xr_o, q2_o = (0, KC * T, 2 * KC * T, 3 * KC * T,
                                        4 * KC * T)
        k2_o = xr_o
        y_o = q2_o

        wsA = sbt("wsA", [128, KC * T], BF16)
        wsB = sbt("wsB", [128, KC * T], BF16)
        fm = sbt("fm", [128, KC * T], BF16)
        u2 = sbt("u2", [128, MS * D], F32)
        scrA = sbt("scrA", [128, D], F32)
        scrB = sbt("scrB", [128, D], F32)
        scrD = sbt("scrD", [128, 4096], BF16)
        ttot = sbt("ttot", [128, 2 * KC], F32)
        bw_all = sbt("bw_all", [128, NE * T], BF16)
        bq_sb = sbt("bq_sb", [128, T], BF16)
        bk_sb = fm[:, 0:T]
        tmpA = sbt("tmpA", [128, T], F32)
        tmpB = sbt("tmpB", [128, T], F32)
        colf = sbt("colf", [128, 16], F32)
        wtot = sbt("wtot", [128, 2 * KC], F32)
        carry_w = sbt("carry_w", [128, KC], F32)
        carry_wv = sbt("carry_wv", [128, KC], F32)
        strow = sbt("strow", [128, 2 * MS], F32)
        cbias = sbt("cbias", [128, 4], F32)

        p0 = pstens("p0", [128, 512], F32)
        p1 = pstens("p1", [128, 512], F32)
        p2 = pstens("p2", [128, 512], F32)
        p3 = pstens("p3", [128, 512], F32)
        p4 = pstens("p4", [128, 512], F32)
        p5 = pstens("p5", [128, 512], F32)
        pbf = pstens("pbf", [128, 512], BF16)
        pstat = pstens("pstat", [1, 512], F32)
        pms = [p0, p1, p2, p3]

        s_ld = semt("s_ld")
        s_g = semt("s_g")
        s_sc = semt("s_sc")
        s_pe = semt("s_pe")
        s_dve = semt("s_dve")
        s_act = semt("s_act")
        s_cc = semt("s_cc")
        SEM = {"ld": s_ld, "g": s_g, "sc": s_sc, "pe": s_pe, "dve": s_dve,
               "act": s_act, "cc": s_cc}
        s_vv = semt("s_vv")
        s_aa = semt("s_aa")

        _SER_OPS = {"tensor_tensor", "tensor_copy", "tensor_scalar",
                    "tensor_scalar_mul", "tensor_scalar_add", "tensor_mul",
                    "tensor_add", "tensor_sub", "reduce_sum", "reduce_max",
                    "reciprocal", "tensor_tensor_scan", "memset",
                    "tensor_single_scalar", "scalar_tensor_tensor",
                    "activation", "copy"}

        class _Ser:

            def __init__(self, eng, ms_sem, ms_key, ser_sem, ser_key):
                self._e = eng
                self._ms = (ms_sem, ms_key)
                self._ser = (ser_sem, ser_key)
                self._prev = None

            def __getattr__(self, name):
                f = getattr(self._e, name)
                if name not in _SER_OPS:
                    return f

                def call(*a, ms=False, **kw):
                    if self._prev is not None:
                        self._e.wait_ge(*self._prev)
                    ins = f(*a, **kw)
                    sem, key = self._ms if ms else self._ser
                    ins.then_inc(sem, 1)
                    inc(key)
                    self._prev = (sem, val(key))
                    return ins
                return call

        sw_u = [semt("sw_u0"), semt("sw_u1")]
        sw_g = [semt("sw_g0"), semt("sw_g1")]
        op_s = [semt("op_s0"), semt("op_s1")]
        m1s = [semt("m1s0"), semt("m1s1")]
        m2s = [semt("m2s0"), semt("m2s1")]
        w3s = [semt("w3s0"), semt("w3s1")]
        wgl = [semt("wgl0"), semt("wgl1")]
        s_wc = semt("s_wc")
        qk = [semt("qk0"), semt("qk1")]
        stg_t = [semt("stg_t0"), semt("stg_t1")]
        stg_s = [semt("stg_s0"), semt("stg_s1")]
        xl_s = semt("xl_s")
        xsp_s = semt("xsp_s")
        cast_sems = {w: semt(f"cast_{w}") for w in
                     ("sw", "wo", "w1", "w2", "w3")}
        cast_q = semt("cast_q")

        def sdma(eng, sem, key, dst, src):
            eng.dma_start(dst, src).then_inc(sem, 16)
            return inc(key, 16)

        def wrapv(v):
            return _Ser(v, s_dve, "dve", s_vv, "vv")

        def wrapa(a):
            return _Ser(a, s_act, "act", s_aa, "aa")

        def dma(eng, q, dst, src, **kw):
            eng.dma_start(dst, src, **kw).then_inc(SEM[q], 16)
            inc(q, 16)

        def ub_c(off, j):
            return ub[:, off + j * T: off + (j + 1) * T]

        def fm_c(kc):
            return fm[:, kc * T:(kc + 1) * T]

        def sp_view(sp, ti):
            return sp.ap()[:, :, ti * T:(ti + 1) * T]

        def w12_slab(sy, sem, key, ws, st, wt, e, h0, hw):
            base = (st % 2) * (KC * 256)
            view = wt.ap().rearrange("(r e ih p) h -> e ih p r h",
                                     r=NCORES, e=NE, p=128)[e]
            dst = ws[:, base:base + KC * hw].rearrange(
                "p (kc h) -> p kc h", h=hw)
            for ih in range(2):
                sdma(sy, sem, key,
                     dst[:, ih::2, :],
                     view[ih][:, :, h0:h0 + hw])

        def w3_slab(sy, st, dr, e):
            buf = [wsA, wsB][st % 2]
            view = w3T.ap().rearrange("(r e ih p) d -> e ih p r d",
                                      r=NCORES, e=NE, p=128)[e]
            dst = buf[:, 0:HC * 512].rearrange("p (kc d) -> p kc d", d=512)
            sdma(sy, w3s[st % 2], f"w3{st % 2}",
                 dst[:, 0::2, :][:, 0:6, :],
                 view[0][:, 0:6, dr * 512:(dr + 1) * 512])
            sdma(sy, w3s[st % 2], f"w3{st % 2}",
                 dst[:, 1::2, :],
                 view[1][:, 0:5, dr * 512:(dr + 1) * 512])

        def ub_3d(off):
            return ub[:, off: off + KC * T].rearrange("p (kc t) -> p kc t", t=T)

        CH = 1 << 21

        def cast_chunks(csrc, cdst):
            sh = csrc.shape
            nm = int(np.prod(sh))
            spec = ("a b -> (a b)" if len(sh) == 2 else "a b c -> (a b c)")
            fsrc = csrc.ap().rearrange(spec)
            fdst = cdst.ap().rearrange(spec)
            return [(fsrc[o:min(o + CH, nm)], fdst[o:min(o + CH, nm)])
                    for o in range(0, nm, CH)]

        sh_pairs = []
        for srcB, dstT, C, R in [(qkvBs.ap(), qkvTs.ap(), SHK, DQKV),
                                 (swBs.ap(), swTs.ap(), SHK, DSW),
                                 (woBs.ap(), woTs.ap(), SHKO, D)]:
            for kc in range(C // 128):
                r0 = 0
                while r0 < R:
                    rs = min(4096, R - r0)
                    sh_pairs.append((srcB, dstT, kc, r0, rs))
                    r0 += rs
        for e in range(NE):
            for srcB, dstT, R in [
                    (w1Bs[e], w1Ts.ap()[e * SHK:(e + 1) * SHK, :], DH),
                    (w2Bs[e], w2Ts.ap()[e * SHK:(e + 1) * SHK, :], DH),
                    (w3Bs[e], w3Ts.ap()[e * SHK:(e + 1) * SHK, :], D)]:
                for kc in range(SHK // 128):
                    sh_pairs.append((srcB, dstT, kc, 0, R))

        stg = [(wsA, 0), (wsA, 4096), (wsB, 0), (wsB, 4096)]
        sl_t = [semt(f"sl_t{i}") for i in range(4)]
        sl_s = [semt(f"sl_s{i}") for i in range(4)]
        cast_all = semt("cast_all")
        s_ccq = semt("s_ccq")
        n_casts = [0]

        with nc.Block("prep") as blk:
            def _g(g):
                for csrc, cdst in [(sh_qkv, qkvBs), (sh_sw, swBs),
                                   (sh_wo, woBs), (w_g, wgB),
                                   (sh_w1, w1Bs), (sh_w2, w2Bs),
                                   (sh_w3, w3Bs)]:
                    for fsrc, fdst in cast_chunks(csrc, cdst):
                        g.dma_start(fdst, fsrc).then_inc(cast_all, 16)
                        n_casts[0] += 1
            blk.gpsimd(_g)

            def _sc(sc):
                sc.wait_ge(cast_all, 16 * n_casts[0])
                for i, (srcB, dstT, kc, r0, rs) in enumerate(sh_pairs):
                    sl = i % 4
                    buf, off = stg[sl]
                    if i >= 4:
                        sc.wait_ge(sl_s[sl], 16 * (i // 4))
                    sc.dma_start(buf[:, off:off + rs],
                                 srcB[r0:r0 + rs, kc * 128:(kc + 1) * 128],
                                 transpose=True).then_inc(sl_t[sl], 16)
                for sl in range(4):
                    nuse = (len(sh_pairs) - sl + 3) // 4
                    sc.wait_ge(sl_t[sl], 16 * nuse)
            blk.scalar(_sc)

            def _sy(sy):
                dma(sy, "ld", c_ident[:, :], ident_bf[:, :])
                dma(sy, "ld", c_onescol[:, :], ones_col[:, :])
                dma(sy, "ld", c_onesrow[:, :], ones_row[:, :])
                dma(sy, "ld", c_mask[:, :], mask_in[:, :])
                for i, (srcB, dstT, kc, r0, rs) in enumerate(sh_pairs):
                    sl = i % 4
                    buf, off = stg[sl]
                    sy.wait_ge(sl_t[sl], 16 * (i // 4 + 1))
                    sy.dma_start(dstT[kc * 128:(kc + 1) * 128, r0:r0 + rs],
                                 buf[:, off:off + rs]).then_inc(sl_s[sl], 16)
                for sl in range(4):
                    nuse = (len(sh_pairs) - sl + 3) // 4
                    sy.wait_ge(sl_s[sl], 16 * nuse)
                sy.wait_ge(s_ld, val("ld"))
            blk.sync(_sy)

            def _v(v):
                v = wrapv(v)
                v.memset(c_zero[:, :], 0.0)
                v.memset(wtot[:, :], 0.0)
                v.memset(cbias[:, 0:1], E1)
                v.memset(cbias[:, 1:2], E1 * E2)
                v.memset(cbias[:, 2:3], E2)
            blk.vector(_v)

        with nc.Block("gather") as blk:
            def _g(g):
                g.collective_compute(
                    "AllGather", ALU.bypass,
                    replica_groups=[list(range(NCORES))],
                    ins=[qkvTs.ap().opt()],
                    outs=[qkvT.ap().opt()]).then_inc(s_ccq, 1)
                for tin, tout in [(swTs, swT), (woTs, woT),
                                  (w1Ts, w1T), (w2Ts, w2T), (w3Ts, w3T)]:
                    g.collective_compute(
                        "AllGather", ALU.bypass,
                        replica_groups=[list(range(NCORES))],
                        ins=[tin.ap().opt()],
                        outs=[tout.ap().opt()]).then_inc(s_cc, 1)
                    inc("cc")
                g.wait_ge(s_ccq, 1)
            blk.gpsimd(_g)

        with nc.Block("wg") as blk:
            pe_b, dve_b = val("pe"), val("dve")
            wgl_base = [val("wgl0"), val("wgl1")]

            def _g(g, pe_b=pe_b, dve_b=dve_b):
                for kc in range(KC):
                    if kc >= 2:
                        g.wait_ge(s_dve, dve_b + kc - 1)
                    sdma(g, wgl[kc % 2], f"wgl{kc % 2}",
                         wg_ld[:, (kc % 2) * 128:(kc % 2) * 128 + 128],
                         wgB.ap()[:, kc * 128:(kc + 1) * 128])
            blk.gpsimd(_g)

            def _t(t, pe_b=pe_b, dve_b=dve_b, wgl_base=wgl_base):
                for kc in range(KC):
                    t.wait_ge(wgl[kc % 2],
                              wgl_base[kc % 2] + 16 * (kc // 2 + 1))
                    if kc >= 1:
                        t.wait_ge(s_dve, dve_b + kc)
                    t.transpose(pbf[:, 0:NE],
                                wg_ld[:, (kc % 2) * 128:(kc % 2) * 128 + 128],
                                c_ident[0:NE, 0:NE]).then_inc(s_pe, 1)
                    inc("pe")
            blk.tensor(_t)

            def _v(v, pe_b=pe_b):
                v = wrapv(v)
                for kc in range(KC):
                    v.wait_ge(s_pe, pe_b + kc + 1)
                    v.tensor_copy(wgT_sb[:, kc * NE:(kc + 1) * NE],
                                  pbf[:, 0:NE], ms=True)
            blk.vector(_v)

        NSLAB_QKV = DQKV // T

        def qkv_slab_dma(sy, s):
            dma(sy, "ld", wsA[:, :] if s % 2 == 0 else wsB[:, :],
                qkvT.ap().rearrange("(kc p) o -> p kc o", p=128)[:, :, s * T:(s + 1) * T])

        for ti in range(NT):
            with nc.Block(f"A_L0_{ti}") as blk:
                def _sy(sy, ti=ti):
                    dma(sy, "ld", u2[:, :].rearrange("p (m d) -> p m d", m=MS),
                        x_in.ap().rearrange("(tt m p) d -> tt p m d",
                                            m=MS, p=128)[ti])
                    sy.wait_ge(s_ld, val("ld"))
                blk.sync(_sy)

            with nc.Block(f"A_C0_{ti}") as blk:
                dve_b, act_b = val("dve"), val("act")

                def _v(v, ti=ti, dve_b=dve_b, act_b=act_b):
                    v = wrapv(v)
                    for m in range(MS):
                        xm = u2[:, m * D:(m + 1) * D]
                        v.reduce_sum(colf[:, 0:1], xm, axis=AX.X)
                        v.wait_ge(s_act, act_b + 2 * m + 1)
                        v.reduce_sum(colf[:, 1:2], scrA[:, :], axis=AX.X)
                        v.tensor_scalar_mul(colf[:, 2:3], colf[:, 0:1], 1.0 / D)
                        v.tensor_scalar_mul(colf[:, 3:4], colf[:, 1:2], 1.0 / D)
                        v.tensor_mul(colf[:, 4:5], colf[:, 2:3], colf[:, 2:3])
                        v.tensor_tensor(colf[:, 5:6], colf[:, 3:4], colf[:, 4:5],
                                        ALU.subtract, ms=True)
                        v.wait_ge(s_act, act_b + 2 * m + 2)
                        v.reciprocal(colf[:, 6:7], colf[:, 6:7])
                        v.reciprocal(colf[:, 7:8], colf[:, 7:8])
                        v.tensor_copy(strow[:, 2 * m:2 * m + 1], colf[:, 6:7])
                        v.tensor_mul(strow[:, 2 * m + 1:2 * m + 2],
                                     colf[:, 2:3], colf[:, 6:7])
                        v.tensor_mul(colf[:, 4:5], colf[:, 2:3], colf[:, 7:8])
                        v.tensor_scalar(ub[:, xr_o + m * D: xr_o + (m + 1) * D],
                                        xm, colf[:, 7:8], colf[:, 4:5],
                                        ALU.mult, ALU.subtract, ms=True)
                blk.vector(_v)

                def _a(a, dve_b=dve_b):
                    a = wrapa(a)
                    for m in range(MS):
                        if m > 0:
                            a.wait_ge(s_dve, dve_b + 2 * m)
                        a.activation(scrA[:, :], u2[:, m * D:(m + 1) * D],
                                     AF.Square, ms=True)
                        a.wait_ge(s_dve, dve_b + 2 * m + 1)
                        a.activation(colf[:, 6:7], colf[:, 5:6], AF.Sqrt,
                                     bias=cbias[:, 0:1])
                        a.activation(colf[:, 7:8], colf[:, 5:6], AF.Sqrt,
                                     scale=1.0 + E2,
                                     bias=cbias[:, 1:2], ms=True)
                blk.scalar(_a)

                dve_end = dve_b + 2 * MS

                def _g(g, ti=ti, dve_end=dve_end):
                    g.wait_ge(s_dve, dve_end)
                    dma(g, "g", st_sp[ti], strow[:, :])
                    g.wait_ge(s_g, val("g"))
                blk.gpsimd(_g)

            with nc.Block(f"A_C1_{ti}") as blk:
                pe_b, dve_b = val("pe"), val("dve")
                NG = MS * (KC // 4)

                def _t(t, pe_b=pe_b, dve_b=dve_b):
                    gidx = 0
                    for m in range(MS):
                        for kg in range(KC // 4):
                            if gidx > 0:
                                t.wait_ge(s_dve, dve_b + gidx)
                            for i in range(4):
                                kc = kg * 4 + i
                                t.transpose(pbf[:, i * 128:(i + 1) * 128],
                                            ub[:, xr_o + m * D + kc * 128:
                                               xr_o + m * D + (kc + 1) * 128],
                                            c_ident[:, :]).then_inc(s_pe, 1)
                                inc("pe")
                            gidx += 1
                blk.tensor(_t)

                def _v(v, pe_b=pe_b):
                    v = wrapv(v)
                    gidx = 0
                    for m in range(MS):
                        for kg in range(KC // 4):
                            v.wait_ge(s_pe, pe_b + 4 * (gidx + 1))
                            v.tensor_copy(
                                fm[:, :].rearrange("p (kc t) -> p kc t", t=T)
                                [:, kg * 4:(kg + 1) * 4, m * 128:(m + 1) * 128],
                                pbf[:, :].rearrange("p (a b) -> p a b", b=128)
                            , ms=True)
                            gidx += 1
                blk.vector(_v)

                def _sy(sy):
                    for s in range(2):
                        sdma(sy, qk[s % 2], f"qk{s % 2}",
                             [wsA, wsB][s % 2][:, :],
                             qkvT.ap().rearrange("(kc p) o -> p kc o", p=128)
                             [:, :, s * T:(s + 1) * T])
                blk.sync(_sy)

            with nc.Block(f"A_C2_{ti}") as blk:
                pe_b, dve_b, act_b = val("pe"), val("dve"), val("act")
                qk_b = [val("qk0") - 16, val("qk1") - 16]

                def _t(t, pe_b=pe_b, dve_b=dve_b, qk_b=qk_b):
                    for s in range(NSLAB_QKV):
                        t.wait_ge(qk[s % 2], qk_b[s % 2] + 16 * (s // 2 + 1))
                        ws = [wsA, wsB][s % 2]
                        for jj in range(4):
                            j = s * 4 + jj
                            if j >= 2:
                                t.wait_ge(s_dve, dve_b + j - 1)
                            pb = [p0, p1][j % 2]
                            for kc in range(KC):
                                ins = t.matmul(pb[:, 0:T],
                                               ws[:, kc * T + jj * 128:
                                                  kc * T + (jj + 1) * 128],
                                               fm_c(kc),
                                               start=(kc == 0),
                                               stop=(kc == KC - 1))
                            ins.then_inc(s_pe, 1)
                            inc("pe")
                blk.tensor(_t)

                def _v(v, pe_b=pe_b):
                    v = wrapv(v)
                    for j in range(4 * NSLAB_QKV):
                        v.wait_ge(s_pe, pe_b + j + 1)
                        off = qb_o if j < KC else (kb_o if j < 2 * KC else vb_o)
                        v.tensor_copy(ub_c(off, j % KC),
                                      [p0, p1][j % 2][:, 0:T], ms=True)
                blk.vector(_v)

                def _a(a, dve_b=dve_b):
                    a = wrapa(a)
                    for j in range(2 * KC):
                        a.wait_ge(s_dve, dve_b + j + 1)
                        src_off = qb_o if j < KC else kb_o
                        dst_off = q2_o if j < KC else k2_o
                        a.activation(ub_c(dst_off, j % KC),
                                     ub_c(src_off, j % KC),
                                     AF.Square, ms=True)
                blk.scalar(_a)

                def _sy(sy, pe_b=pe_b):
                    for s in range(2, NSLAB_QKV):
                        sy.wait_ge(s_pe, pe_b + 4 * (s - 1))
                        sdma(sy, qk[s % 2], f"qk{s % 2}",
                             [wsA, wsB][s % 2][:, :],
                             qkvT.ap().rearrange("(kc p) o -> p kc o", p=128)
                             [:, :, s * T:(s + 1) * T])
                    sy.wait_ge(s_ld, val("ld"))
                blk.sync(_sy)


            with nc.Block(f"A_C3_{ti}") as blk:
                pe_b, dve_b, act_b = val("pe"), val("dve"), val("act")

                def _t(t, pe_b=pe_b, dve_b=dve_b):
                    for kc in range(KC):
                        ins = t.matmul(pstat[0:1, 0:T], c_onescol[:, 0:1],
                                       ub_c(q2_o, kc), start=(kc == 0),
                                       stop=(kc == KC - 1))
                    ins.then_inc(s_pe, 1)
                    inc("pe")
                    t.wait_ge(s_dve, dve_b + 1)
                    for kc in range(KC):
                        ins = t.matmul(pstat[0:1, 0:T], c_onescol[:, 0:1],
                                       ub_c(k2_o, kc), start=(kc == 0),
                                       stop=(kc == KC - 1))
                    ins.then_inc(s_pe, 1)
                    inc("pe")
                    t.wait_ge(s_dve, dve_b + 3)
                    t.matmul(p2[:, 0:T], c_onesrow[0:1, :], bw_all[0:1, 0:T],
                             start=True, stop=True).then_inc(s_pe, 1)
                    inc("pe")
                    t.matmul(p3[:, 0:T], c_onesrow[0:1, :], bw_all[0:1, T:2 * T],
                             start=True, stop=True).then_inc(s_pe, 1)
                    inc("pe")
                blk.tensor(_t)

                def _v(v, pe_b=pe_b, act_b=act_b):
                    v = wrapv(v)
                    v.wait_ge(s_pe, pe_b + 1)
                    v.tensor_copy(u2[0:1, 0:T], pstat[0:1, 0:T], ms=True)
                    v.wait_ge(s_pe, pe_b + 2)
                    v.tensor_copy(u2[0:1, T:2 * T], pstat[0:1, 0:T], ms=True)
                    v.wait_ge(s_act, act_b + 1)
                    v.reciprocal(u2[0:1, 0:T], u2[0:1, 0:T])
                    v.reciprocal(u2[0:1, T:2 * T], u2[0:1, T:2 * T])
                    v.tensor_copy(bw_all[0:1, 0:2 * T], u2[0:1, 0:2 * T], ms=True)
                    v.wait_ge(s_pe, pe_b + 3)
                    v.tensor_copy(bq_sb[:, :], p2[:, 0:T])
                    v.wait_ge(s_pe, pe_b + 4)
                    v.tensor_copy(bk_sb[:, :], p3[:, 0:T])
                blk.vector(_v)

                def _a(a, dve_b=dve_b):
                    a = wrapa(a)
                    a.wait_ge(s_dve, dve_b + 2)
                    a.activation(u2[0:1, 0:T], u2[0:1, 0:T], AF.Sqrt,
                                 scale=1.0 / D, bias=cbias[0:1, 2:3])
                    a.activation(u2[0:1, T:2 * T], u2[0:1, T:2 * T],
                                 AF.Sqrt, scale=1.0 / D,
                                 bias=cbias[0:1, 2:3], ms=True)
                blk.scalar(_a)

            with nc.Block(f"A_C4_{ti}") as blk:
                dve_b, act_b = val("dve"), val("act")
                tp = [tmpA, tmpB]

                def _v(v, dve_b=dve_b, act_b=act_b):
                    v = wrapv(v)
                    for j in range(KC):
                        if j >= 2:
                            v.wait_ge(s_act, act_b + j - 1)
                        v.tensor_tensor(tp[j % 2][:, :], ub_c(qb_o, j),
                                        bq_sb[:, :], ALU.mult, ms=True)
                    for j in range(KC):
                        v.wait_ge(s_act, act_b + KC + max(0, j - 1))
                        v.tensor_tensor(tp[j % 2][:, :], ub_c(kb_o, j),
                                        bk_sb[:, :], ALU.mult, ms=True)
                    for j in range(KC):
                        v.wait_ge(s_act, act_b + KC + j + 1)
                        v.tensor_tensor(ub_c(vb_o, j), ub_c(kb_o, j),
                                        ub_c(vb_o, j), ALU.mult)
                        v.tensor_tensor_scan(ub_c(kb_o, j), ub_c(kb_o, j),
                                             c_zero[:, :], 0.0,
                                             ALU.add, ALU.add)
                        v.tensor_copy(ttot[:, j:j + 1],
                                      ub_c(kb_o, j)[:, T - 1:T])
                        v.tensor_tensor(wtot[:, j:j + 1], wtot[:, j:j + 1],
                                        ttot[:, j:j + 1], ALU.add)
                        v.tensor_tensor_scan(ub_c(vb_o, j), ub_c(vb_o, j),
                                             c_zero[:, :], 0.0,
                                             ALU.add, ALU.add)
                        v.tensor_copy(ttot[:, KC + j:KC + j + 1],
                                      ub_c(vb_o, j)[:, T - 1:T])
                        v.tensor_tensor(wtot[:, KC + j:KC + j + 1],
                                        wtot[:, KC + j:KC + j + 1],
                                        ttot[:, KC + j:KC + j + 1],
                                        ALU.add, ms=True)
                blk.vector(_v)

                def _a(a, dve_b=dve_b):
                    a = wrapa(a)
                    for j in range(KC):
                        a.wait_ge(s_dve, dve_b + j + 1)
                        a.activation(ub_c(qb_o, j), tp[j % 2][:, :],
                                     AF.Sigmoid, ms=True)
                    for j in range(KC):
                        a.wait_ge(s_dve, dve_b + KC + j + 1)
                        a.activation(ub_c(kb_o, j), tp[j % 2][:, :],
                                     AF.Exp, ms=True)
                blk.scalar(_a)

                dve_end = dve_b + 3 * KC
                act_end = act_b + 2 * KC

                def _g(g, ti=ti, dve_end=dve_end, act_end=act_end):
                    g.wait_ge(s_dve, dve_end)
                    g.wait_ge(s_act, act_end)
                    dma(g, "g", sp_view(sq_sp, ti), ub_3d(qb_o))
                    dma(g, "g", sp_view(w_sp, ti), ub_3d(kb_o))
                    dma(g, "g", sp_view(wv_sp, ti), ub_3d(vb_o))
                    dma(g, "g", tt_sp[ti], ttot[:, :])
                    g.wait_ge(s_g, val("g"))
                blk.gpsimd(_g)

        gbuf = scrA
        with nc.Block("cc1") as blk:
            def _g(g):
                g.wait_ge(s_cc, 5)
                dma(g, "g", cc_in[:, :], wtot[:, :])
                g.wait_ge(s_g, val("g"))
                g.collective_compute(
                    "AllGather", ALU.bypass,
                    replica_groups=[list(range(NCORES))],
                    ins=[cc_in.ap().opt()],
                    outs=[cc_out.ap().opt()]).then_inc(s_cc, 1)
                inc("cc")
                g.wait_ge(s_cc, val("cc"))
                dma(g, "g", gbuf[:, 0:NCORES * 2 * KC].rearrange("p (r c) -> p r c", r=NCORES),
                    cc_out.ap().rearrange("(r p) c -> p r c", p=128))
                g.wait_ge(s_g, val("g"))
            blk.gpsimd(_g)

        with nc.Block("cc2") as blk:
            def _v(v):
                v = wrapv(v)
                v.memset(carry_w[:, :], 0.0)
                v.memset(carry_wv[:, :], 0.0)
                for r in range(NCORES):
                    v.tensor_scalar_mul(tmpA[:, 0:2 * KC],
                                        gbuf[:, r * 2 * KC:(r + 1) * 2 * KC],
                                        c_mask[:, r:r + 1])
                    v.tensor_tensor(carry_w[:, :], carry_w[:, :],
                                    tmpA[:, 0:KC], ALU.add)
                    v.tensor_tensor(carry_wv[:, :], carry_wv[:, :],
                                    tmpA[:, KC:2 * KC], ALU.add)
                v.tensor_scalar_add(carry_w[:, :], carry_w[:, :], 1e-6)
            blk.vector(_v)

        NSW = DFF // T
        NDR = D // 512

        for ti in range(NT):
            slotb = {k: val(k) for k in ("swu0", "swu1", "swg0", "swg1",
                                         "ops0", "ops1", "mh10", "mh11",
                                         "mh20", "mh21", "w30", "w31")}
            with nc.Block(f"B_L0_{ti}") as blk:
                def _sy(sy, ti=ti):
                    dma(sy, "ld", strow[:, :], st_sp[ti])
                    dma(sy, "ld", ttot[:, :], tt_sp[ti])
                    dma(sy, "ld", ub_3d(qb_o), sp_view(sq_sp, ti))
                    dma(sy, "ld", ub_3d(kb_o), sp_view(w_sp, ti))
                    dma(sy, "ld", ub_3d(vb_o), sp_view(wv_sp, ti))
                    dma(sy, "ld", u2[:, :].rearrange("p (m d) -> p m d", m=MS),
                        x_in.ap().rearrange("(tt m p) d -> tt p m d",
                                            m=MS, p=128)[ti])
                    OWp = 256
                    for st in range(2):
                        o0 = st * OWp
                        sdma(sy, sw_u[st % 2], f"swu{st % 2}",
                             wsA[:, (st % 2) * KC * OWp:
                                 (st % 2 + 1) * KC * OWp].rearrange(
                                 "p (kc o) -> p kc o", o=OWp),
                             swT.ap().rearrange("(kc p) o -> p kc o", p=128)
                             [:, :, o0:o0 + OWp])
                        sdma(sy, sw_g[st % 2], f"swg{st % 2}",
                             wsB[:, (st % 2) * KC * OWp:
                                 (st % 2 + 1) * KC * OWp].rearrange(
                                 "p (kc o) -> p kc o", o=OWp),
                             swT.ap().rearrange("(kc p) o -> p kc o", p=128)
                             [:, :, DFF + o0:DFF + o0 + OWp])
                    sy.wait_ge(s_ld, val("ld"))
                blk.sync(_sy)

            OW = 256
            NJ = DFF // 128
            NST = DFF // OW
            with nc.Block(f"B_sw_{ti}") as blk:
                pe_b, dve_b, act_b = val("pe"), val("dve"), val("act")
                swu_b = [slotb["swu0"], slotb["swu1"]]
                swg_b = [slotb["swg0"], slotb["swg1"]]
                tp = [tmpA, tmpB]


                def _t(t, pe_b=pe_b, dve_b=dve_b, act_b=act_b,
                       swu_b=swu_b, swg_b=swg_b):
                    for j in range(NJ):
                        st = j // 2
                        jj = j % 2
                        if jj == 0:
                            t.wait_ge(sw_u[st % 2],
                                      swu_b[st % 2] + 16 * (st // 2 + 1))
                            t.wait_ge(sw_g[st % 2],
                                      swg_b[st % 2] + 16 * (st // 2 + 1))
                        pu = [p0, p1][j % 2]
                        pg = [p2, p3][j % 2]
                        if j >= 2:
                            t.wait_ge(s_dve, dve_b + 16 + j - 1)
                            t.wait_ge(s_act, act_b + 16 + j - 1)
                        if j == 4:
                            t.wait_ge(s_dve, dve_b + KC)
                        ubase = (st % 2) * KC * OW
                        for kc in range(KC):
                            if j < 4:
                                t.wait_ge(s_dve, dve_b + kc + 1)
                            ins = t.matmul(pu[:, 0:T],
                                           wsA[:, ubase + kc * OW + jj * 128:
                                               ubase + kc * OW + (jj + 1) * 128],
                                           ub_c(y_o, kc),
                                           start=(kc == 0), stop=(kc == KC - 1))
                        ins.then_inc(s_pe, 1)
                        inc("pe")
                        for kc in range(KC):
                            if j < 4:
                                t.wait_ge(s_dve, dve_b + kc + 1)
                            ins = t.matmul(pg[:, 0:T],
                                           wsB[:, ubase + kc * OW + jj * 128:
                                               ubase + kc * OW + (jj + 1) * 128],
                                           ub_c(y_o, kc),
                                           start=(kc == 0), stop=(kc == KC - 1))
                        ins.then_inc(s_pe, 1)
                        inc("pe")
                blk.tensor(_t)

                def _a(a, pe_b=pe_b, dve_b=dve_b):
                    a = wrapa(a)
                    for kc in range(KC):
                        if kc >= 4:
                            a.wait_ge(s_dve, dve_b + kc - 3)
                        sl = scrA[:, (kc % 4) * 512:(kc % 4) * 512 + T]
                        a.activation(sl, ub_c(kb_o, kc), AF.Ln,
                                     bias=carry_w[:, kc:kc + 1])
                        a.activation(sl, sl, AF.Exp, scale=-1.0, ms=True)
                    for j in range(NJ):
                        a.wait_ge(s_pe, pe_b + 2 * j + 2)
                        a.activation(tp[j % 2][:, :], [p2, p3][j % 2][:, 0:T],
                                     AF.Silu, ms=True)
                blk.scalar(_a)

                def _v(v, pe_b=pe_b, act_b=act_b):
                    v = wrapv(v)
                    for kc in range(KC):
                        v.tensor_scalar(tmpB[:, :], ub_c(vb_o, kc),
                                        carry_wv[:, kc:kc + 1], None, ALU.add)
                        v.wait_ge(s_act, act_b + kc + 1)
                        v.tensor_mul(tmpB[:, :], tmpB[:, :],
                                     scrA[:, (kc % 4) * 512:(kc % 4) * 512 + T])
                        v.tensor_tensor(ub_c(y_o, kc), tmpB[:, :],
                                        ub_c(qb_o, kc), ALU.mult, ms=True)
                    for m in range(MS):
                        v.tensor_scalar(u2[:, m * D:(m + 1) * D],
                                        u2[:, m * D:(m + 1) * D],
                                        strow[:, 2 * m:2 * m + 1],
                                        strow[:, 2 * m + 1:2 * m + 2],
                                        ALU.mult, ALU.subtract)
                    v.tensor_tensor(carry_w[:, :], carry_w[:, :],
                                    ttot[:, 0:KC], ALU.add)
                    v.tensor_tensor(carry_wv[:, :], carry_wv[:, :],
                                    ttot[:, KC:2 * KC], ALU.add)
                    for j in range(NJ):
                        v.wait_ge(s_act, act_b + 16 + j + 1)
                        v.tensor_tensor(ub_c(0, j), tp[j % 2][:, :],
                                        [p0, p1][j % 2][:, 0:T],
                                        ALU.mult, ms=True)
                blk.vector(_v)

                def _sy(sy, pe_b=pe_b):
                    for st in range(2, NST):
                        sy.wait_ge(s_pe, pe_b + 4 * (st - 1))
                        o0 = st * OW
                        sdma(sy, sw_u[st % 2], f"swu{st % 2}",
                             wsA[:, (st % 2) * KC * OW:
                                 (st % 2 + 1) * KC * OW].rearrange(
                                 "p (kc o) -> p kc o", o=OW),
                             swT.ap().rearrange("(kc p) o -> p kc o", p=128)
                             [:, :, o0:o0 + OW])
                        sdma(sy, sw_g[st % 2], f"swg{st % 2}",
                             wsB[:, (st % 2) * KC * OW:
                                 (st % 2 + 1) * KC * OW].rearrange(
                                 "p (kc o) -> p kc o", o=OW),
                             swT.ap().rearrange("(kc p) o -> p kc o", p=128)
                             [:, :, DFF + o0:DFF + o0 + OW])
                    sy.wait_ge(s_pe, pe_b + 2 * NJ)
                    for st in range(2):
                        sdma(sy, op_s[st % 2], f"ops{st % 2}",
                             [wsA, wsB][st % 2][:, :].rearrange(
                                 "p (kc d) -> p kc d", d=512),
                             woT.ap().rearrange("(kc p) d -> p kc d", p=128)
                             [:, (st % 4) * 16:(st % 4 + 1) * 16, 0:512])
                    sy.wait_ge(s_ld, val("ld"))
                blk.sync(_sy)

            with nc.Block(f"B_op_{ti}") as blk:
                pe_b, dve_b = val("pe"), val("dve")
                ops_b = [slotb["ops0"], slotb["ops1"]]

                def _t(t, pe_b=pe_b, dve_b=dve_b, ops_b=ops_b):
                    for st in range(16):
                        dr, qi = st // 4, st % 4
                        t.wait_ge(op_s[st % 2],
                                  ops_b[st % 2] + 16 * (st // 2 + 1))
                        if qi == 0 and dr > 0:
                            t.wait_ge(s_dve, dve_b + dr)
                        buf = [wsA, wsB][st % 2]
                        for kcl in range(16):
                            kc = qi * 16 + kcl
                            for m in range(MS):
                                ins = t.matmul(
                                    pms[m][:, 0:512],
                                    ub[:, kc * T + m * 128:
                                       kc * T + (m + 1) * 128],
                                    buf[:, kcl * 512:(kcl + 1) * 512],
                                    start=(qi == 0 and kcl == 0),
                                    stop=(qi == 3 and kcl == 15))
                        ins.then_inc(s_pe, 1)
                        inc("pe")
                blk.tensor(_t)

                def _v(v, pe_b=pe_b):
                    v = wrapv(v)
                    for dr in range(NDR):
                        v.wait_ge(s_pe, pe_b + 4 * (dr + 1))
                        for m in range(MS):
                            v.tensor_tensor(
                                u2[:, m * D + dr * 512:m * D + (dr + 1) * 512],
                                u2[:, m * D + dr * 512:m * D + (dr + 1) * 512],
                                pms[m][:, 0:512], ALU.add, ms=(m == MS - 1))
                blk.vector(_v)

                def _sy(sy, pe_b=pe_b):
                    for st in range(2, 16):
                        dr, qi = st // 4, st % 4
                        sy.wait_ge(s_pe, pe_b + st - 1)
                        sdma(sy, op_s[st % 2], f"ops{st % 2}",
                             [wsA, wsB][st % 2][:, :].rearrange(
                                 "p (kc d) -> p kc d", d=512),
                             woT.ap().rearrange("(kc p) d -> p kc d", p=128)
                             [:, qi * 16:(qi + 1) * 16,
                              dr * 512:(dr + 1) * 512])
                    sy.wait_ge(s_ld, val("ld"))
                blk.sync(_sy)


            with nc.Block(f"B_x1_{ti}") as blk:
                dve_b, act_b = val("dve"), val("act")
                xl_b, xsp_b = val("xl"), val("xsp")

                def _sy(sy, ti=ti, dve_b=dve_b, xsp_b=xsp_b):
                    for m in range(MS):
                        if m >= 1:
                            sy.wait_ge(s_dve, dve_b + 4 * m)
                            sy.wait_ge(xsp_s, xsp_b + 16 * m)
                        sy.dma_start(
                            scrB[:, :],
                            x_in.ap()[ti * T + m * 128:
                                      ti * T + (m + 1) * 128, :]).then_inc(
                                          xl_s, 16)
                        inc("xl", 16)
                blk.sync(_sy)

                def _v(v, dve_b=dve_b, act_b=act_b, xl_b=xl_b):
                    v = wrapv(v)
                    for m in range(MS):
                        am = u2[:, m * D:(m + 1) * D]
                        v.reduce_sum(colf[:, 8:9], am, axis=AX.X)
                        v.wait_ge(s_act, act_b + 4 * m + 1)
                        v.tensor_scalar_mul(colf[:, 11:12], colf[:, 9:10], 1.0 / D)
                        v.tensor_scalar_mul(colf[:, 10:11], colf[:, 8:9], 1.0 / D)
                        v.tensor_mul(colf[:, 12:13], colf[:, 10:11], colf[:, 10:11])
                        v.tensor_tensor(colf[:, 13:14], colf[:, 11:12],
                                        colf[:, 12:13], ALU.subtract, ms=True)
                        v.wait_ge(s_act, act_b + 4 * m + 2)
                        v.reciprocal(colf[:, 12:13], colf[:, 12:13])
                        v.tensor_mul(colf[:, 15:16], colf[:, 10:11], colf[:, 12:13])
                        v.wait_ge(xl_s, xl_b + 16 * (m + 1))
                        v.scalar_tensor_tensor(scrB[:, :], am, colf[:, 12:13],
                                               scrB[:, :], ALU.mult, ALU.add)
                        v.tensor_scalar(scrB[:, :], scrB[:, :], colf[:, 15:16],
                                        None, ALU.subtract, ms=True)
                        v.reduce_sum(colf[:, 8:9], scrB[:, :], axis=AX.X)
                        v.wait_ge(s_act, act_b + 4 * m + 3)
                        v.tensor_scalar_mul(colf[:, 11:12], colf[:, 9:10], 1.0 / D)
                        v.tensor_scalar_mul(colf[:, 10:11], colf[:, 8:9], 1.0 / D)
                        v.tensor_mul(colf[:, 12:13], colf[:, 10:11], colf[:, 10:11])
                        v.tensor_tensor(colf[:, 13:14], colf[:, 11:12],
                                        colf[:, 12:13], ALU.subtract, ms=True)
                        v.wait_ge(s_act, act_b + 4 * m + 4)
                        v.reciprocal(colf[:, 12:13], colf[:, 12:13])
                        v.tensor_mul(colf[:, 15:16], colf[:, 10:11], colf[:, 12:13])
                        v.tensor_scalar(ub[:, m * D:(m + 1) * D], scrB[:, :],
                                        colf[:, 12:13], colf[:, 15:16],
                                        ALU.mult, ALU.subtract, ms=True)
                blk.vector(_v)

                def _a(a, dve_b=dve_b):
                    a = wrapa(a)
                    for m in range(MS):
                        if m >= 1:
                            a.wait_ge(s_dve, dve_b + 4 * m)
                        a.activation(scrA[:, :], u2[:, m * D:(m + 1) * D],
                                     AF.Square, accum_out=colf[:, 9:10],
                                     ms=True)
                        a.wait_ge(s_dve, dve_b + 4 * m + 1)
                        a.activation(colf[:, 12:13], colf[:, 13:14], AF.Sqrt,
                                     bias=cbias[:, 0:1], ms=True)
                        a.wait_ge(s_dve, dve_b + 4 * m + 2)
                        a.activation(scrA[:, :], scrB[:, :], AF.Square,
                                     accum_out=colf[:, 9:10], ms=True)
                        a.wait_ge(s_dve, dve_b + 4 * m + 3)
                        a.activation(colf[:, 12:13], colf[:, 13:14], AF.Sqrt,
                                     bias=cbias[:, 0:1], ms=True)
                blk.scalar(_a)

                def _g(g, ti=ti, dve_b=dve_b):
                    for m in range(MS):
                        g.wait_ge(s_dve, dve_b + 4 * m + 2)
                        g.dma_start(
                            x1_sp.ap()[ti * T + m * 128:
                                       ti * T + (m + 1) * 128, :],
                            scrB[:, :]).then_inc(xsp_s, 16)
                        inc("xsp", 16)
                    g.wait_ge(xsp_s, val("xsp"))
                blk.gpsimd(_g)


            with nc.Block(f"B_tr_{ti}") as blk:
                pe_b, dve_b = val("pe"), val("dve")

                def _t(t, pe_b=pe_b, dve_b=dve_b):
                    gidx = 0
                    for m in range(MS):
                        for kg in range(KC // 4):
                            if gidx > 0:
                                t.wait_ge(s_dve, dve_b + gidx)
                            for i in range(4):
                                kc = kg * 4 + i
                                t.transpose(pbf[:, i * 128:(i + 1) * 128],
                                            ub[:, m * D + kc * 128:
                                               m * D + (kc + 1) * 128],
                                            c_ident[:, :]).then_inc(s_pe, 1)
                                inc("pe")
                            gidx += 1
                blk.tensor(_t)

                def _v(v, pe_b=pe_b):
                    v = wrapv(v)
                    gidx = 0
                    for m in range(MS):
                        for kg in range(KC // 4):
                            v.wait_ge(s_pe, pe_b + 4 * (gidx + 1))
                            v.tensor_copy(
                                fm[:, :].rearrange("p (kc t) -> p kc t", t=T)
                                [:, kg * 4:(kg + 1) * 4, m * 128:(m + 1) * 128],
                                pbf[:, :].rearrange("p (a b) -> p a b", b=128)
                            , ms=True)
                            gidx += 1
                blk.vector(_v)

            with nc.Block(f"B_rt_{ti}") as blk:
                pe_b, dve_b, act_b = val("pe"), val("dve"), val("act")

                def _t(t, pe_b=pe_b, dve_b=dve_b):
                    for m in range(MS):
                        if m > 0:
                            t.wait_ge(s_dve, dve_b + 4 * m)
                        for kc in range(KC):
                            ins = t.matmul(p4[:, 0:NE],
                                           fm[:, kc * T + m * 128:
                                              kc * T + (m + 1) * 128],
                                           wgT_sb[:, kc * NE:(kc + 1) * NE],
                                           start=(kc == 0), stop=(kc == KC - 1))
                        ins.then_inc(s_pe, 1)
                        inc("pe")
                        t.wait_ge(s_dve, dve_b + 4 * m + 2)
                        for e in range(4):
                            ins = t.transpose(
                                pbf[0:1, e * 128:(e + 1) * 128],
                                bq_sb[:, e:e + 1], c_ident[:, :])
                        ins.then_inc(s_pe, 1)
                        inc("pe")
                        t.wait_ge(s_dve, dve_b + 4 * m + 3)
                        for e in range(4):
                            ins = t.transpose(
                                pbf[0:1, e * 128:(e + 1) * 128],
                                bq_sb[:, 4 + e:5 + e], c_ident[:, :])
                        ins.then_inc(s_pe, 1)
                        inc("pe")
                    for e in range(NE):
                        pb = [p2, p3][e % 2]
                        if e >= 2:
                            t.wait_ge(s_dve, dve_b + 4 * MS + e - 1)
                        t.matmul(pb[:, 0:T], c_onesrow[0:1, :],
                                 bw_all[0:1, e * T:(e + 1) * T],
                                 start=True, stop=True).then_inc(s_pe, 1)
                        inc("pe")
                blk.tensor(_t)

                def _v(v, pe_b=pe_b, dve_b=dve_b, act_b=act_b):
                    v = wrapv(v)
                    for m in range(MS):
                        v.wait_ge(s_pe, pe_b + 3 * m + 1)
                        v.tensor_copy(tmpA[:, 0:NE], p4[:, 0:NE])
                        v.reduce_max(colf[:, 8:9], tmpA[:, 0:NE], axis=AX.X)
                        v.tensor_single_scalar(tmpB[:, 0:NE], tmpA[:, 0:NE],
                                               colf[:, 8:9], ALU.is_ge)
                        v.scalar_tensor_tensor(tmpB[:, 8:16], tmpB[:, 0:NE],
                                               BIG_NEG, tmpA[:, 0:NE],
                                               ALU.mult, ALU.add)
                        v.reduce_max(colf[:, 9:10], tmpB[:, 8:16], axis=AX.X)
                        v.tensor_tensor(colf[:, 10:11], colf[:, 8:9],
                                        colf[:, 9:10],
                                        ALU.subtract, ms=True)
                        v.wait_ge(s_act, act_b + m + 1)
                        v.tensor_tensor(colf[:, 13:14], colf[:, 11:12],
                                        colf[:, 12:13], ALU.subtract)
                        v.tensor_single_scalar(tmpA[:, 8:16], tmpA[:, 0:NE],
                                               colf[:, 9:10], ALU.is_ge)
                        v.tensor_single_scalar(tmpA[:, 0:NE], tmpA[:, 8:16],
                                               colf[:, 12:13], ALU.mult)
                        v.scalar_tensor_tensor(tmpA[:, 0:NE], tmpB[:, 0:NE],
                                               colf[:, 13:14], tmpA[:, 0:NE],
                                               ALU.mult, ALU.add)
                        v.tensor_copy(bq_sb[:, 0:NE],
                                      tmpA[:, 0:NE], ms=True)
                        v.wait_ge(s_pe, pe_b + 3 * m + 2)
                        for e in range(4):
                            v.tensor_copy(
                                bw_all[0:1, e * T + m * 128:
                                       e * T + (m + 1) * 128],
                                pbf[0:1, e * 128:(e + 1) * 128], ms=(e == 3))
                        v.wait_ge(s_pe, pe_b + 3 * m + 3)
                        for e in range(4):
                            v.tensor_copy(
                                bw_all[0:1, (4 + e) * T + m * 128:
                                       (4 + e) * T + (m + 1) * 128],
                                pbf[0:1, e * 128:(e + 1) * 128], ms=(e == 3))
                    for e in range(NE):
                        v.wait_ge(s_pe, pe_b + 3 * MS + e + 1)
                        v.tensor_copy(bw_all[:, e * T:(e + 1) * T],
                                      [p2, p3][e % 2][:, 0:T], ms=True)
                blk.vector(_v)

                def _a(a, dve_b=dve_b):
                    a = wrapa(a)
                    for m in range(MS):
                        a.wait_ge(s_dve, dve_b + 4 * m + 1)
                        a.activation(colf[:, 11:12], colf[:, 10:11], AF.Sigmoid)
                        a.activation(colf[:, 12:13], colf[:, 10:11], AF.Sigmoid,
                                     scale=-1.0, ms=True)
                blk.scalar(_a)

                def _sy(sy):
                    for st in range(2):
                        e, h0, hw = (0, st * 256, 256)
                        w12_slab(sy, m1s[st % 2], f"mh1{st % 2}", wsA, st,
                                 w1T, e, h0, hw)
                        w12_slab(sy, m2s[st % 2], f"mh2{st % 2}", wsB, st,
                                 w2T, e, h0, hw)
                blk.sync(_sy)

            MH_STEPS = []
            for e in range(NE):
                for q, (h0, hw) in enumerate([(0, 256), (256, 256), (512, 256),
                                              (768, 256), (1024, 256),
                                              (1280, 128)]):
                    MH_STEPS.append((e, h0, hw))
            GCUM = [0]
            for (_, _, hw) in MH_STEPS:
                GCUM.append(GCUM[-1] + hw // 128)

            with nc.Block(f"B_mh_{ti}") as blk:
                pe_b, dve_b, act_b = val("pe"), val("dve"), val("act")
                tp = [tmpA, tmpB]

                def _t(t, pe_b=pe_b, dve_b=dve_b, slotb=slotb):
                    for st, (e, h0, hw) in enumerate(MH_STEPS):
                        t.wait_ge(m1s[st % 2],
                                  slotb[f"mh1{st % 2}"] + 32 * (st // 2 + 1))
                        t.wait_ge(m2s[st % 2],
                                  slotb[f"mh2{st % 2}"] + 32 * (st // 2 + 1))
                        base = (st % 2) * (KC * 256)
                        for hj in range(hw // 128):
                            g = GCUM[st] + hj
                            pa = [p0, p1][g % 2]
                            pb = [p2, p3][g % 2]
                            if g >= 2:
                                t.wait_ge(s_dve, dve_b + 2 * (g - 1))
                            for kc in range(KC):
                                ins = t.matmul(pa[:, 0:T],
                                               wsA[:, base + kc * hw + hj * 128:
                                                   base + kc * hw + (hj + 1) * 128],
                                               fm_c(kc), start=(kc == 0),
                                               stop=(kc == KC - 1))
                            ins.then_inc(s_pe, 1)
                            inc("pe")
                            for kc in range(KC):
                                ins = t.matmul(pb[:, 0:T],
                                               wsB[:, base + kc * hw + hj * 128:
                                                   base + kc * hw + (hj + 1) * 128],
                                               fm_c(kc), start=(kc == 0),
                                               stop=(kc == KC - 1))
                            ins.then_inc(s_pe, 1)
                            inc("pe")
                blk.tensor(_t)

                def _a(a, pe_b=pe_b, dve_b=dve_b):
                    a = wrapa(a)
                    for g in range(GCUM[-1]):
                        a.wait_ge(s_pe, pe_b + 2 * g + 1)
                        a.activation(tp[g % 2][:, :], [p0, p1][g % 2][:, 0:T],
                                     AF.Silu, ms=True)
                blk.scalar(_a)

                def _v(v, pe_b=pe_b, act_b=act_b):
                    v = wrapv(v)
                    for st, (e, h0, hw) in enumerate(MH_STEPS):
                        for hj in range(hw // 128):
                            g = GCUM[st] + hj
                            v.wait_ge(s_act, act_b + g + 1)
                            v.wait_ge(s_pe, pe_b + 2 * g + 2)
                            v.tensor_tensor(tp[g % 2][:, :], tp[g % 2][:, :],
                                            [p2, p3][g % 2][:, 0:T],
                                            ALU.mult, ms=True)
                            slot = e * HC + h0 // 128 + hj
                            v.tensor_tensor(ub_c(0, slot), tp[g % 2][:, :],
                                            bw_all[:, e * T:(e + 1) * T],
                                            ALU.mult, ms=True)
                blk.vector(_v)

                def _sy(sy, pe_b=pe_b, slotb=slotb):
                    for st in range(2, len(MH_STEPS)):
                        e, h0, hw = MH_STEPS[st]
                        sy.wait_ge(s_pe, pe_b + 2 * GCUM[st - 1])
                        w12_slab(sy, m1s[st % 2], f"mh1{st % 2}", wsA, st,
                                 w1T, e, h0, hw)
                        w12_slab(sy, m2s[st % 2], f"mh2{st % 2}", wsB, st,
                                 w2T, e, h0, hw)
                    sy.wait_ge(s_pe, pe_b + 2 * GCUM[-1])
                    for st in range(2):
                        w3_slab(sy, st, 0, st)
                    sy.wait_ge(s_ld, val("ld"))
                blk.sync(_sy)

            with nc.Block(f"B_w3_{ti}") as blk:
                pe_b, dve_b = val("pe"), val("dve")

                def _t(t, pe_b=pe_b, dve_b=dve_b, slotb=slotb):
                    for st in range(NDR * NE):
                        dr, e = st // NE, st % NE
                        t.wait_ge(w3s[st % 2],
                                  slotb[f"w3{st % 2}"] + 32 * (st // 2 + 1))
                        if e == 0 and dr > 0:
                            t.wait_ge(s_dve, dve_b + dr)
                        buf = [wsA, wsB][st % 2]
                        for kc in range(HC):
                            for m in range(MS):
                                ins = t.matmul(
                                    pms[m][:, 0:512],
                                    ub[:, (e * HC + kc) * T + m * 128:
                                       (e * HC + kc) * T + (m + 1) * 128],
                                    buf[:, kc * 512:(kc + 1) * 512],
                                    start=(e == 0 and kc == 0),
                                    stop=(e == NE - 1 and kc == HC - 1))
                        ins.then_inc(s_pe, 1)
                        inc("pe")
                blk.tensor(_t)

                def _v(v, pe_b=pe_b):
                    v = wrapv(v)
                    for dr in range(NDR):
                        v.wait_ge(s_pe, pe_b + NE * (dr + 1))
                        for m in range(MS):
                            v.tensor_copy(
                                u2[:, m * D + dr * 512:m * D + (dr + 1) * 512],
                                pms[m][:, 0:512], ms=(m == MS - 1))
                blk.vector(_v)

                def _sy(sy, pe_b=pe_b, slotb=slotb):
                    for st in range(2, NDR * NE):
                        dr, e = st // NE, st % NE
                        sy.wait_ge(s_pe, pe_b + st - 1)
                        w3_slab(sy, st, dr, e)
                    sy.wait_ge(s_ld, val("ld"))
                blk.sync(_sy)


            with nc.Block(f"B_x2_{ti}") as blk:
                dve_b, act_b = val("dve"), val("act")
                xl_b, xsp_b = val("xl"), val("xsp")

                def _sy(sy, ti=ti, dve_b=dve_b, xsp_b=xsp_b):
                    for m in range(MS):
                        if m >= 1:
                            sy.wait_ge(s_dve, dve_b + 2 * m)
                            sy.wait_ge(xsp_s, xsp_b + 16 * m)
                        sy.dma_start(
                            scrB[:, :],
                            x1_sp.ap()[ti * T + m * 128:
                                       ti * T + (m + 1) * 128, :]).then_inc(
                                           xl_s, 16)
                        inc("xl", 16)
                blk.sync(_sy)

                def _v(v, dve_b=dve_b, act_b=act_b, xl_b=xl_b):
                    v = wrapv(v)
                    for m in range(MS):
                        fmv = u2[:, m * D:(m + 1) * D]
                        v.reduce_sum(colf[:, 8:9], fmv, axis=AX.X)
                        v.wait_ge(s_act, act_b + 2 * m + 1)
                        v.tensor_scalar_mul(colf[:, 11:12], colf[:, 9:10], 1.0 / D)
                        v.tensor_scalar_mul(colf[:, 10:11], colf[:, 8:9], 1.0 / D)
                        v.tensor_mul(colf[:, 12:13], colf[:, 10:11], colf[:, 10:11])
                        v.tensor_tensor(colf[:, 13:14], colf[:, 11:12],
                                        colf[:, 12:13], ALU.subtract, ms=True)
                        v.wait_ge(s_act, act_b + 2 * m + 2)
                        v.reciprocal(colf[:, 12:13], colf[:, 12:13])
                        v.tensor_mul(colf[:, 15:16], colf[:, 10:11], colf[:, 12:13])
                        v.wait_ge(xl_s, xl_b + 16 * (m + 1))
                        v.scalar_tensor_tensor(scrB[:, :], fmv, colf[:, 12:13],
                                               scrB[:, :], ALU.mult, ALU.add)
                        v.tensor_scalar(scrB[:, :], scrB[:, :], colf[:, 15:16],
                                        None, ALU.subtract, ms=True)
                blk.vector(_v)

                def _a(a, dve_b=dve_b):
                    a = wrapa(a)
                    for m in range(MS):
                        if m >= 1:
                            a.wait_ge(s_dve, dve_b + 2 * m)
                        a.activation(scrA[:, :], u2[:, m * D:(m + 1) * D],
                                     AF.Square, accum_out=colf[:, 9:10],
                                     ms=True)
                        a.wait_ge(s_dve, dve_b + 2 * m + 1)
                        a.activation(colf[:, 12:13], colf[:, 13:14], AF.Sqrt,
                                     bias=cbias[:, 0:1], ms=True)
                blk.scalar(_a)

                def _g(g, ti=ti, dve_b=dve_b):
                    for m in range(MS):
                        g.wait_ge(s_dve, dve_b + 2 * m + 2)
                        g.dma_start(
                            y_out.ap()[ti * T + m * 128:
                                       ti * T + (m + 1) * 128, :],
                            scrB[:, :]).then_inc(xsp_s, 16)
                        inc("xsp", 16)
                    g.wait_ge(xsp_s, val("xsp"))
                blk.gpsimd(_g)


        if dbg:
            dumps = [("d_qkvT", qkvT, [D, DQKV], BF16),
                     ("d_swT", swT, [D, DSW], BF16),
                     ("d_sq", sq_sp, [128, KC, n_tok], BF16),
                     ("d_w", w_sp, [128, KC, n_tok], BF16),
                     ("d_wv", wv_sp, [128, KC, n_tok], BF16),
                     ("d_st", st_sp, [NT, 128, 2 * MS], F32),
                     ("d_cc", cc_out, [NCORES * 128, 2 * KC], F32),
                     ("d_x1", x1_sp, [n_tok, D], F32)]
            with nc.Block("dbg") as blk:
                def _g(g):
                    for nm, src, sh, dt in dumps:
                        od = nc.dram_tensor(nm, sh, dt, kind="ExternalOutput")
                        spec = ("a b -> (a b)" if len(sh) == 2 else
                                "a b c -> (a b c)")
                        dma(g, "g", od.ap().rearrange(spec),
                            src.ap().rearrange(spec))
                    g.wait_ge(s_g, val("g"))
                blk.gpsimd(_g)

        with nc.Block("fin") as blk:
            def _g(g):
                g.wait_ge(s_g, val("g"))
            blk.gpsimd(_g)

    return nc


_NC_CACHE = {}


def kernel(L0_attn_x, L0_attn_w_qkv, L0_attn_w_swiglu, L0_attn_w_out,
           L0_ffn_Wg, L0_ffn_W1, L0_ffn_W2, L0_ffn_W3):
    B, S, _ = L0_attn_x.shape
    n_tok = (B * S) // NCORES
    chunks_per_seq = NCORES // B

    if n_tok not in _NC_CACHE:
        _NC_CACHE[n_tok] = build_nc(n_tok)
    nc = _NC_CACHE[n_tok]

    x = np.ascontiguousarray(np.asarray(L0_attn_x, dtype=np.float32))
    wq = np.asarray(L0_attn_w_qkv, np.float32)
    wsw = np.asarray(L0_attn_w_swiglu, np.float32)
    wo = np.asarray(L0_attn_w_out, np.float32)
    w1 = np.asarray(L0_ffn_W1, np.float32)
    w2 = np.asarray(L0_ffn_W2, np.float32)
    w3 = np.asarray(L0_ffn_W3, np.float32)
    w3p = np.zeros((NE, D, D), np.float32)
    w3p[:, :, 0:DH] = w3
    SHK = D // NCORES
    SHKO = DFF // NCORES
    consts = {
        "w_g": np.ascontiguousarray(np.asarray(L0_ffn_Wg, np.float32)),
        "ident_bf": np.eye(128, dtype=ml_dtypes.bfloat16),
        "ones_col": np.ones((128, 1), dtype=ml_dtypes.bfloat16),
        "ones_row": np.ones((1, 128), dtype=ml_dtypes.bfloat16),
    }
    in_maps = []
    for c in range(NCORES):
        b = c // chunks_per_seq
        p = c % chunks_per_seq
        shard = np.ascontiguousarray(x[b, p * n_tok:(p + 1) * n_tok, :])
        mrow = np.zeros((NCORES,), dtype=np.float32)
        for r in range(NCORES):
            if r // chunks_per_seq == b and r % chunks_per_seq < p:
                mrow[r] = 1.0
        mask = np.broadcast_to(mrow, (128, NCORES)).copy()
        in_maps.append(dict(
            consts, x=shard, mask=mask,
            sh_qkv=np.ascontiguousarray(wq[:, c * SHK:(c + 1) * SHK]),
            sh_sw=np.ascontiguousarray(wsw[:, c * SHK:(c + 1) * SHK]),
            sh_wo=np.ascontiguousarray(wo[:, c * SHKO:(c + 1) * SHKO]),
            sh_w1=np.ascontiguousarray(w1[:, :, c * SHK:(c + 1) * SHK]),
            sh_w2=np.ascontiguousarray(w2[:, :, c * SHK:(c + 1) * SHK]),
            sh_w3=np.ascontiguousarray(w3p[:, :, c * SHK:(c + 1) * SHK])))

    res = run_bass_kernel_spmd(nc, in_maps, core_ids=list(range(NCORES)))
    out = np.zeros((B, S, D), dtype=np.float32)
    for c in range(NCORES):
        b = c // chunks_per_seq
        p = c % chunks_per_seq
        out[b, p * n_tok:(p + 1) * n_tok, :] = np.asarray(res.results[c]["y"])
    return out


# version 09
# speedup vs baseline: 1.0288x; 1.0028x over previous
import contextlib
import numpy as np
import ml_dtypes

import concourse.bass as bass
import concourse.mybir as mybir
from concourse.bass_utils import run_bass_kernel_spmd

F32 = mybir.dt.float32
BF16 = mybir.dt.bfloat16
AF = mybir.ActivationFunctionType
ALU = mybir.AluOpType
AX = mybir.AxisListType

D = 2048
DQKV = 3 * D
DFF = 8192
DSW = 2 * DFF
NE = 8
DH = 1408
KC = D // 128
HC = DH // 128
E1 = 1e-5
E2 = 1.1920929e-07
NCORES = 8
BIG_NEG = -1.0e9
T = 512
MS = T // 128


def build_nc(n_tok=4096, dbg=False):
    assert n_tok % T == 0
    NT = n_tok // T

    nc = bass.Bass("TRN2", target_bir_lowering=False)

    SHK = D // NCORES
    SHKO = DFF // NCORES
    x_in = nc.dram_tensor("x", [n_tok, D], F32, kind="ExternalInput")
    sh_qkv = nc.dram_tensor("sh_qkv", [DQKV, SHK], F32, kind="ExternalInput")
    sh_sw = nc.dram_tensor("sh_sw", [DSW, SHK], F32, kind="ExternalInput")
    sh_wo = nc.dram_tensor("sh_wo", [D, SHKO], F32, kind="ExternalInput")
    w_g = nc.dram_tensor("w_g", [NE, D], F32, kind="ExternalInput")
    sh_w1 = nc.dram_tensor("sh_w1", [NE, DH, SHK], F32, kind="ExternalInput")
    sh_w2 = nc.dram_tensor("sh_w2", [NE, DH, SHK], F32, kind="ExternalInput")
    sh_w3 = nc.dram_tensor("sh_w3", [NE, D, SHK], F32, kind="ExternalInput")
    ident_bf = nc.dram_tensor("ident_bf", [128, 128], BF16, kind="ExternalInput")
    ones_col = nc.dram_tensor("ones_col", [128, 1], BF16, kind="ExternalInput")
    ones_row = nc.dram_tensor("ones_row", [1, 128], BF16, kind="ExternalInput")
    mask_in = nc.dram_tensor("mask", [128, NCORES], F32, kind="ExternalInput")
    y_out = nc.dram_tensor("y", [n_tok, D], F32, kind="ExternalOutput")

    qkvBs = nc.dram_tensor("qkvBs", [DQKV, SHK], BF16)
    swBs = nc.dram_tensor("swBs", [DSW, SHK], BF16)
    woBs = nc.dram_tensor("woBs", [D, SHKO], BF16)
    wgB = nc.dram_tensor("wgB", [NE, D], BF16)
    w1Bs = nc.dram_tensor("w1Bs", [NE, DH, SHK], BF16)
    w2Bs = nc.dram_tensor("w2Bs", [NE, DH, SHK], BF16)
    w3Bs = nc.dram_tensor("w3Bs", [NE, D, SHK], BF16)
    qkvTs = nc.dram_tensor("qkvTs", [SHK, DQKV], BF16)
    swTs = nc.dram_tensor("swTs", [SHK, DSW], BF16)
    woTs = nc.dram_tensor("woTs", [SHKO, D], BF16)
    w1Ts = nc.dram_tensor("w1Ts", [NE * SHK, DH], BF16)
    w2Ts = nc.dram_tensor("w2Ts", [NE * SHK, DH], BF16)
    w3Ts = nc.dram_tensor("w3Ts", [NE * SHK, D], BF16)
    qkvT = nc.dram_tensor("qkvT", [D, DQKV], BF16, addr_space="Shared")
    swT = nc.dram_tensor("swT", [D, DSW], BF16, addr_space="Shared")
    woT = nc.dram_tensor("woT", [DFF, D], BF16, addr_space="Shared")
    w1T = nc.dram_tensor("w1T", [NCORES * NE * SHK, DH], BF16, addr_space="Shared")
    w2T = nc.dram_tensor("w2T", [NCORES * NE * SHK, DH], BF16, addr_space="Shared")
    w3T = nc.dram_tensor("w3T", [NCORES * NE * SHK, D], BF16, addr_space="Shared")
    sq_sp = nc.dram_tensor("sq_sp", [128, KC, n_tok], BF16)
    w_sp = nc.dram_tensor("w_sp", [128, KC, n_tok], BF16)
    wv_sp = nc.dram_tensor("wv_sp", [128, KC, n_tok], BF16)
    x1_sp = nc.dram_tensor("x1_sp", [n_tok, D], F32)
    st_sp = nc.dram_tensor("st_sp", [NT, 128, 2 * MS], F32)
    tt_sp = nc.dram_tensor("tt_sp", [NT, 128, 2 * KC], F32)
    cc_in = nc.dram_tensor("cc_in", [128, 2 * KC], F32)
    cc_out = nc.dram_tensor("cc_out", [NCORES * 128, 2 * KC], F32,
                            addr_space="Shared")

    n = {}

    def inc(k, a=1):
        n[k] = n.get(k, 0) + a
        return n[k]

    def val(k):
        return n.get(k, 0)

    with contextlib.ExitStack() as est:
        sbt = lambda nm, sh, dt: est.enter_context(nc.sbuf_tensor(nm, sh, dt))
        pstens = lambda nm, sh, dt: est.enter_context(nc.psum_tensor(nm, sh, dt))
        semt = lambda nm: est.enter_context(nc.semaphore(nm))

        c_ident = sbt("c_ident", [128, 128], BF16)
        c_onescol = sbt("c_onescol", [128, 1], BF16)
        c_onesrow = sbt("c_onesrow", [1, 128], BF16)
        c_mask = sbt("c_mask", [128, NCORES], F32)
        c_zero = sbt("c_zero", [128, T], BF16)
        wgT_sb = sbt("wgT_sb", [128, KC * NE], BF16)
        wg_ld = sbt("wg_ld", [NE, 256], BF16)

        UBN = NE * HC * T
        ub = sbt("ub", [128, UBN], BF16)
        qb_o, kb_o, vb_o, xr_o, q2_o = (0, KC * T, 2 * KC * T, 3 * KC * T,
                                        4 * KC * T)
        k2_o = xr_o
        y_o = q2_o

        wsA = sbt("wsA", [128, KC * T], BF16)
        wsB = sbt("wsB", [128, KC * T], BF16)
        fm = sbt("fm", [128, KC * T], BF16)
        u2 = sbt("u2", [128, MS * D], F32)
        scrA = sbt("scrA", [128, D], F32)
        scrB = sbt("scrB", [128, D], F32)
        scrD = sbt("scrD", [128, 4096], BF16)
        ttot = sbt("ttot", [128, 2 * KC], F32)
        bw_all = sbt("bw_all", [128, NE * T], BF16)
        bq_sb = sbt("bq_sb", [128, T], BF16)
        bk_sb = fm[:, 0:T]
        tmpA = sbt("tmpA", [128, T], F32)
        tmpB = sbt("tmpB", [128, T], F32)
        colf = sbt("colf", [128, 16], F32)
        wtot = sbt("wtot", [128, 2 * KC], F32)
        carry_w = sbt("carry_w", [128, KC], F32)
        carry_wv = sbt("carry_wv", [128, KC], F32)
        strow = sbt("strow", [128, 2 * MS], F32)
        cbias = sbt("cbias", [128, 4], F32)

        p0 = pstens("p0", [128, 512], F32)
        p1 = pstens("p1", [128, 512], F32)
        p2 = pstens("p2", [128, 512], F32)
        p3 = pstens("p3", [128, 512], F32)
        p4 = pstens("p4", [128, 512], F32)
        p5 = pstens("p5", [128, 512], F32)
        pbf = pstens("pbf", [128, 512], BF16)
        pstat = pstens("pstat", [1, 512], F32)
        pms = [p0, p1, p2, p3]

        s_ld = semt("s_ld")
        s_g = semt("s_g")
        s_sc = semt("s_sc")
        s_pe = semt("s_pe")
        s_dve = semt("s_dve")
        s_act = semt("s_act")
        s_cc = semt("s_cc")
        SEM = {"ld": s_ld, "g": s_g, "sc": s_sc, "pe": s_pe, "dve": s_dve,
               "act": s_act, "cc": s_cc}
        s_vv = semt("s_vv")
        s_aa = semt("s_aa")

        _SER_OPS = {"tensor_tensor", "tensor_copy", "tensor_scalar",
                    "tensor_scalar_mul", "tensor_scalar_add", "tensor_mul",
                    "tensor_add", "tensor_sub", "reduce_sum", "reduce_max",
                    "reciprocal", "tensor_tensor_scan", "memset",
                    "tensor_single_scalar", "scalar_tensor_tensor",
                    "activation", "copy"}

        class _Ser:

            def __init__(self, eng, ms_sem, ms_key, ser_sem, ser_key):
                self._e = eng
                self._ms = (ms_sem, ms_key)
                self._ser = (ser_sem, ser_key)
                self._prev = None

            def __getattr__(self, name):
                f = getattr(self._e, name)
                if name not in _SER_OPS:
                    return f

                def call(*a, ms=False, **kw):
                    if self._prev is not None:
                        self._e.wait_ge(*self._prev)
                    ins = f(*a, **kw)
                    sem, key = self._ms if ms else self._ser
                    ins.then_inc(sem, 1)
                    inc(key)
                    self._prev = (sem, val(key))
                    return ins
                return call

        sw_u = [semt("sw_u0"), semt("sw_u1")]
        sw_g = [semt("sw_g0"), semt("sw_g1")]
        op_s = [semt("op_s0"), semt("op_s1")]
        m1s = [semt("m1s0"), semt("m1s1")]
        m2s = [semt("m2s0"), semt("m2s1")]
        w3s = [semt("w3s0"), semt("w3s1")]
        wgl = [semt("wgl0"), semt("wgl1")]
        s_wc = semt("s_wc")
        qk = [semt("qk0"), semt("qk1")]
        stg_t = [semt("stg_t0"), semt("stg_t1")]
        stg_s = [semt("stg_s0"), semt("stg_s1")]
        xl_s = semt("xl_s")
        xsp_s = semt("xsp_s")
        cast_sems = {w: semt(f"cast_{w}") for w in
                     ("sw", "wo", "w1", "w2", "w3")}
        cast_q = semt("cast_q")

        def sdma(eng, sem, key, dst, src):
            eng.dma_start(dst, src).then_inc(sem, 16)
            return inc(key, 16)

        def wrapv(v):
            return _Ser(v, s_dve, "dve", s_vv, "vv")

        def wrapa(a):
            return _Ser(a, s_act, "act", s_aa, "aa")

        def dma(eng, q, dst, src, **kw):
            eng.dma_start(dst, src, **kw).then_inc(SEM[q], 16)
            inc(q, 16)

        def ub_c(off, j):
            return ub[:, off + j * T: off + (j + 1) * T]

        def fm_c(kc):
            return fm[:, kc * T:(kc + 1) * T]

        def sp_view(sp, ti):
            return sp.ap()[:, :, ti * T:(ti + 1) * T]

        def w12_slab(sy, sem, key, ws, st, wt, e, h0, hw):
            base = (st % 2) * (KC * 256)
            view = wt.ap().rearrange("(r e ih p) h -> e ih p r h",
                                     r=NCORES, e=NE, p=128)[e]
            dst = ws[:, base:base + KC * hw].rearrange(
                "p (kc h) -> p kc h", h=hw)
            for ih in range(2):
                sdma(sy, sem, key,
                     dst[:, ih::2, :],
                     view[ih][:, :, h0:h0 + hw])

        def w3_slab(sy, st, dr, e):
            buf = [wsA, wsB][st % 2]
            view = w3T.ap().rearrange("(r e ih p) d -> e ih p r d",
                                      r=NCORES, e=NE, p=128)[e]
            dst = buf[:, 0:HC * 512].rearrange("p (kc d) -> p kc d", d=512)
            sdma(sy, w3s[st % 2], f"w3{st % 2}",
                 dst[:, 0::2, :][:, 0:6, :],
                 view[0][:, 0:6, dr * 512:(dr + 1) * 512])
            sdma(sy, w3s[st % 2], f"w3{st % 2}",
                 dst[:, 1::2, :],
                 view[1][:, 0:5, dr * 512:(dr + 1) * 512])

        def ub_3d(off):
            return ub[:, off: off + KC * T].rearrange("p (kc t) -> p kc t", t=T)

        CH = 1 << 21

        def cast_chunks(csrc, cdst):
            sh = csrc.shape
            nm = int(np.prod(sh))
            spec = ("a b -> (a b)" if len(sh) == 2 else "a b c -> (a b c)")
            fsrc = csrc.ap().rearrange(spec)
            fdst = cdst.ap().rearrange(spec)
            return [(fsrc[o:min(o + CH, nm)], fdst[o:min(o + CH, nm)])
                    for o in range(0, nm, CH)]

        sh_pairs = []
        for srcB, dstT, C, R in [(qkvBs.ap(), qkvTs.ap(), SHK, DQKV),
                                 (swBs.ap(), swTs.ap(), SHK, DSW),
                                 (woBs.ap(), woTs.ap(), SHKO, D)]:
            for kc in range(C // 128):
                r0 = 0
                while r0 < R:
                    rs = min(4096, R - r0)
                    sh_pairs.append((srcB, dstT, kc, r0, rs))
                    r0 += rs
        for e in range(NE):
            for srcB, dstT, R in [
                    (w1Bs[e], w1Ts.ap()[e * SHK:(e + 1) * SHK, :], DH),
                    (w2Bs[e], w2Ts.ap()[e * SHK:(e + 1) * SHK, :], DH),
                    (w3Bs[e], w3Ts.ap()[e * SHK:(e + 1) * SHK, :], D)]:
                for kc in range(SHK // 128):
                    sh_pairs.append((srcB, dstT, kc, 0, R))

        stg = [(wsA, 0), (wsA, 4096), (wsB, 0), (wsB, 4096)]
        sl_t = [semt(f"sl_t{i}") for i in range(4)]
        sl_s = [semt(f"sl_s{i}") for i in range(4)]
        cast_all = semt("cast_all")
        s_ccq = semt("s_ccq")
        n_casts = [0]

        with nc.Block("prep") as blk:
            def _g(g):
                for csrc, cdst in [(sh_qkv, qkvBs), (sh_sw, swBs),
                                   (sh_wo, woBs), (w_g, wgB),
                                   (sh_w1, w1Bs), (sh_w2, w2Bs),
                                   (sh_w3, w3Bs)]:
                    for fsrc, fdst in cast_chunks(csrc, cdst):
                        g.dma_start(fdst, fsrc).then_inc(cast_all, 16)
                        n_casts[0] += 1
            blk.gpsimd(_g)

            def _sc(sc):
                sc.wait_ge(cast_all, 16 * n_casts[0])
                for i, (srcB, dstT, kc, r0, rs) in enumerate(sh_pairs):
                    sl = i % 4
                    buf, off = stg[sl]
                    if i >= 4:
                        sc.wait_ge(sl_s[sl], 16 * (i // 4))
                    sc.dma_start(buf[:, off:off + rs],
                                 srcB[r0:r0 + rs, kc * 128:(kc + 1) * 128],
                                 transpose=True).then_inc(sl_t[sl], 16)
                for sl in range(4):
                    nuse = (len(sh_pairs) - sl + 3) // 4
                    sc.wait_ge(sl_t[sl], 16 * nuse)
            blk.scalar(_sc)

            def _sy(sy):
                dma(sy, "ld", c_ident[:, :], ident_bf[:, :])
                dma(sy, "ld", c_onescol[:, :], ones_col[:, :])
                dma(sy, "ld", c_onesrow[:, :], ones_row[:, :])
                dma(sy, "ld", c_mask[:, :], mask_in[:, :])
                for i, (srcB, dstT, kc, r0, rs) in enumerate(sh_pairs):
                    sl = i % 4
                    buf, off = stg[sl]
                    sy.wait_ge(sl_t[sl], 16 * (i // 4 + 1))
                    sy.dma_start(dstT[kc * 128:(kc + 1) * 128, r0:r0 + rs],
                                 buf[:, off:off + rs]).then_inc(sl_s[sl], 16)
                for sl in range(4):
                    nuse = (len(sh_pairs) - sl + 3) // 4
                    sy.wait_ge(sl_s[sl], 16 * nuse)
                sy.wait_ge(s_ld, val("ld"))
            blk.sync(_sy)

            def _v(v):
                v = wrapv(v)
                v.memset(c_zero[:, :], 0.0)
                v.memset(wtot[:, :], 0.0)
                v.memset(cbias[:, 0:1], E1)
                v.memset(cbias[:, 1:2], E1 * E2)
                v.memset(cbias[:, 2:3], E2)
            blk.vector(_v)

        with nc.Block("gather") as blk:
            def _g(g):
                g.collective_compute(
                    "AllGather", ALU.bypass,
                    replica_groups=[list(range(NCORES))],
                    ins=[qkvTs.ap().opt()],
                    outs=[qkvT.ap().opt()]).then_inc(s_ccq, 1)
                for tin, tout in [(swTs, swT), (woTs, woT),
                                  (w1Ts, w1T), (w2Ts, w2T), (w3Ts, w3T)]:
                    g.collective_compute(
                        "AllGather", ALU.bypass,
                        replica_groups=[list(range(NCORES))],
                        ins=[tin.ap().opt()],
                        outs=[tout.ap().opt()]).then_inc(s_cc, 1)
                    inc("cc")
                g.wait_ge(s_ccq, 1)
            blk.gpsimd(_g)

        with nc.Block("wg") as blk:
            pe_b, dve_b = val("pe"), val("dve")
            wgl_base = [val("wgl0"), val("wgl1")]

            def _g(g, pe_b=pe_b, dve_b=dve_b):
                for kc in range(KC):
                    if kc >= 2:
                        g.wait_ge(s_dve, dve_b + kc - 1)
                    sdma(g, wgl[kc % 2], f"wgl{kc % 2}",
                         wg_ld[:, (kc % 2) * 128:(kc % 2) * 128 + 128],
                         wgB.ap()[:, kc * 128:(kc + 1) * 128])
            blk.gpsimd(_g)

            def _t(t, pe_b=pe_b, dve_b=dve_b, wgl_base=wgl_base):
                for kc in range(KC):
                    t.wait_ge(wgl[kc % 2],
                              wgl_base[kc % 2] + 16 * (kc // 2 + 1))
                    if kc >= 1:
                        t.wait_ge(s_dve, dve_b + kc)
                    t.transpose(pbf[:, 0:NE],
                                wg_ld[:, (kc % 2) * 128:(kc % 2) * 128 + 128],
                                c_ident[0:NE, 0:NE]).then_inc(s_pe, 1)
                    inc("pe")
            blk.tensor(_t)

            def _v(v, pe_b=pe_b):
                v = wrapv(v)
                for kc in range(KC):
                    v.wait_ge(s_pe, pe_b + kc + 1)
                    v.tensor_copy(wgT_sb[:, kc * NE:(kc + 1) * NE],
                                  pbf[:, 0:NE], ms=True)
            blk.vector(_v)

        NSLAB_QKV = DQKV // T

        def qkv_slab_dma(sy, s):
            dma(sy, "ld", wsA[:, :] if s % 2 == 0 else wsB[:, :],
                qkvT.ap().rearrange("(kc p) o -> p kc o", p=128)[:, :, s * T:(s + 1) * T])

        for ti in range(NT):
            with nc.Block(f"A_L0_{ti}") as blk:
                def _sy(sy, ti=ti):
                    dma(sy, "ld", u2[:, :].rearrange("p (m d) -> p m d", m=MS),
                        x_in.ap().rearrange("(tt m p) d -> tt p m d",
                                            m=MS, p=128)[ti])
                    sy.wait_ge(s_ld, val("ld"))
                blk.sync(_sy)

            with nc.Block(f"A_C0_{ti}") as blk:
                dve_b, act_b = val("dve"), val("act")

                def _v(v, ti=ti, dve_b=dve_b, act_b=act_b):
                    v = wrapv(v)
                    for m in range(MS):
                        xm = u2[:, m * D:(m + 1) * D]
                        v.reduce_sum(colf[:, 0:1], xm, axis=AX.X)
                        v.wait_ge(s_act, act_b + 2 * m + 1)
                        v.reduce_sum(colf[:, 1:2], scrA[:, :], axis=AX.X)
                        v.tensor_scalar_mul(colf[:, 2:3], colf[:, 0:1], 1.0 / D)
                        v.tensor_scalar_mul(colf[:, 3:4], colf[:, 1:2], 1.0 / D)
                        v.tensor_mul(colf[:, 4:5], colf[:, 2:3], colf[:, 2:3])
                        v.tensor_tensor(colf[:, 5:6], colf[:, 3:4], colf[:, 4:5],
                                        ALU.subtract, ms=True)
                        v.wait_ge(s_act, act_b + 2 * m + 2)
                        v.reciprocal(colf[:, 6:7], colf[:, 6:7])
                        v.reciprocal(colf[:, 7:8], colf[:, 7:8])
                        v.tensor_copy(strow[:, 2 * m:2 * m + 1], colf[:, 6:7])
                        v.tensor_mul(strow[:, 2 * m + 1:2 * m + 2],
                                     colf[:, 2:3], colf[:, 6:7])
                        v.tensor_mul(colf[:, 4:5], colf[:, 2:3], colf[:, 7:8])
                        v.tensor_scalar(ub[:, xr_o + m * D: xr_o + (m + 1) * D],
                                        xm, colf[:, 7:8], colf[:, 4:5],
                                        ALU.mult, ALU.subtract, ms=True)
                blk.vector(_v)

                def _a(a, dve_b=dve_b):
                    a = wrapa(a)
                    for m in range(MS):
                        if m > 0:
                            a.wait_ge(s_dve, dve_b + 2 * m)
                        a.activation(scrA[:, :], u2[:, m * D:(m + 1) * D],
                                     AF.Square, ms=True)
                        a.wait_ge(s_dve, dve_b + 2 * m + 1)
                        a.activation(colf[:, 6:7], colf[:, 5:6], AF.Sqrt,
                                     bias=cbias[:, 0:1])
                        a.activation(colf[:, 7:8], colf[:, 5:6], AF.Sqrt,
                                     scale=1.0 + E2,
                                     bias=cbias[:, 1:2], ms=True)
                blk.scalar(_a)

                dve_end = dve_b + 2 * MS

                def _g(g, ti=ti, dve_end=dve_end):
                    g.wait_ge(s_dve, dve_end)
                    dma(g, "g", st_sp[ti], strow[:, :])
                    g.wait_ge(s_g, val("g"))
                blk.gpsimd(_g)

            with nc.Block(f"A_C1_{ti}") as blk:
                pe_b, dve_b = val("pe"), val("dve")
                NG = MS * (KC // 4)

                def _t(t, pe_b=pe_b, dve_b=dve_b):
                    gidx = 0
                    for m in range(MS):
                        for kg in range(KC // 4):
                            if gidx > 0:
                                t.wait_ge(s_dve, dve_b + gidx)
                            for i in range(4):
                                kc = kg * 4 + i
                                t.transpose(pbf[:, i * 128:(i + 1) * 128],
                                            ub[:, xr_o + m * D + kc * 128:
                                               xr_o + m * D + (kc + 1) * 128],
                                            c_ident[:, :]).then_inc(s_pe, 1)
                                inc("pe")
                            gidx += 1
                blk.tensor(_t)

                def _v(v, pe_b=pe_b):
                    v = wrapv(v)
                    gidx = 0
                    for m in range(MS):
                        for kg in range(KC // 4):
                            v.wait_ge(s_pe, pe_b + 4 * (gidx + 1))
                            v.tensor_copy(
                                fm[:, :].rearrange("p (kc t) -> p kc t", t=T)
                                [:, kg * 4:(kg + 1) * 4, m * 128:(m + 1) * 128],
                                pbf[:, :].rearrange("p (a b) -> p a b", b=128)
                            , ms=True)
                            gidx += 1
                blk.vector(_v)

                def _sy(sy):
                    for s in range(2):
                        sdma(sy, qk[s % 2], f"qk{s % 2}",
                             [wsA, wsB][s % 2][:, :],
                             qkvT.ap().rearrange("(kc p) o -> p kc o", p=128)
                             [:, :, s * T:(s + 1) * T])
                blk.sync(_sy)

            with nc.Block(f"A_C2_{ti}") as blk:
                pe_b, dve_b, act_b = val("pe"), val("dve"), val("act")
                qk_b = [val("qk0") - 16, val("qk1") - 16]

                def _t(t, pe_b=pe_b, dve_b=dve_b, qk_b=qk_b):
                    for s in range(NSLAB_QKV):
                        t.wait_ge(qk[s % 2], qk_b[s % 2] + 16 * (s // 2 + 1))
                        ws = [wsA, wsB][s % 2]
                        for jj in range(4):
                            j = s * 4 + jj
                            if j >= 2:
                                t.wait_ge(s_dve, dve_b + j - 1)
                            pb = [p0, p1][j % 2]
                            for kc in range(KC):
                                ins = t.matmul(pb[:, 0:T],
                                               ws[:, kc * T + jj * 128:
                                                  kc * T + (jj + 1) * 128],
                                               fm_c(kc),
                                               start=(kc == 0),
                                               stop=(kc == KC - 1))
                            ins.then_inc(s_pe, 1)
                            inc("pe")
                blk.tensor(_t)

                def _v(v, pe_b=pe_b):
                    v = wrapv(v)
                    for j in range(4 * NSLAB_QKV):
                        v.wait_ge(s_pe, pe_b + j + 1)
                        off = qb_o if j < KC else (kb_o if j < 2 * KC else vb_o)
                        v.tensor_copy(ub_c(off, j % KC),
                                      [p0, p1][j % 2][:, 0:T], ms=True)
                blk.vector(_v)

                def _a(a, dve_b=dve_b):
                    a = wrapa(a)
                    for j in range(2 * KC):
                        a.wait_ge(s_dve, dve_b + j + 1)
                        src_off = qb_o if j < KC else kb_o
                        dst_off = q2_o if j < KC else k2_o
                        a.activation(ub_c(dst_off, j % KC),
                                     ub_c(src_off, j % KC),
                                     AF.Square, ms=True)
                blk.scalar(_a)

                def _sy(sy, pe_b=pe_b):
                    for s in range(2, NSLAB_QKV):
                        sy.wait_ge(s_pe, pe_b + 4 * (s - 1))
                        sdma(sy, qk[s % 2], f"qk{s % 2}",
                             [wsA, wsB][s % 2][:, :],
                             qkvT.ap().rearrange("(kc p) o -> p kc o", p=128)
                             [:, :, s * T:(s + 1) * T])
                    sy.wait_ge(s_ld, val("ld"))
                blk.sync(_sy)


            with nc.Block(f"A_C3_{ti}") as blk:
                pe_b, dve_b, act_b = val("pe"), val("dve"), val("act")

                def _t(t, pe_b=pe_b, dve_b=dve_b):
                    for kc in range(KC):
                        ins = t.matmul(pstat[0:1, 0:T], c_onescol[:, 0:1],
                                       ub_c(q2_o, kc), start=(kc == 0),
                                       stop=(kc == KC - 1))
                    ins.then_inc(s_pe, 1)
                    inc("pe")
                    t.wait_ge(s_dve, dve_b + 1)
                    for kc in range(KC):
                        ins = t.matmul(pstat[0:1, 0:T], c_onescol[:, 0:1],
                                       ub_c(k2_o, kc), start=(kc == 0),
                                       stop=(kc == KC - 1))
                    ins.then_inc(s_pe, 1)
                    inc("pe")
                    t.wait_ge(s_dve, dve_b + 3)
                    t.matmul(p2[:, 0:T], c_onesrow[0:1, :], bw_all[0:1, 0:T],
                             start=True, stop=True).then_inc(s_pe, 1)
                    inc("pe")
                    t.matmul(p3[:, 0:T], c_onesrow[0:1, :], bw_all[0:1, T:2 * T],
                             start=True, stop=True).then_inc(s_pe, 1)
                    inc("pe")
                blk.tensor(_t)

                def _v(v, pe_b=pe_b, act_b=act_b):
                    v = wrapv(v)
                    v.wait_ge(s_pe, pe_b + 1)
                    v.tensor_copy(u2[0:1, 0:T], pstat[0:1, 0:T], ms=True)
                    v.wait_ge(s_pe, pe_b + 2)
                    v.tensor_copy(u2[0:1, T:2 * T], pstat[0:1, 0:T], ms=True)
                    v.wait_ge(s_act, act_b + 1)
                    v.reciprocal(u2[0:1, 0:T], u2[0:1, 0:T])
                    v.reciprocal(u2[0:1, T:2 * T], u2[0:1, T:2 * T])
                    v.tensor_copy(bw_all[0:1, 0:2 * T], u2[0:1, 0:2 * T], ms=True)
                    v.wait_ge(s_pe, pe_b + 3)
                    v.tensor_copy(bq_sb[:, :], p2[:, 0:T])
                    v.wait_ge(s_pe, pe_b + 4)
                    v.tensor_copy(bk_sb[:, :], p3[:, 0:T])
                blk.vector(_v)

                def _a(a, dve_b=dve_b):
                    a = wrapa(a)
                    a.wait_ge(s_dve, dve_b + 2)
                    a.activation(u2[0:1, 0:T], u2[0:1, 0:T], AF.Sqrt,
                                 scale=1.0 / D, bias=cbias[0:1, 2:3])
                    a.activation(u2[0:1, T:2 * T], u2[0:1, T:2 * T],
                                 AF.Sqrt, scale=1.0 / D,
                                 bias=cbias[0:1, 2:3], ms=True)
                blk.scalar(_a)

            with nc.Block(f"A_C4_{ti}") as blk:
                dve_b, act_b = val("dve"), val("act")
                tp = [tmpA, tmpB]

                def _v(v, dve_b=dve_b, act_b=act_b):
                    v = wrapv(v)
                    for j in range(KC):
                        if j >= 2:
                            v.wait_ge(s_act, act_b + j - 1)
                        v.tensor_tensor(tp[j % 2][:, :], ub_c(qb_o, j),
                                        bq_sb[:, :], ALU.mult, ms=True)
                    for j in range(KC):
                        v.wait_ge(s_act, act_b + KC + max(0, j - 1))
                        v.tensor_tensor(tp[j % 2][:, :], ub_c(kb_o, j),
                                        bk_sb[:, :], ALU.mult, ms=True)
                    for j in range(KC):
                        v.wait_ge(s_act, act_b + KC + j + 1)
                        v.tensor_tensor(ub_c(vb_o, j), ub_c(kb_o, j),
                                        ub_c(vb_o, j), ALU.mult)
                        v.tensor_tensor_scan(ub_c(kb_o, j), ub_c(kb_o, j),
                                             c_zero[:, :], 0.0,
                                             ALU.add, ALU.add, ms=True)
                        v.tensor_tensor(wtot[:, j:j + 1], wtot[:, j:j + 1],
                                        ub_c(kb_o, j)[:, T - 1:T], ALU.add)
                        v.tensor_tensor_scan(ub_c(vb_o, j), ub_c(vb_o, j),
                                             c_zero[:, :], 0.0,
                                             ALU.add, ALU.add, ms=True)
                        v.tensor_tensor(wtot[:, KC + j:KC + j + 1],
                                        wtot[:, KC + j:KC + j + 1],
                                        ub_c(vb_o, j)[:, T - 1:T], ALU.add)
                blk.vector(_v)

                def _a(a, dve_b=dve_b):
                    a = wrapa(a)
                    for j in range(KC):
                        a.wait_ge(s_dve, dve_b + j + 1)
                        a.activation(ub_c(qb_o, j), tp[j % 2][:, :],
                                     AF.Sigmoid, ms=True)
                    for j in range(KC):
                        a.wait_ge(s_dve, dve_b + KC + j + 1)
                        a.activation(ub_c(kb_o, j), tp[j % 2][:, :],
                                     AF.Exp, ms=True)
                    for j in range(KC):
                        a.wait_ge(s_dve, dve_b + 2 * KC + 2 * j + 1)
                        a.activation(ttot[:, j:j + 1],
                                     ub_c(kb_o, j)[:, T - 1:T],
                                     AF.Copy, ms=True)
                        a.wait_ge(s_dve, dve_b + 2 * KC + 2 * j + 2)
                        a.activation(ttot[:, KC + j:KC + j + 1],
                                     ub_c(vb_o, j)[:, T - 1:T],
                                     AF.Copy, ms=True)
                blk.scalar(_a)

                dve_end = dve_b + 4 * KC
                act_end = act_b + 4 * KC

                def _g(g, ti=ti, dve_end=dve_end, act_end=act_end):
                    g.wait_ge(s_dve, dve_end)
                    g.wait_ge(s_act, act_end)
                    dma(g, "g", sp_view(sq_sp, ti), ub_3d(qb_o))
                    dma(g, "g", sp_view(w_sp, ti), ub_3d(kb_o))
                    dma(g, "g", sp_view(wv_sp, ti), ub_3d(vb_o))
                    dma(g, "g", tt_sp[ti], ttot[:, :])
                    g.wait_ge(s_g, val("g"))
                blk.gpsimd(_g)

        gbuf = scrA
        with nc.Block("cc1") as blk:
            def _g(g):
                g.wait_ge(s_cc, 5)
                dma(g, "g", cc_in[:, :], wtot[:, :])
                g.wait_ge(s_g, val("g"))
                g.collective_compute(
                    "AllGather", ALU.bypass,
                    replica_groups=[list(range(NCORES))],
                    ins=[cc_in.ap().opt()],
                    outs=[cc_out.ap().opt()]).then_inc(s_cc, 1)
                inc("cc")
                g.wait_ge(s_cc, val("cc"))
                dma(g, "g", gbuf[:, 0:NCORES * 2 * KC].rearrange("p (r c) -> p r c", r=NCORES),
                    cc_out.ap().rearrange("(r p) c -> p r c", p=128))
                g.wait_ge(s_g, val("g"))
            blk.gpsimd(_g)

        with nc.Block("cc2") as blk:
            def _v(v):
                v = wrapv(v)
                v.memset(carry_w[:, :], 0.0)
                v.memset(carry_wv[:, :], 0.0)
                for r in range(NCORES):
                    v.tensor_scalar_mul(tmpA[:, 0:2 * KC],
                                        gbuf[:, r * 2 * KC:(r + 1) * 2 * KC],
                                        c_mask[:, r:r + 1])
                    v.tensor_tensor(carry_w[:, :], carry_w[:, :],
                                    tmpA[:, 0:KC], ALU.add)
                    v.tensor_tensor(carry_wv[:, :], carry_wv[:, :],
                                    tmpA[:, KC:2 * KC], ALU.add)
                v.tensor_scalar_add(carry_w[:, :], carry_w[:, :], 1e-6)
            blk.vector(_v)

        NSW = DFF // T
        NDR = D // 512

        for ti in range(NT):
            slotb = {k: val(k) for k in ("swu0", "swu1", "swg0", "swg1",
                                         "ops0", "ops1", "mh10", "mh11",
                                         "mh20", "mh21", "w30", "w31")}
            with nc.Block(f"B_L0_{ti}") as blk:
                def _sy(sy, ti=ti):
                    dma(sy, "ld", strow[:, :], st_sp[ti])
                    dma(sy, "ld", ttot[:, :], tt_sp[ti])
                    dma(sy, "ld", ub_3d(qb_o), sp_view(sq_sp, ti))
                    dma(sy, "ld", ub_3d(kb_o), sp_view(w_sp, ti))
                    dma(sy, "ld", ub_3d(vb_o), sp_view(wv_sp, ti))
                    dma(sy, "ld", u2[:, :].rearrange("p (m d) -> p m d", m=MS),
                        x_in.ap().rearrange("(tt m p) d -> tt p m d",
                                            m=MS, p=128)[ti])
                    OWp = 256
                    for st in range(2):
                        o0 = st * OWp
                        sdma(sy, sw_u[st % 2], f"swu{st % 2}",
                             wsA[:, (st % 2) * KC * OWp:
                                 (st % 2 + 1) * KC * OWp].rearrange(
                                 "p (kc o) -> p kc o", o=OWp),
                             swT.ap().rearrange("(kc p) o -> p kc o", p=128)
                             [:, :, o0:o0 + OWp])
                        sdma(sy, sw_g[st % 2], f"swg{st % 2}",
                             wsB[:, (st % 2) * KC * OWp:
                                 (st % 2 + 1) * KC * OWp].rearrange(
                                 "p (kc o) -> p kc o", o=OWp),
                             swT.ap().rearrange("(kc p) o -> p kc o", p=128)
                             [:, :, DFF + o0:DFF + o0 + OWp])
                    sy.wait_ge(s_ld, val("ld"))
                blk.sync(_sy)

            with nc.Block(f"B_C0_{ti}") as blk:
                dve_b, act_b = val("dve"), val("act")

                def _a(a, dve_b=dve_b):
                    a = wrapa(a)
                    for kc in range(KC):
                        if kc >= 4:
                            a.wait_ge(s_dve, dve_b + kc - 3)
                        sl = scrA[:, (kc % 4) * 512:(kc % 4) * 512 + T]
                        a.activation(sl, ub_c(kb_o, kc), AF.Ln,
                                     bias=carry_w[:, kc:kc + 1])
                        a.activation(sl, sl, AF.Exp, scale=-1.0, ms=True)
                blk.scalar(_a)

                def _v(v, act_b=act_b):
                    v = wrapv(v)
                    for m in range(MS):
                        v.tensor_scalar(u2[:, m * D:(m + 1) * D],
                                        u2[:, m * D:(m + 1) * D],
                                        strow[:, 2 * m:2 * m + 1],
                                        strow[:, 2 * m + 1:2 * m + 2],
                                        ALU.mult, ALU.subtract)
                    for kc in range(KC):
                        v.tensor_scalar(tmpB[:, :], ub_c(vb_o, kc),
                                        carry_wv[:, kc:kc + 1], None, ALU.add)
                        v.wait_ge(s_act, act_b + kc + 1)
                        v.tensor_mul(tmpB[:, :], tmpB[:, :],
                                     scrA[:, (kc % 4) * 512:(kc % 4) * 512 + T])
                        v.tensor_tensor(ub_c(y_o, kc), tmpB[:, :],
                                        ub_c(qb_o, kc), ALU.mult, ms=True)
                    v.tensor_tensor(carry_w[:, :], carry_w[:, :],
                                    ttot[:, 0:KC], ALU.add)
                    v.tensor_tensor(carry_wv[:, :], carry_wv[:, :],
                                    ttot[:, KC:2 * KC], ALU.add)
                blk.vector(_v)

            OW = 256
            NJ = DFF // 128
            NST = DFF // OW
            with nc.Block(f"B_sw_{ti}") as blk:
                pe_b, dve_b, act_b = val("pe"), val("dve"), val("act")
                swu_b = [slotb["swu0"], slotb["swu1"]]
                swg_b = [slotb["swg0"], slotb["swg1"]]
                tp = [tmpA, tmpB]

                def _t(t, pe_b=pe_b, dve_b=dve_b, act_b=act_b,
                       swu_b=swu_b, swg_b=swg_b):
                    for j in range(NJ):
                        st = j // 2
                        jj = j % 2
                        if jj == 0:
                            t.wait_ge(sw_u[st % 2],
                                      swu_b[st % 2] + 16 * (st // 2 + 1))
                            t.wait_ge(sw_g[st % 2],
                                      swg_b[st % 2] + 16 * (st // 2 + 1))
                        pu = [p0, p1][j % 2]
                        pg = [p2, p3][j % 2]
                        if j >= 2:
                            t.wait_ge(s_dve, dve_b + j - 1)
                            t.wait_ge(s_act, act_b + j - 1)
                        ubase = (st % 2) * KC * OW
                        for kc in range(KC):
                            ins = t.matmul(pu[:, 0:T],
                                           wsA[:, ubase + kc * OW + jj * 128:
                                               ubase + kc * OW + (jj + 1) * 128],
                                           ub_c(y_o, kc),
                                           start=(kc == 0), stop=(kc == KC - 1))
                        ins.then_inc(s_pe, 1)
                        inc("pe")
                        for kc in range(KC):
                            ins = t.matmul(pg[:, 0:T],
                                           wsB[:, ubase + kc * OW + jj * 128:
                                               ubase + kc * OW + (jj + 1) * 128],
                                           ub_c(y_o, kc),
                                           start=(kc == 0), stop=(kc == KC - 1))
                        ins.then_inc(s_pe, 1)
                        inc("pe")
                blk.tensor(_t)

                def _a(a, pe_b=pe_b, dve_b=dve_b):
                    a = wrapa(a)
                    for j in range(NJ):
                        a.wait_ge(s_pe, pe_b + 2 * j + 2)
                        a.activation(tp[j % 2][:, :], [p2, p3][j % 2][:, 0:T],
                                     AF.Silu, ms=True)
                blk.scalar(_a)

                def _v(v, pe_b=pe_b, act_b=act_b):
                    v = wrapv(v)
                    for j in range(NJ):
                        v.wait_ge(s_act, act_b + j + 1)
                        v.tensor_tensor(ub_c(0, j), tp[j % 2][:, :],
                                        [p0, p1][j % 2][:, 0:T],
                                        ALU.mult, ms=True)
                blk.vector(_v)

                def _sy(sy, pe_b=pe_b):
                    for st in range(2, NST):
                        sy.wait_ge(s_pe, pe_b + 4 * (st - 1))
                        o0 = st * OW
                        sdma(sy, sw_u[st % 2], f"swu{st % 2}",
                             wsA[:, (st % 2) * KC * OW:
                                 (st % 2 + 1) * KC * OW].rearrange(
                                 "p (kc o) -> p kc o", o=OW),
                             swT.ap().rearrange("(kc p) o -> p kc o", p=128)
                             [:, :, o0:o0 + OW])
                        sdma(sy, sw_g[st % 2], f"swg{st % 2}",
                             wsB[:, (st % 2) * KC * OW:
                                 (st % 2 + 1) * KC * OW].rearrange(
                                 "p (kc o) -> p kc o", o=OW),
                             swT.ap().rearrange("(kc p) o -> p kc o", p=128)
                             [:, :, DFF + o0:DFF + o0 + OW])
                    sy.wait_ge(s_pe, pe_b + 2 * NJ)
                    for st in range(2):
                        sdma(sy, op_s[st % 2], f"ops{st % 2}",
                             [wsA, wsB][st % 2][:, :].rearrange(
                                 "p (kc d) -> p kc d", d=512),
                             woT.ap().rearrange("(kc p) d -> p kc d", p=128)
                             [:, (st % 4) * 16:(st % 4 + 1) * 16, 0:512])
                    sy.wait_ge(s_ld, val("ld"))
                blk.sync(_sy)

            with nc.Block(f"B_op_{ti}") as blk:
                pe_b, dve_b = val("pe"), val("dve")
                ops_b = [slotb["ops0"], slotb["ops1"]]

                def _t(t, pe_b=pe_b, dve_b=dve_b, ops_b=ops_b):
                    for st in range(16):
                        dr, qi = st // 4, st % 4
                        t.wait_ge(op_s[st % 2],
                                  ops_b[st % 2] + 16 * (st // 2 + 1))
                        if qi == 0 and dr > 0:
                            t.wait_ge(s_dve, dve_b + dr)
                        buf = [wsA, wsB][st % 2]
                        for kcl in range(16):
                            kc = qi * 16 + kcl
                            for m in range(MS):
                                ins = t.matmul(
                                    pms[m][:, 0:512],
                                    ub[:, kc * T + m * 128:
                                       kc * T + (m + 1) * 128],
                                    buf[:, kcl * 512:(kcl + 1) * 512],
                                    start=(qi == 0 and kcl == 0),
                                    stop=(qi == 3 and kcl == 15))
                        ins.then_inc(s_pe, 1)
                        inc("pe")
                blk.tensor(_t)

                def _v(v, pe_b=pe_b):
                    v = wrapv(v)
                    for dr in range(NDR):
                        v.wait_ge(s_pe, pe_b + 4 * (dr + 1))
                        for m in range(MS):
                            v.tensor_tensor(
                                u2[:, m * D + dr * 512:m * D + (dr + 1) * 512],
                                u2[:, m * D + dr * 512:m * D + (dr + 1) * 512],
                                pms[m][:, 0:512], ALU.add, ms=(m == MS - 1))
                blk.vector(_v)

                def _sy(sy, pe_b=pe_b):
                    for st in range(2, 16):
                        dr, qi = st // 4, st % 4
                        sy.wait_ge(s_pe, pe_b + st - 1)
                        sdma(sy, op_s[st % 2], f"ops{st % 2}",
                             [wsA, wsB][st % 2][:, :].rearrange(
                                 "p (kc d) -> p kc d", d=512),
                             woT.ap().rearrange("(kc p) d -> p kc d", p=128)
                             [:, qi * 16:(qi + 1) * 16,
                              dr * 512:(dr + 1) * 512])
                    sy.wait_ge(s_ld, val("ld"))
                blk.sync(_sy)


            with nc.Block(f"B_x1_{ti}") as blk:
                dve_b, act_b = val("dve"), val("act")
                xl_b, xsp_b = val("xl"), val("xsp")

                def _sy(sy, ti=ti, dve_b=dve_b, xsp_b=xsp_b):
                    for m in range(MS):
                        if m >= 1:
                            sy.wait_ge(s_dve, dve_b + 4 * m)
                            sy.wait_ge(xsp_s, xsp_b + 16 * m)
                        sy.dma_start(
                            scrB[:, :],
                            x_in.ap()[ti * T + m * 128:
                                      ti * T + (m + 1) * 128, :]).then_inc(
                                          xl_s, 16)
                        inc("xl", 16)
                blk.sync(_sy)

                def _v(v, dve_b=dve_b, act_b=act_b, xl_b=xl_b):
                    v = wrapv(v)
                    for m in range(MS):
                        am = u2[:, m * D:(m + 1) * D]
                        v.reduce_sum(colf[:, 8:9], am, axis=AX.X)
                        v.wait_ge(s_act, act_b + 4 * m + 1)
                        v.tensor_scalar_mul(colf[:, 11:12], colf[:, 9:10], 1.0 / D)
                        v.tensor_scalar_mul(colf[:, 10:11], colf[:, 8:9], 1.0 / D)
                        v.tensor_mul(colf[:, 12:13], colf[:, 10:11], colf[:, 10:11])
                        v.tensor_tensor(colf[:, 13:14], colf[:, 11:12],
                                        colf[:, 12:13], ALU.subtract, ms=True)
                        v.wait_ge(s_act, act_b + 4 * m + 2)
                        v.reciprocal(colf[:, 12:13], colf[:, 12:13])
                        v.tensor_mul(colf[:, 15:16], colf[:, 10:11], colf[:, 12:13])
                        v.wait_ge(xl_s, xl_b + 16 * (m + 1))
                        v.scalar_tensor_tensor(scrB[:, :], am, colf[:, 12:13],
                                               scrB[:, :], ALU.mult, ALU.add)
                        v.tensor_scalar(scrB[:, :], scrB[:, :], colf[:, 15:16],
                                        None, ALU.subtract, ms=True)
                        v.reduce_sum(colf[:, 8:9], scrB[:, :], axis=AX.X)
                        v.wait_ge(s_act, act_b + 4 * m + 3)
                        v.tensor_scalar_mul(colf[:, 11:12], colf[:, 9:10], 1.0 / D)
                        v.tensor_scalar_mul(colf[:, 10:11], colf[:, 8:9], 1.0 / D)
                        v.tensor_mul(colf[:, 12:13], colf[:, 10:11], colf[:, 10:11])
                        v.tensor_tensor(colf[:, 13:14], colf[:, 11:12],
                                        colf[:, 12:13], ALU.subtract, ms=True)
                        v.wait_ge(s_act, act_b + 4 * m + 4)
                        v.reciprocal(colf[:, 12:13], colf[:, 12:13])
                        v.tensor_mul(colf[:, 15:16], colf[:, 10:11], colf[:, 12:13])
                        v.tensor_scalar(ub[:, m * D:(m + 1) * D], scrB[:, :],
                                        colf[:, 12:13], colf[:, 15:16],
                                        ALU.mult, ALU.subtract, ms=True)
                blk.vector(_v)

                def _a(a, dve_b=dve_b):
                    a = wrapa(a)
                    for m in range(MS):
                        if m >= 1:
                            a.wait_ge(s_dve, dve_b + 4 * m)
                        a.activation(scrA[:, :], u2[:, m * D:(m + 1) * D],
                                     AF.Square, accum_out=colf[:, 9:10],
                                     ms=True)
                        a.wait_ge(s_dve, dve_b + 4 * m + 1)
                        a.activation(colf[:, 12:13], colf[:, 13:14], AF.Sqrt,
                                     bias=cbias[:, 0:1], ms=True)
                        a.wait_ge(s_dve, dve_b + 4 * m + 2)
                        a.activation(scrA[:, :], scrB[:, :], AF.Square,
                                     accum_out=colf[:, 9:10], ms=True)
                        a.wait_ge(s_dve, dve_b + 4 * m + 3)
                        a.activation(colf[:, 12:13], colf[:, 13:14], AF.Sqrt,
                                     bias=cbias[:, 0:1], ms=True)
                blk.scalar(_a)

                def _g(g, ti=ti, dve_b=dve_b):
                    for m in range(MS):
                        g.wait_ge(s_dve, dve_b + 4 * m + 2)
                        g.dma_start(
                            x1_sp.ap()[ti * T + m * 128:
                                       ti * T + (m + 1) * 128, :],
                            scrB[:, :]).then_inc(xsp_s, 16)
                        inc("xsp", 16)
                    g.wait_ge(xsp_s, val("xsp"))
                blk.gpsimd(_g)


            with nc.Block(f"B_tr_{ti}") as blk:
                pe_b, dve_b = val("pe"), val("dve")

                def _t(t, pe_b=pe_b, dve_b=dve_b):
                    gidx = 0
                    for m in range(MS):
                        for kg in range(KC // 4):
                            if gidx > 0:
                                t.wait_ge(s_dve, dve_b + gidx)
                            for i in range(4):
                                kc = kg * 4 + i
                                t.transpose(pbf[:, i * 128:(i + 1) * 128],
                                            ub[:, m * D + kc * 128:
                                               m * D + (kc + 1) * 128],
                                            c_ident[:, :]).then_inc(s_pe, 1)
                                inc("pe")
                            gidx += 1
                blk.tensor(_t)

                def _v(v, pe_b=pe_b):
                    v = wrapv(v)
                    gidx = 0
                    for m in range(MS):
                        for kg in range(KC // 4):
                            v.wait_ge(s_pe, pe_b + 4 * (gidx + 1))
                            v.tensor_copy(
                                fm[:, :].rearrange("p (kc t) -> p kc t", t=T)
                                [:, kg * 4:(kg + 1) * 4, m * 128:(m + 1) * 128],
                                pbf[:, :].rearrange("p (a b) -> p a b", b=128)
                            , ms=True)
                            gidx += 1
                blk.vector(_v)

            with nc.Block(f"B_rt_{ti}") as blk:
                pe_b, dve_b, act_b = val("pe"), val("dve"), val("act")

                def _t(t, pe_b=pe_b, dve_b=dve_b):
                    for m in range(MS):
                        if m > 0:
                            t.wait_ge(s_dve, dve_b + 4 * m)
                        for kc in range(KC):
                            ins = t.matmul(p4[:, 0:NE],
                                           fm[:, kc * T + m * 128:
                                              kc * T + (m + 1) * 128],
                                           wgT_sb[:, kc * NE:(kc + 1) * NE],
                                           start=(kc == 0), stop=(kc == KC - 1))
                        ins.then_inc(s_pe, 1)
                        inc("pe")
                        t.wait_ge(s_dve, dve_b + 4 * m + 2)
                        for e in range(4):
                            ins = t.transpose(
                                pbf[0:1, e * 128:(e + 1) * 128],
                                bq_sb[:, e:e + 1], c_ident[:, :])
                        ins.then_inc(s_pe, 1)
                        inc("pe")
                        t.wait_ge(s_dve, dve_b + 4 * m + 3)
                        for e in range(4):
                            ins = t.transpose(
                                pbf[0:1, e * 128:(e + 1) * 128],
                                bq_sb[:, 4 + e:5 + e], c_ident[:, :])
                        ins.then_inc(s_pe, 1)
                        inc("pe")
                    for e in range(NE):
                        pb = [p2, p3][e % 2]
                        if e >= 2:
                            t.wait_ge(s_dve, dve_b + 4 * MS + e - 1)
                        t.matmul(pb[:, 0:T], c_onesrow[0:1, :],
                                 bw_all[0:1, e * T:(e + 1) * T],
                                 start=True, stop=True).then_inc(s_pe, 1)
                        inc("pe")
                blk.tensor(_t)

                def _v(v, pe_b=pe_b, dve_b=dve_b, act_b=act_b):
                    v = wrapv(v)
                    for m in range(MS):
                        v.wait_ge(s_pe, pe_b + 3 * m + 1)
                        v.tensor_copy(tmpA[:, 0:NE], p4[:, 0:NE])
                        v.reduce_max(colf[:, 8:9], tmpA[:, 0:NE], axis=AX.X)
                        v.tensor_single_scalar(tmpB[:, 0:NE], tmpA[:, 0:NE],
                                               colf[:, 8:9], ALU.is_ge)
                        v.scalar_tensor_tensor(tmpB[:, 8:16], tmpB[:, 0:NE],
                                               BIG_NEG, tmpA[:, 0:NE],
                                               ALU.mult, ALU.add)
                        v.reduce_max(colf[:, 9:10], tmpB[:, 8:16], axis=AX.X)
                        v.tensor_tensor(colf[:, 10:11], colf[:, 8:9],
                                        colf[:, 9:10],
                                        ALU.subtract, ms=True)
                        v.wait_ge(s_act, act_b + m + 1)
                        v.tensor_tensor(colf[:, 13:14], colf[:, 11:12],
                                        colf[:, 12:13], ALU.subtract)
                        v.tensor_single_scalar(tmpA[:, 8:16], tmpA[:, 0:NE],
                                               colf[:, 9:10], ALU.is_ge)
                        v.tensor_single_scalar(tmpA[:, 0:NE], tmpA[:, 8:16],
                                               colf[:, 12:13], ALU.mult)
                        v.scalar_tensor_tensor(tmpA[:, 0:NE], tmpB[:, 0:NE],
                                               colf[:, 13:14], tmpA[:, 0:NE],
                                               ALU.mult, ALU.add)
                        v.tensor_copy(bq_sb[:, 0:NE],
                                      tmpA[:, 0:NE], ms=True)
                        v.wait_ge(s_pe, pe_b + 3 * m + 2)
                        for e in range(4):
                            v.tensor_copy(
                                bw_all[0:1, e * T + m * 128:
                                       e * T + (m + 1) * 128],
                                pbf[0:1, e * 128:(e + 1) * 128], ms=(e == 3))
                        v.wait_ge(s_pe, pe_b + 3 * m + 3)
                        for e in range(4):
                            v.tensor_copy(
                                bw_all[0:1, (4 + e) * T + m * 128:
                                       (4 + e) * T + (m + 1) * 128],
                                pbf[0:1, e * 128:(e + 1) * 128], ms=(e == 3))
                    for e in range(NE):
                        v.wait_ge(s_pe, pe_b + 3 * MS + e + 1)
                        v.tensor_copy(bw_all[:, e * T:(e + 1) * T],
                                      [p2, p3][e % 2][:, 0:T], ms=True)
                blk.vector(_v)

                def _a(a, dve_b=dve_b):
                    a = wrapa(a)
                    for m in range(MS):
                        a.wait_ge(s_dve, dve_b + 4 * m + 1)
                        a.activation(colf[:, 11:12], colf[:, 10:11], AF.Sigmoid)
                        a.activation(colf[:, 12:13], colf[:, 10:11], AF.Sigmoid,
                                     scale=-1.0, ms=True)
                blk.scalar(_a)

                def _sy(sy):
                    for st in range(2):
                        e, h0, hw = (0, st * 256, 256)
                        w12_slab(sy, m1s[st % 2], f"mh1{st % 2}", wsA, st,
                                 w1T, e, h0, hw)
                        w12_slab(sy, m2s[st % 2], f"mh2{st % 2}", wsB, st,
                                 w2T, e, h0, hw)
                blk.sync(_sy)

            MH_STEPS = []
            for e in range(NE):
                for q, (h0, hw) in enumerate([(0, 256), (256, 256), (512, 256),
                                              (768, 256), (1024, 256),
                                              (1280, 128)]):
                    MH_STEPS.append((e, h0, hw))
            GCUM = [0]
            for (_, _, hw) in MH_STEPS:
                GCUM.append(GCUM[-1] + hw // 128)

            with nc.Block(f"B_mh_{ti}") as blk:
                pe_b, dve_b, act_b = val("pe"), val("dve"), val("act")
                tp = [tmpA, tmpB]

                def _t(t, pe_b=pe_b, dve_b=dve_b, slotb=slotb):
                    for st, (e, h0, hw) in enumerate(MH_STEPS):
                        t.wait_ge(m1s[st % 2],
                                  slotb[f"mh1{st % 2}"] + 32 * (st // 2 + 1))
                        t.wait_ge(m2s[st % 2],
                                  slotb[f"mh2{st % 2}"] + 32 * (st // 2 + 1))
                        base = (st % 2) * (KC * 256)
                        for hj in range(hw // 128):
                            g = GCUM[st] + hj
                            pa = [p0, p1][g % 2]
                            pb = [p2, p3][g % 2]
                            if g >= 2:
                                t.wait_ge(s_dve, dve_b + 2 * (g - 1))
                            for kc in range(KC):
                                ins = t.matmul(pa[:, 0:T],
                                               wsA[:, base + kc * hw + hj * 128:
                                                   base + kc * hw + (hj + 1) * 128],
                                               fm_c(kc), start=(kc == 0),
                                               stop=(kc == KC - 1))
                            ins.then_inc(s_pe, 1)
                            inc("pe")
                            for kc in range(KC):
                                ins = t.matmul(pb[:, 0:T],
                                               wsB[:, base + kc * hw + hj * 128:
                                                   base + kc * hw + (hj + 1) * 128],
                                               fm_c(kc), start=(kc == 0),
                                               stop=(kc == KC - 1))
                            ins.then_inc(s_pe, 1)
                            inc("pe")
                blk.tensor(_t)

                def _a(a, pe_b=pe_b, dve_b=dve_b):
                    a = wrapa(a)
                    for g in range(GCUM[-1]):
                        a.wait_ge(s_pe, pe_b + 2 * g + 1)
                        a.activation(tp[g % 2][:, :], [p0, p1][g % 2][:, 0:T],
                                     AF.Silu, ms=True)
                blk.scalar(_a)

                def _v(v, pe_b=pe_b, act_b=act_b):
                    v = wrapv(v)
                    for st, (e, h0, hw) in enumerate(MH_STEPS):
                        for hj in range(hw // 128):
                            g = GCUM[st] + hj
                            v.wait_ge(s_act, act_b + g + 1)
                            v.wait_ge(s_pe, pe_b + 2 * g + 2)
                            v.tensor_tensor(tp[g % 2][:, :], tp[g % 2][:, :],
                                            [p2, p3][g % 2][:, 0:T],
                                            ALU.mult, ms=True)
                            slot = e * HC + h0 // 128 + hj
                            v.tensor_tensor(ub_c(0, slot), tp[g % 2][:, :],
                                            bw_all[:, e * T:(e + 1) * T],
                                            ALU.mult, ms=True)
                blk.vector(_v)

                def _sy(sy, pe_b=pe_b, slotb=slotb):
                    for st in range(2, len(MH_STEPS)):
                        e, h0, hw = MH_STEPS[st]
                        sy.wait_ge(s_pe, pe_b + 2 * GCUM[st - 1])
                        w12_slab(sy, m1s[st % 2], f"mh1{st % 2}", wsA, st,
                                 w1T, e, h0, hw)
                        w12_slab(sy, m2s[st % 2], f"mh2{st % 2}", wsB, st,
                                 w2T, e, h0, hw)
                    sy.wait_ge(s_pe, pe_b + 2 * GCUM[-1])
                    for st in range(2):
                        w3_slab(sy, st, 0, st)
                    sy.wait_ge(s_ld, val("ld"))
                blk.sync(_sy)

            with nc.Block(f"B_w3_{ti}") as blk:
                pe_b, dve_b = val("pe"), val("dve")

                def _t(t, pe_b=pe_b, dve_b=dve_b, slotb=slotb):
                    for st in range(NDR * NE):
                        dr, e = st // NE, st % NE
                        t.wait_ge(w3s[st % 2],
                                  slotb[f"w3{st % 2}"] + 32 * (st // 2 + 1))
                        if e == 0 and dr > 0:
                            t.wait_ge(s_dve, dve_b + dr)
                        buf = [wsA, wsB][st % 2]
                        for kc in range(HC):
                            for m in range(MS):
                                ins = t.matmul(
                                    pms[m][:, 0:512],
                                    ub[:, (e * HC + kc) * T + m * 128:
                                       (e * HC + kc) * T + (m + 1) * 128],
                                    buf[:, kc * 512:(kc + 1) * 512],
                                    start=(e == 0 and kc == 0),
                                    stop=(e == NE - 1 and kc == HC - 1))
                        ins.then_inc(s_pe, 1)
                        inc("pe")
                blk.tensor(_t)

                def _v(v, pe_b=pe_b):
                    v = wrapv(v)
                    for dr in range(NDR):
                        v.wait_ge(s_pe, pe_b + NE * (dr + 1))
                        for m in range(MS):
                            v.tensor_copy(
                                u2[:, m * D + dr * 512:m * D + (dr + 1) * 512],
                                pms[m][:, 0:512], ms=(m == MS - 1))
                blk.vector(_v)

                def _sy(sy, pe_b=pe_b, slotb=slotb):
                    for st in range(2, NDR * NE):
                        dr, e = st // NE, st % NE
                        sy.wait_ge(s_pe, pe_b + st - 1)
                        w3_slab(sy, st, dr, e)
                    sy.wait_ge(s_ld, val("ld"))
                blk.sync(_sy)


            with nc.Block(f"B_x2_{ti}") as blk:
                dve_b, act_b = val("dve"), val("act")
                xl_b, xsp_b = val("xl"), val("xsp")

                def _sy(sy, ti=ti, dve_b=dve_b, xsp_b=xsp_b):
                    for m in range(MS):
                        if m >= 1:
                            sy.wait_ge(s_dve, dve_b + 2 * m)
                            sy.wait_ge(xsp_s, xsp_b + 16 * m)
                        sy.dma_start(
                            scrB[:, :],
                            x1_sp.ap()[ti * T + m * 128:
                                       ti * T + (m + 1) * 128, :]).then_inc(
                                           xl_s, 16)
                        inc("xl", 16)
                blk.sync(_sy)

                def _v(v, dve_b=dve_b, act_b=act_b, xl_b=xl_b):
                    v = wrapv(v)
                    for m in range(MS):
                        fmv = u2[:, m * D:(m + 1) * D]
                        v.reduce_sum(colf[:, 8:9], fmv, axis=AX.X)
                        v.wait_ge(s_act, act_b + 2 * m + 1)
                        v.tensor_scalar_mul(colf[:, 11:12], colf[:, 9:10], 1.0 / D)
                        v.tensor_scalar_mul(colf[:, 10:11], colf[:, 8:9], 1.0 / D)
                        v.tensor_mul(colf[:, 12:13], colf[:, 10:11], colf[:, 10:11])
                        v.tensor_tensor(colf[:, 13:14], colf[:, 11:12],
                                        colf[:, 12:13], ALU.subtract, ms=True)
                        v.wait_ge(s_act, act_b + 2 * m + 2)
                        v.reciprocal(colf[:, 12:13], colf[:, 12:13])
                        v.tensor_mul(colf[:, 15:16], colf[:, 10:11], colf[:, 12:13])
                        v.wait_ge(xl_s, xl_b + 16 * (m + 1))
                        v.scalar_tensor_tensor(scrB[:, :], fmv, colf[:, 12:13],
                                               scrB[:, :], ALU.mult, ALU.add)
                        v.tensor_scalar(scrB[:, :], scrB[:, :], colf[:, 15:16],
                                        None, ALU.subtract, ms=True)
                blk.vector(_v)

                def _a(a, dve_b=dve_b):
                    a = wrapa(a)
                    for m in range(MS):
                        if m >= 1:
                            a.wait_ge(s_dve, dve_b + 2 * m)
                        a.activation(scrA[:, :], u2[:, m * D:(m + 1) * D],
                                     AF.Square, accum_out=colf[:, 9:10],
                                     ms=True)
                        a.wait_ge(s_dve, dve_b + 2 * m + 1)
                        a.activation(colf[:, 12:13], colf[:, 13:14], AF.Sqrt,
                                     bias=cbias[:, 0:1], ms=True)
                blk.scalar(_a)

                def _g(g, ti=ti, dve_b=dve_b):
                    for m in range(MS):
                        g.wait_ge(s_dve, dve_b + 2 * m + 2)
                        g.dma_start(
                            y_out.ap()[ti * T + m * 128:
                                       ti * T + (m + 1) * 128, :],
                            scrB[:, :]).then_inc(xsp_s, 16)
                        inc("xsp", 16)
                    g.wait_ge(xsp_s, val("xsp"))
                blk.gpsimd(_g)


        if dbg:
            dumps = [("d_qkvT", qkvT, [D, DQKV], BF16),
                     ("d_swT", swT, [D, DSW], BF16),
                     ("d_sq", sq_sp, [128, KC, n_tok], BF16),
                     ("d_w", w_sp, [128, KC, n_tok], BF16),
                     ("d_wv", wv_sp, [128, KC, n_tok], BF16),
                     ("d_st", st_sp, [NT, 128, 2 * MS], F32),
                     ("d_cc", cc_out, [NCORES * 128, 2 * KC], F32),
                     ("d_x1", x1_sp, [n_tok, D], F32)]
            with nc.Block("dbg") as blk:
                def _g(g):
                    for nm, src, sh, dt in dumps:
                        od = nc.dram_tensor(nm, sh, dt, kind="ExternalOutput")
                        spec = ("a b -> (a b)" if len(sh) == 2 else
                                "a b c -> (a b c)")
                        dma(g, "g", od.ap().rearrange(spec),
                            src.ap().rearrange(spec))
                    g.wait_ge(s_g, val("g"))
                blk.gpsimd(_g)

        with nc.Block("fin") as blk:
            def _g(g):
                g.wait_ge(s_g, val("g"))
            blk.gpsimd(_g)

    return nc


_NC_CACHE = {}


def kernel(L0_attn_x, L0_attn_w_qkv, L0_attn_w_swiglu, L0_attn_w_out,
           L0_ffn_Wg, L0_ffn_W1, L0_ffn_W2, L0_ffn_W3):
    B, S, _ = L0_attn_x.shape
    n_tok = (B * S) // NCORES
    chunks_per_seq = NCORES // B

    if n_tok not in _NC_CACHE:
        _NC_CACHE[n_tok] = build_nc(n_tok)
    nc = _NC_CACHE[n_tok]

    x = np.ascontiguousarray(np.asarray(L0_attn_x, dtype=np.float32))
    wq = np.asarray(L0_attn_w_qkv, np.float32)
    wsw = np.asarray(L0_attn_w_swiglu, np.float32)
    wo = np.asarray(L0_attn_w_out, np.float32)
    w1 = np.asarray(L0_ffn_W1, np.float32)
    w2 = np.asarray(L0_ffn_W2, np.float32)
    w3 = np.asarray(L0_ffn_W3, np.float32)
    w3p = np.zeros((NE, D, D), np.float32)
    w3p[:, :, 0:DH] = w3
    SHK = D // NCORES
    SHKO = DFF // NCORES
    consts = {
        "w_g": np.ascontiguousarray(np.asarray(L0_ffn_Wg, np.float32)),
        "ident_bf": np.eye(128, dtype=ml_dtypes.bfloat16),
        "ones_col": np.ones((128, 1), dtype=ml_dtypes.bfloat16),
        "ones_row": np.ones((1, 128), dtype=ml_dtypes.bfloat16),
    }
    in_maps = []
    for c in range(NCORES):
        b = c // chunks_per_seq
        p = c % chunks_per_seq
        shard = np.ascontiguousarray(x[b, p * n_tok:(p + 1) * n_tok, :])
        mrow = np.zeros((NCORES,), dtype=np.float32)
        for r in range(NCORES):
            if r // chunks_per_seq == b and r % chunks_per_seq < p:
                mrow[r] = 1.0
        mask = np.broadcast_to(mrow, (128, NCORES)).copy()
        in_maps.append(dict(
            consts, x=shard, mask=mask,
            sh_qkv=np.ascontiguousarray(wq[:, c * SHK:(c + 1) * SHK]),
            sh_sw=np.ascontiguousarray(wsw[:, c * SHK:(c + 1) * SHK]),
            sh_wo=np.ascontiguousarray(wo[:, c * SHKO:(c + 1) * SHKO]),
            sh_w1=np.ascontiguousarray(w1[:, :, c * SHK:(c + 1) * SHK]),
            sh_w2=np.ascontiguousarray(w2[:, :, c * SHK:(c + 1) * SHK]),
            sh_w3=np.ascontiguousarray(w3p[:, :, c * SHK:(c + 1) * SHK])))

    res = run_bass_kernel_spmd(nc, in_maps, core_ids=list(range(NCORES)))
    out = np.zeros((B, S, D), dtype=np.float32)
    for c in range(NCORES):
        b = c // chunks_per_seq
        p = c % chunks_per_seq
        out[b, p * n_tok:(p + 1) * n_tok, :] = np.asarray(res.results[c]["y"])
    return out
